# version 01
# baseline (speedup 1.0000x reference)
import contextlib

import numpy as np
import ml_dtypes

import concourse.bass as bass
import concourse.mybir as mybir
from concourse.bass_utils import run_bass_kernel_spmd

F32 = mybir.dt.float32
BF16 = mybir.dt.bfloat16
AF = mybir.ActivationFunctionType
ALU = mybir.AluOpType
AX = mybir.AxisListType
bfloat16 = ml_dtypes.bfloat16

B, L, D = 2, 8192, 1024
NCORE = 8
T = 2048
NCH, CH = 16, 128
NF = 8
NH, DH = 16, 64
HDIM, HPAD, NHT = 2730, 2816, 22
E = 8
TOPK = 4
LN_EPS = 1e-5
TINY = float(np.finfo(np.float32).tiny)
LN2 = float(np.log(2.0))
BIG = 1.0e38
FFN_NEI = E + 1
W3A = list(range(0, 11))
W3B = list(range(11, NHT))

_ENGINES = ("sync", "gpsimd", "vector", "scalar", "tensor")


def _collapse(waits):
    best = {}
    for w in waits:
        if w is None:
            continue
        s, v = w
        if s not in best or v > best[s]:
            best[s] = v
    return tuple(best.items())


class _Sched:
    def __init__(self):
        self.ops = {e: [] for e in _ENGINES}
        self.counts = {}

    def emit(self, eng, fn, waits=(), sig=()):
        sig = tuple(sig)
        self.ops[eng].append((_collapse(waits), fn, sig))
        toks = []
        for s, a in sig:
            self.counts[s] = self.counts.get(s, 0) + a
            toks.append((s, self.counts[s]))
        if len(toks) == 1:
            return toks[0]
        return toks

    def sig(self, eng, fn, sem, amt=1, waits=()):
        return self.emit(eng, fn, waits=waits, sig=[(sem, amt)])

    def realize(self, block, handles):
        for ename in _ENGINES:
            ops = self.ops[ename]

            def body(eng, ops=ops):
                for waits, fn, sig in ops:
                    for s, v in waits:
                        eng.wait_ge(handles[s], v)
                    inst = fn(eng)
                    for s, a in sig:
                        inst.then_inc(handles[s], a)

            getattr(block, ename)(body)


def _apn(t, off, *dims):
    a = t[:, :]
    return bass.AP(a.tensor, off,
                   [list(a.ap[0])] + [[s, c] for (s, c) in dims])



def _lhsT_tiles(wT, k, m):
    assert wT.shape == (k, m) and k % 128 == 0 and m % 128 == 0
    return np.ascontiguousarray(
        wT.reshape(k // 128, 128, m // 128, 128).transpose(2, 0, 1, 3))


def _pad_h_cols(wT):
    out = np.zeros((wT.shape[0], HPAD), np.float32)
    out[:, :HDIM] = wT
    return out


def _pad_h_rows(wT):
    out = np.zeros((HPAD, wT.shape[1]), np.float32)
    out[:HDIM, :] = wT
    return out


def _host_prep(inputs):
    f32 = lambda a: np.ascontiguousarray(np.asarray(a, np.float32))
    x = f32(inputs["L0_attn_x"])
    sh = {}
    sh["wq"] = np.ascontiguousarray(f32(inputs["L0_attn_Wq"]).T)
    sh["wv"] = np.ascontiguousarray(f32(inputs["L0_attn_Wv"]).T)
    sh["wg"] = np.ascontiguousarray(f32(inputs["L0_attn_Wg"]).T)
    sh["wo"] = np.ascontiguousarray(f32(inputs["L0_attn_Wo"]).T)
    beta = f32(inputs["L0_attn_beta"]).reshape(16)
    sh["beta_row"] = np.ascontiguousarray(beta.reshape(1, 16))
    sh["beta_col"] = np.ascontiguousarray(beta.reshape(16, 1))
    sh["gamma"] = f32(inputs["L0_attn_ln_gamma"]).reshape(1, D)
    sh["lnbeta"] = f32(inputs["L0_attn_ln_beta"]).reshape(1, D)
    sh["wgr"] = np.ascontiguousarray(f32(inputs["L0_ffn_Wg"]).T)
    sh["rbias"] = f32(inputs["L0_ffn_routing_bias"]).reshape(1, E)
    sh["ws1"] = _lhsT_tiles(_pad_h_cols(f32(inputs["L0_ffn_Ws1"]).T), D, HPAD)
    sh["ws2"] = _lhsT_tiles(_pad_h_cols(f32(inputs["L0_ffn_Ws2"]).T), D, HPAD)
    sh["ws3"] = np.ascontiguousarray(
        _pad_h_rows(f32(inputs["L0_ffn_Ws3"]).T).reshape(NHT, 128, D))
    sh["w1"] = np.stack(
        [_lhsT_tiles(_pad_h_cols(f32(inputs["L0_ffn_W1"][e]).T), D, HPAD)
         for e in range(E)])
    sh["w2"] = np.stack(
        [_lhsT_tiles(_pad_h_cols(f32(inputs["L0_ffn_W2"][e]).T), D, HPAD)
         for e in range(E)])
    sh["w3"] = np.stack(
        [np.ascontiguousarray(
            _pad_h_rows(f32(inputs["L0_ffn_W3"][e]).T).reshape(NHT, 128, D))
         for e in range(E)])

    sh["ident"] = np.eye(128, dtype=bfloat16)
    e127 = np.zeros((128, 1), bfloat16); e127[127, 0] = 1
    sh["e127"] = e127
    sh["ones116"] = np.ones((1, 16), bfloat16)
    sh["ones1128b"] = np.ones((1, 128), bfloat16)
    sh["ones1128f"] = np.ones((1, 128), np.float32)
    d2t = np.full((128, 128), BIG, np.float32)
    for k in range(128):
        d2t[k, k:] = np.arange(0, 128 - k, dtype=np.float32)
    sh["d2t"] = d2t.astype(bfloat16)
    sh["tp16"] = np.broadcast_to(
        np.arange(1, 129, dtype=np.float32), (16, 128)).astype(bfloat16).copy()
    dlx = np.full((17, 16), BIG, np.float32)
    for s in range(17):
        for c in range(16):
            if c >= s:
                dlx[s, c] = 128.0 * (c - s)
    sh["dlx"] = dlx
    dend = np.full((17, 1), BIG, np.float32)
    for s in range(1, 17):
        dend[s, 0] = 128.0 * (16 - s)
    sh["dend"] = dend
    bdmask = np.zeros((16, D), bfloat16)
    for h in range(16):
        bdmask[h, h * 64:(h + 1) * 64] = 1
    sh["bdmask"] = bdmask
    sh["iota8"] = np.broadcast_to(
        np.arange(E, dtype=np.float32), (128, E)).copy()

    in_maps = []
    for c in range(NCORE):
        b, s = c // 4, c % 4
        dcore = np.full((8, 1), BIG, np.float32)
        for i in range(8):
            if i // 4 == b and (i % 4) < s:
                dcore[i, 0] = 2048.0 * (s - 1 - (i % 4))
        m = dict(sh)
        m["x0"] = np.ascontiguousarray(x[b, s * T:(s + 1) * T, :])
        m["dcore8"] = dcore
        in_maps.append(m)
    return in_maps



def build_program(max_stage=99, dbg=None):
    nc = bass.Bass("TRN2", target_bir_lowering=False)

    d = {}
    def din(name, shape, dtype=F32):
        d[name] = nc.dram_tensor(name, shape, dtype, kind="ExternalInput").ap()
    din("x0", [T, D])
    for n in ("wq", "wv", "wg", "wo"):
        din(n, [D, D])
    din("wgr", [D, E]); din("rbias", [1, E])
    din("ws1", [NHT, NF, 128, 128]); din("ws2", [NHT, NF, 128, 128])
    din("ws3", [NHT, 128, D])
    din("w1", [E, NHT, NF, 128, 128]); din("w2", [E, NHT, NF, 128, 128])
    din("w3", [E, NHT, 128, D])
    din("beta_row", [1, 16]); din("beta_col", [16, 1])
    din("gamma", [1, D]); din("lnbeta", [1, D])
    din("ident", [128, 128], BF16); din("e127", [128, 1], BF16)
    din("ones116", [1, 16], BF16); din("ones1128b", [1, 128], BF16)
    din("ones1128f", [1, 128])
    din("d2t", [128, 128], BF16); din("tp16", [16, 128], BF16)
    din("dlx", [17, 16]); din("dend", [17, 1]); din("dcore8", [8, 1])
    din("bdmask", [16, D], BF16); din("iota8", [128, E])

    out_d = nc.dram_tensor("out", [T, D], F32, kind="ExternalOutput").ap()
    x1d = nc.dram_tensor("x1d", [T, D], F32, kind="Internal")
    qd = nc.dram_tensor("qd", [T, D], BF16, kind="Internal")
    accd = nc.dram_tensor("accd", [T, D], F32, kind="Internal")
    dscr = nc.dram_tensor("dscr", [16 * D], BF16, kind="Internal")
    dscr2 = nc.dram_tensor("dscr2", [16 * D], BF16, kind="Internal")
    cc_in = nc.dram_tensor("cc_in", [1, D], F32, kind="Internal")
    cc_out = nc.dram_tensor("cc_out", [8, D], F32, kind="Internal")

    S = _Sched()
    TT = {"tok": {}}

    with contextlib.ExitStack() as es:
        def sb(name, shape, dtype):
            t = es.enter_context(nc.sbuf_tensor(name, shape, dtype))
            TT[name] = t
            return t

        def pstn(name, shape, dtype):
            t = es.enter_context(nc.psum_tensor(name, shape, dtype))
            TT[name] = t
            return t

        arena = sb("arena", [128, 49152], BF16)
        U1T, ST0, VAL = 0, 16384, 32768
        warena = sb("warena", [128, 16384], BF16)
        fpool = sb("fpool", [128, 4096], F32)
        bpool = sb("bpool", [128, 18 * 1024], BF16)
        BGS, BST, BRET, BY, BYGT, BAO, BU2, BXN, BQT = (
            0, 2048, 4096, 6144, 8192, 10240, 12288, 14336, 16384)

        gamma_rep = sb("gamma_rep", [128, D], BF16)
        beta_rep = sb("beta_rep", [128, D], BF16)
        gates_all = sb("gates_all", [128, NCH * E], F32)
        w_all = sb("w_all", [128, NCH * E], F32)
        gmax = sb("gmax", [128, NCH], F32)
        gidx = sb("gidx", [128, NCH], F32)
        gm1 = sb("gm1", [128, NCH], F32)
        gz = sb("gz", [128, NCH], F32)

        c_ident = sb("c_ident", [128, 128], BF16)
        c_e127 = sb("c_e127", [128, 1], BF16)
        c_ones116 = sb("c_ones116", [1, 16], BF16)
        c_ones1128b = sb("c_ones1128b", [1, 128], BF16)
        c_ones1128f = sb("c_ones1128f", [1, 128], F32)
        c_d2t = sb("c_d2t", [128, 128], BF16)
        c_tp16 = sb("c_tp16", [16, 128], BF16)
        c_dlx = sb("c_dlx", [17, 16], F32)
        c_dend = sb("c_dend", [17, 1], F32)
        c_dcore = sb("c_dcore", [8, 1], F32)
        c_bdmask = sb("c_bdmask", [16, D], BF16)
        c_iota8 = sb("c_iota8", [128, E], F32)
        beta_row = sb("beta_row_sb", [1, 16], F32)
        beta_col = sb("beta_col_sb", [16, 1], F32)
        lnl_row = sb("lnl_row", [1, 16], F32)
        lnl_col = sb("lnl_col", [16, 1], F32)
        lam_rep = sb("lam_rep", [128, 16], F32)
        A_all = sb("A_all", [128, NH * 128], BF16)
        P_all = sb("P_all", [16, 128], BF16)
        Lx_all = sb("Lx_all", [17, NH * 16], BF16)
        EndW = sb("EndW", [17, 16], BF16)
        CoreW = sb("CoreW", [8, 16], BF16)
        eps_col = sb("eps_col", [128, 1], F32)
        rbias_b = sb("rbias_b", [1, E], BF16)
        rbias_f = sb("rbias_f", [1, E], F32)
        wgr_b = sb("wgr_b", [128, NF * E], BF16)
        stat = sb("stat", [128, 24], F32)
        carr_buf = sb("carr_buf", [1, D], BF16)
        bd_sb = sb("bd_sb", [16, D], BF16)

        pA = pstn("pA", [128, 1024], F32)
        pB = pstn("pB", [128, 1024], F32)
        pC = pstn("pC", [128, 1024], F32)
        pD = pstn("pD", [128, 2048], BF16)

        x0sl = lambda p: fpool[:, p * 1024:(p + 1) * 1024]
        x1sl = lambda p: fpool[:, (2 + p) * 1024:(3 + p) * 1024]
        stage_sl = x1sl
        tkw = fpool[:, 0:128]; tkm = fpool[:, 128:256]
        tkt = fpool[:, 256:384]; tkt2 = fpool[:, 384:512]
        rend_sb = fpool[0:1, 2 * 1024:3 * 1024]
        rendall = fpool[0:8, 3 * 1024:4 * 1024]

        gsl = lambda p: bpool[:, BGS + p * 1024: BGS + (p + 1) * 1024]
        stsl = lambda p: bpool[:, BST + p * 1024: BST + (p + 1) * 1024]
        retsl = lambda p: bpool[:, BRET + p * 1024: BRET + (p + 1) * 1024]
        ysl = lambda i: bpool[:, BY + i * 1024: BY + (i + 1) * 1024]
        u1sl = ysl
        ygtsl = lambda p: bpool[:, BYGT + p * 1024: BYGT + (p + 1) * 1024]
        aosl = lambda p: bpool[:, BAO + p * 1024: BAO + (p + 1) * 1024]
        u2sl = lambda p: bpool[:, BU2 + p * 1024: BU2 + (p + 1) * 1024]
        s1sl = u2sl
        xnsl = lambda p: bpool[:, BXN + p * 1024: BXN + (p + 1) * 1024]
        qtsl = lambda p: bpool[:, BQT + p * 1024: BQT + (p + 1) * 1024]
        ln1_trash = bpool[:, BGS:BGS + 1024]
        trash2 = A_all[:, 0:1024]
        r_buf = bpool[0:1, BGS:BGS + 2048]
        rendall_b = bpool[0:8, BRET:BRET + 1024]
        rx_mat = bpool[0:17, BAO:BAO + 1024]
        carr_sb = bpool[0:16, BYGT:BYGT + 1024]
        gamma_in = bpool[0:1, BU2:BU2 + 1024]
        lnbeta_in = bpool[0:1, BU2 + 1024:BU2 + 2048]

        def stat_g(base, parity):
            o = base + 4 * parity
            return (stat[:, o:o + 1], stat[:, o + 1:o + 2],
                    stat[:, o + 2:o + 3], stat[:, o + 3:o + 4])

        def w_slot(i):
            return warena[:, i * 1024:(i + 1) * 1024]

        def w_region(i0):
            return warena[:, i0 * 1024:(i0 + 8) * 1024]

        def u1T_tile(k, c):
            return _apn(arena, U1T + k * 2048 + c * 128, (1, 128))

        def st0_sl(c):
            return _apn(arena, ST0 + c * 1024, (1, 1024))

        def z_tile(m, c8):
            return _apn(arena, m * 1024 + c8 * 128, (1, 128))

        def z_m(m):
            return _apn(arena, m * 1024, (1, 1024))

        def u2T_rhs(k, th, nh):
            return _apn(arena, VAL + k * 2048 + th * 1024 + nh * 512, (1, 512))

        def u2T_cell(c):
            return _apn(arena, VAL + c * 128, (2048, 8), (1, 128))

        creqs = [
            (d["d2t"], c_d2t), (d["tp16"], c_tp16), (d["dlx"], c_dlx),
            (d["dend"], c_dend), (d["dcore8"], c_dcore), (d["iota8"], c_iota8),
            (d["beta_row"], beta_row), (d["beta_col"], beta_col),
            (d["rbias"], rbias_f),
            (d["ident"], c_ident), (d["e127"], c_e127),
            (d["ones116"], c_ones116), (d["ones1128b"], c_ones1128b),
            (d["ones1128f"], c_ones1128f), (d["bdmask"], c_bdmask),
        ]
        for src, dst in creqs:
            S.sig("sync", lambda e, s=src, t=dst: e.dma_start(out=t[:, :], in_=s),
                  "dcst", 16)
        tok_cst = ("dcst", 16 * len(creqs))

        S.sig("gpsimd", lambda e: e.dma_start(
            out=gamma_in, in_=d["gamma"]), "dgb", 16)
        S.sig("gpsimd", lambda e: e.dma_start(
            out=lnbeta_in, in_=d["lnbeta"]), "dgb", 16)
        tok_gb = ("dgb", 32)
        tok_wgr = S.sig("gpsimd", lambda e: e.dma_start(
            out=wgr_b[:, :].rearrange("p (k m) -> p k m", m=E),
            in_=d["wgr"].rearrange("(k p) m -> p k m", p=128)),
            "dwgr", 16)
        tok_wv = S.sig("gpsimd", lambda e: e.dma_start(
            out=w_region(0).rearrange("p (k m) -> p k m", m=1024), in_=d["wv"].rearrange("(k p) m -> p k m", p=128)),
            "wva", 16)
        tok_wq = S.sig("gpsimd", lambda e: e.dma_start(
            out=w_region(8).rearrange("p (k m) -> p k m", m=1024), in_=d["wq"].rearrange("(k p) m -> p k m", p=128)),
            "wqa", 16)

        tok_eps = S.sig("vector", lambda e: e.memset(eps_col[:, :], LN_EPS),
                        "V", 1)

        def lam_pipe(src, dst):
            t1 = S.sig("scalar", lambda e: e.activation(
                dst[:, :], src[:, :], AF.Exp, scale=-LN2), "S", 1,
                waits=[tok_cst])
            t2 = S.sig("vector", lambda e: e.tensor_scalar(
                dst[:, :], dst[:, :], -1.0, 1.0, ALU.mult, ALU.add), "V", 1,
                waits=[t1])
            t3 = S.sig("vector", lambda e: e.tensor_scalar(
                dst[:, :], dst[:, :], TINY, 1.0 - 1e-9, ALU.max, ALU.min),
                "V", 1, waits=[t2])
            return S.sig("scalar", lambda e: e.activation(
                dst[:, :], dst[:, :], AF.Ln), "S", 1, waits=[t3])
        tok_lr = lam_pipe(beta_row, lnl_row)
        tok_lc = lam_pipe(beta_col, lnl_col)

        tok_t = S.sig("tensor", lambda e: e.matmul(
            pA[:, 0:16], c_ones1128f[:, :], lnl_row[:, :], start=True,
            stop=True), "T", 1, waits=[tok_cst, tok_lr])
        tok_lrep = S.sig("vector", lambda e: e.tensor_copy(
            lam_rep[:, :], pA[:, 0:16]), "V", 1, waits=[tok_t])

        def rep_row(src_row, dst, prev):
            tks = None
            for half in range(2):
                tks = S.sig("tensor", lambda e, h=half: e.matmul(
                    pB[:, h * 512:(h + 1) * 512], c_ones1128b[:, :],
                    src_row[0:1, h * 512:(h + 1) * 512], start=True, stop=True),
                    "T", 1, waits=[tok_cst, tok_gb, prev] if half == 0 else [])
            return S.sig("vector", lambda e: e.tensor_copy(dst[:, :], pB[:, :]),
                         "V", 1, waits=[tks])
        tok_grep = rep_row(gamma_in, gamma_rep, None)
        tok_brep = rep_row(lnbeta_in, beta_rep, tok_grep)

        tok_rb = S.sig("vector", lambda e: e.tensor_copy(
            rbias_b[:, :], rbias_f[:, :]), "V", 1, waits=[tok_cst])

        def act_build(dst_ap, src_ap, scale_ap, waits):
            return S.sig("scalar", lambda e: e.activation(
                dst_ap, src_ap, AF.Exp, scale=scale_ap), "S", 1, waits=waits)
        tok_A = tok_P = tok_Lx = tok_End = tok_Core = None
        for h in range(NH):
            tok_A = act_build(A_all[:, h * 128:(h + 1) * 128], c_d2t[:, :],
                              lam_rep[:, h:h + 1], [tok_lrep, tok_cst])
        tok_P = act_build(P_all[:, :], c_tp16[:, :], lnl_col[:, :],
                          [tok_lc, tok_cst])
        for h in range(NH):
            tok_Lx = act_build(Lx_all[:, h * 16:(h + 1) * 16], c_dlx[:, :],
                               lam_rep[0:17, h:h + 1], [tok_lrep, tok_cst])
        for h in range(NH):
            tok_End = act_build(EndW[:, h:h + 1], c_dend[:, :],
                                lam_rep[0:17, h:h + 1], [tok_lrep, tok_cst])
        for h in range(NH):
            tok_Core = act_build(CoreW[:, h:h + 1], c_dcore[:, :],
                                 lam_rep[0:8, h:h + 1], [tok_lrep, tok_cst])

        def ln_stats(in_ap, grp, wait_in, trash_ap):
            sm, sq, rs, nm = grp
            t1 = S.sig("vector", lambda e: e.tensor_reduce(
                sm, in_ap, AX.X, ALU.add), "V", 1, waits=[wait_in])
            t2 = S.sig("vector", lambda e: e.tensor_scalar(
                sm, sm, -1.0 / D, None, ALU.mult), "V", 1, waits=[t1])
            t3 = S.sig("scalar", lambda e: e.activation(
                trash_ap, in_ap, AF.Square, bias=sm, accum_out=sq), "S", 1,
                waits=[t2])
            t4 = S.sig("scalar", lambda e: e.activation(
                rs, sq, AF.Ln, scale=1.0 / D, bias=eps_col[:, :]), "S", 1,
                waits=[t3, tok_eps])
            t5 = S.sig("scalar", lambda e: e.activation(
                rs, rs, AF.Exp, scale=-0.5), "S", 1, waits=[t4])
            t6 = S.sig("vector", lambda e: e.tensor_tensor(
                nm, sm, rs, ALU.mult), "V", 1, waits=[t5])
            return t6

        if max_stage >= 2:
            tok_x0 = [None] * NCH
            tok_u1T_ev = [None] * NCH
            tok_vev = [None] * NCH
            tok_qst = [None] * NCH
            prev_tpD = None
            prev_u1_used = [None, None]
            prev_x0_used = [None, None]
            prev_pAB_ev = [tok_lrep, tok_brep]
            prev_qt_used = [None, None]

            for c in range(NCH):
                p = c % 2
                tok_x0[c] = S.sig("sync", lambda e, c=c, p=p: e.dma_start(
                    out=x0sl(p), in_=d["x0"][c * CH:(c + 1) * CH, :]),
                    f"dx0{p}", 16, waits=[prev_x0_used[p]])
                grp = stat_g(0, p)
                tok_ln1 = ln_stats(x0sl(p), grp, tok_x0[c], ln1_trash)
                rs, nm = grp[2], grp[3]
                t7 = S.sig("scalar", lambda e, p=p, rs=rs, nm=nm: e.activation(
                    u1sl(p), x0sl(p), AF.Identity, scale=rs, bias=nm), "S", 1,
                    waits=[tok_ln1, prev_u1_used[p]])
                prev_x0_used[p] = t7

                ttp = None
                for k in range(NF):
                    ttp = S.sig("tensor", lambda e, p=p, k=k: e.transpose(
                        pD[:, k * 128:(k + 1) * 128],
                        u1sl(p)[:, k * 128:(k + 1) * 128], c_ident[:, :]),
                        "T", 1, waits=[t7, prev_tpD, tok_cst] if k == 0 else [])
                ev = None
                for half in range(2):
                    ev = S.sig("vector", lambda e, c=c, half=half: e.tensor_copy(
                        _apn(arena, U1T + half * 4 * 2048 + c * 128,
                             (2048, 4), (1, 128)),
                        _apn(pD, half * 512, (128, 4), (1, 128))),
                        "V", 1, waits=[ttp] if half == 0 else [])
                prev_tpD = ev
                prev_u1_used[p] = ev
                tok_u1T_ev[c] = ev

                vps = pA if p == 0 else pB
                tmm = None
                for k in range(NF):
                    for nh in range(2):
                        first = (k == 0 and nh == 0)
                        tmm = S.sig("tensor",
                                    lambda e, c=c, k=k, nh=nh, vps=vps: e.matmul(
                                        vps[:, nh * 512:(nh + 1) * 512],
                                        u1T_tile(k, c),
                                        w_slot(k)[:, nh * 512:(nh + 1) * 512],
                                        start=(k == 0), stop=(k == NF - 1)),
                                    "T", 1,
                                    waits=[ev, tok_wv, prev_pAB_ev[p]]
                                    if first else [])
                tok_vev[c] = S.sig("scalar", lambda e, c=c, vps=vps: e.activation(
                    _apn(arena, VAL + c * 1024, (1, 1024)), vps[:, :],
                    AF.Identity), "S", 1, waits=[tmm])
                qmm = None
                for k in range(NF):
                    for nh in range(2):
                        first = (k == 0 and nh == 0)
                        qmm = S.sig("tensor",
                                    lambda e, c=c, k=k, nh=nh, vps=vps: e.matmul(
                                        vps[:, nh * 512:(nh + 1) * 512],
                                        u1T_tile(k, c),
                                        w_slot(8 + k)[:, nh * 512:(nh + 1) * 512],
                                        start=(k == 0), stop=(k == NF - 1)),
                                    "T", 1,
                                    waits=[tok_vev[c], tok_wq] if first else [])
                tok_qev = S.sig("scalar", lambda e, p=p, vps=vps: e.activation(
                    qtsl(p), vps[:, :], AF.Identity), "S", 1,
                    waits=[qmm, prev_qt_used[p]])
                tok_qst[c] = S.sig("sync", lambda e, c=c, p=p: e.dma_start(
                    out=qd.ap()[c * CH:(c + 1) * CH, :], in_=qtsl(p)),
                    f"dqs{p}", 16, waits=[tok_qev])
                prev_qt_used[p] = tok_qst[c]
                prev_pAB_ev[p] = tok_qev
            TT["tok"].update(vev=tok_vev, u1T=tok_u1T_ev, qst=tok_qst)

        if max_stage >= 3:
            tok_st0 = [None] * NH
            prev_h_ev = [prev_pAB_ev[0], prev_pAB_ev[1]]
            for h in range(NH):
                hp = h % 2
                hps = pA if hp == 0 else pB
                mm = None
                for half in range(2):
                    mm = S.sig("tensor",
                               lambda e, h=h, half=half, hps=hps: e.matmul(
                                   hps[:, half * 512:(half + 1) * 512],
                                   A_all[:, h * 128:(h + 1) * 128],
                                   _apn(arena, VAL + half * 8 * 1024 + h * 64,
                                        (1024, 8), (1, 64)),
                                   start=True, stop=True), "T", 1,
                               waits=[tok_A, tok_vev[NCH - 1], prev_h_ev[hp]]
                               if half == 0 else [])
                tok_st0[h] = S.sig("vector", lambda e, h=h, hps=hps: e.tensor_copy(
                    _apn(arena, ST0 + h * 64, (1024, 16), (1, 64)),
                    _apn(hps, 0, (64, 16), (1, 64))),
                    "V", 1, waits=[mm])
                prev_h_ev[hp] = tok_st0[h]

            tok_rx0 = S.sig("vector", lambda e: e.memset(rx_mat, 0.0),
                            "V", 1, waits=[prev_u1_used[0], prev_u1_used[1]])
            tok_rdma = [None] * NCH
            prev_r = [prev_h_ev[0], prev_h_ev[1]]
            prev_rb = [tok_vev[NCH - 1], tok_vev[NCH - 1]]
            for c in range(NCH):
                rp = c % 2
                rps = pC if rp == 0 else pA
                mm = None
                for half in range(2):
                    mm = S.sig("tensor",
                               lambda e, c=c, half=half, rps=rps: e.matmul(
                                   rps[0:1, half * 512:(half + 1) * 512],
                                   c_e127[:, :],
                                   _apn(arena, ST0 + c * 1024 + half * 512,
                                        (1, 512)),
                                   start=True, stop=True), "T", 1,
                               waits=[tok_st0[NH - 1], tok_st0[NH - 2],
                                      prev_r[rp], tok_cst] if half == 0 else [])
                tok_rev = S.sig("vector", lambda e, rp=rp, rps=rps: e.tensor_copy(
                    r_buf[0:1, rp * 1024:(rp + 1) * 1024], rps[0:1, 0:1024]),
                    "V", 1, waits=[mm, prev_rb[rp]])
                prev_r[rp] = tok_rev
                tok_rdma[c] = S.sig("sync", lambda e, c=c, rp=rp: e.dma_start(
                    out=dscr.ap()[c * 1024:(c + 1) * 1024],
                    in_=r_buf[0:1, rp * 1024:(rp + 1) * 1024]),
                    f"drf{rp}", 16, waits=[tok_rev])
                prev_rb[rp] = tok_rdma[c]

            t_rd2 = S.sig("sync", lambda e: e.dma_start(
                out=rx_mat[1:17, :],
                in_=dscr.ap().rearrange("(a b) -> a b", b=D)), "drd", 16,
                waits=[tok_rdma[NCH - 1], tok_rdma[NCH - 2], tok_rx0])

            mm = None
            for h in range(NH):
                mm = S.sig("tensor", lambda e, h=h: e.matmul(
                    pC[0:1, h * 64:(h + 1) * 64], EndW[:, h:h + 1],
                    rx_mat[0:17, h * 64:(h + 1) * 64], start=True, stop=True),
                    "T", 1,
                    waits=[t_rd2, tok_rx0, tok_End, prev_r[0], prev_r[1]]
                    if h == 0 else [])
            tok_rend = S.sig("vector", lambda e: e.tensor_copy(
                rend_sb, pC[0:1, 0:1024]), "V", 1, waits=[mm])
            TT["tok"].update(st0=tok_st0, rend=tok_rend)

        if max_stage >= 4:
            t_cc1 = S.sig("sync", lambda e: e.dma_start(
                out=cc_in.ap(), in_=rend_sb), "dcc", 16, waits=[tok_rend])
            t_cc = S.sig("gpsimd", lambda e: e.collective_compute(
                "AllGather", ALU.bypass, replica_groups=[list(range(8))],
                ins=[cc_in.ap().opt()], outs=[cc_out.ap().opt()]), "G", 1,
                waits=[t_cc1])
            t_cc2 = S.sig("gpsimd", lambda e: e.dma_start(
                out=rendall, in_=cc_out.ap()), "dcc", 16, waits=[t_cc])
            t_rab = S.sig("vector", lambda e: e.tensor_copy(
                rendall_b, rendall), "V", 1, waits=[t_cc2])

            mm = None
            for h in range(NH):
                mm = S.sig("tensor", lambda e, h=h: e.matmul(
                    pC[0:1, h * 64:(h + 1) * 64], CoreW[:, h:h + 1],
                    rendall_b[0:8, h * 64:(h + 1) * 64], start=True, stop=True),
                    "T", 1, waits=[t_rab, tok_Core, tok_rend] if h == 0 else [])
            tok_cc_ev = S.sig("vector", lambda e: e.tensor_copy(
                rx_mat[0:1, :], pC[0:1, 0:1024]), "V", 1, waits=[mm])

            mm = None
            for h in range(NH):
                mm = S.sig("tensor", lambda e, h=h: e.matmul(
                    pC[0:16, h * 64:(h + 1) * 64],
                    Lx_all[:, h * 16:(h + 1) * 16],
                    rx_mat[0:17, h * 64:(h + 1) * 64], start=True, stop=True),
                    "T", 1, waits=[tok_cc_ev, tok_Lx] if h == 0 else [])
            tok_carr = S.sig("vector", lambda e: e.tensor_copy(
                carr_sb, pC[0:16, :]), "V", 1, waits=[mm])
            t_cf1 = S.sig("sync", lambda e: e.dma_start(
                out=dscr2.ap().rearrange("(a b) -> a b", b=D), in_=carr_sb),
                "drd", 16, waits=[tok_carr])
            TT["tok"].update(carr=tok_carr, cf1=t_cf1)

        if max_stage >= 5:
            tok_wgl = S.sig("gpsimd", lambda e: e.dma_start(
                out=w_region(0).rearrange("p (k m) -> p k m", m=1024),
                in_=d["wg"].rearrange("(k p) m -> p k m", p=128)),
                "wva", 16, waits=[tok_vev[NCH - 1]])
            tok_wol = S.sig("gpsimd", lambda e: e.dma_start(
                out=w_region(8).rearrange("p (k m) -> p k m", m=1024),
                in_=d["wo"].rearrange("(k p) m -> p k m", p=128)),
                "wqa", 16, waits=[prev_pAB_ev[0], prev_pAB_ev[1]])

            prev_g_used = [None, None]
            prev_pA_user = prev_r[1]
            prev_pB_user = prev_h_ev[1]
            prev_pC_user = tok_carr
            prev_tpD2 = prev_tpD
            prev_x0p_used = [prev_x0_used[0], prev_x0_used[1]]
            prev_xn_used = [None, None]
            prev_x1_used = [None, None]
            prev_st_used = [None, None]
            prev_ret_used = [None, None]
            prev_ao_used = [None, None]
            prev_u2_used = [None, None]
            prev_ygt_used = [tok_carr, tok_carr]
            prev_qt2_used = [prev_qt_used[0], prev_qt_used[1]]
            prev_cb_used = t_cf1
            tok_u2T_ev = [None] * NCH
            tok_x1st = [None] * NCH
            tok_gates = [None] * NCH
            tok_g15 = None
            tok_wo15 = None

            for c in range(NCH):
                p = c % 2
                t_qld = S.sig("sync", lambda e, c=c, p=p: e.dma_start(
                    out=qtsl(p), in_=qd.ap()[c * CH:(c + 1) * CH, :]),
                    f"dql{p}", 16,
                    waits=[prev_qt2_used[p], tok_qst[c]])
                t_cld = S.sig("sync", lambda e, c=c: e.dma_start(
                    out=carr_buf[:, :],
                    in_=dscr2.ap()[c * 1024:(c + 1) * 1024]), "dcl", 16,
                    waits=[t_cf1, prev_cb_used])

                gmm = None
                for k in range(NF):
                    for nh in range(2):
                        first = (k == 0 and nh == 0)
                        gmm = S.sig("tensor",
                                    lambda e, c=c, k=k, nh=nh: e.matmul(
                                        pB[:, nh * 512:(nh + 1) * 512],
                                        u1T_tile(k, c),
                                        w_slot(k)[:, nh * 512:(nh + 1) * 512],
                                        start=(k == 0), stop=(k == NF - 1)),
                                    "T", 1,
                                    waits=[tok_wgl, prev_pB_user,
                                           tok_u1T_ev[c]] if first else [])
                tok_gev = S.sig("scalar", lambda e, p=p: e.activation(
                    gsl(p), pB[:, :], AF.Silu), "S", 1,
                    waits=[gmm, prev_g_used[p]])
                prev_pB_user = tok_gev
                if c == NCH - 1:
                    tok_g15 = tok_gev

                bmm = None
                for nh in range(2):
                    bmm = S.sig("tensor", lambda e, nh=nh: e.matmul(
                        pC[0:16, nh * 512:(nh + 1) * 512], c_ones116[:, :],
                        carr_buf[0:1, nh * 512:(nh + 1) * 512],
                        start=True, stop=True), "T", 1,
                        waits=[t_cld, prev_pC_user] if nh == 0 else [])
                tok_bd = S.sig("vector", lambda e: e.tensor_tensor(
                    bd_sb[:, :], pC[0:16, :], c_bdmask[:, :], ALU.mult), "V", 1,
                    waits=[bmm])
                prev_cb_used = tok_bd
                cmm = None
                for nh in range(2):
                    cmm = S.sig("tensor", lambda e, nh=nh: e.matmul(
                        pC[:, nh * 512:(nh + 1) * 512], P_all[:, :],
                        bd_sb[:, nh * 512:(nh + 1) * 512],
                        start=True, stop=True), "T", 1,
                        waits=[tok_bd, tok_P] if nh == 0 else [])

                grp = stat_g(0, p)
                sm, sq, rs, nm = grp
                t_st = S.sig("vector", lambda e, c=c, p=p: e.scalar_tensor_tensor(
                    stsl(p), pC[:, :], 1.0, st0_sl(c), ALU.mult, ALU.add),
                    "V", 1, waits=[cmm, prev_st_used[p]])
                prev_pC_user = t_st
                t_ret = S.sig("vector",
                              lambda e, p=p, sm=sm: e.scalar_tensor_tensor(
                                  retsl(p), stsl(p), 1.0, qtsl(p),
                                  ALU.mult, ALU.mult, accum_out=sm), "V", 1,
                              waits=[t_st, t_qld, prev_ret_used[p]])
                prev_st_used[p] = t_ret
                prev_qt2_used[p] = t_ret
                t2 = S.sig("vector", lambda e, sm=sm: e.tensor_scalar(
                    sm, sm, -1.0 / D, None, ALU.mult), "V", 1, waits=[t_ret])
                t3 = S.sig("scalar", lambda e, p=p, sm=sm, sq=sq: e.activation(
                    trash2, retsl(p), AF.Square, bias=sm, accum_out=sq),
                    "S", 1, waits=[t2])
                t4 = S.sig("scalar", lambda e, sq=sq, rs=rs: e.activation(
                    rs, sq, AF.Ln, scale=1.0 / D, bias=eps_col[:, :]), "S", 1,
                    waits=[t3, tok_eps])
                t5 = S.sig("scalar", lambda e, rs=rs: e.activation(
                    rs, rs, AF.Exp, scale=-0.5), "S", 1, waits=[t4])
                t6 = S.sig("vector", lambda e, sm=sm, rs=rs, nm=nm:
                           e.tensor_tensor(nm, sm, rs, ALU.mult), "V", 1,
                           waits=[t5])
                t_y0 = S.sig("scalar", lambda e, p=p, rs=rs, nm=nm: e.activation(
                    ysl(0), retsl(p), AF.Identity, scale=rs, bias=nm), "S", 1,
                    waits=[t6])
                prev_ret_used[p] = t_y0
                t_y1 = S.sig("vector", lambda e: e.tensor_tensor(
                    ysl(1), ysl(0), gamma_rep[:, :], ALU.mult), "V", 1,
                    waits=[t_y0, tok_grep])
                t_y2 = S.sig("vector", lambda e: e.tensor_tensor(
                    ysl(0), ysl(1), beta_rep[:, :], ALU.add), "V", 1,
                    waits=[t_y1, tok_brep])
                t_yg = S.sig("vector", lambda e, p=p: e.tensor_tensor(
                    ysl(1), ysl(0), gsl(p), ALU.mult), "V", 1,
                    waits=[t_y2, tok_gev])
                prev_g_used[p] = t_yg

                ttp = None
                for k in range(NF):
                    ttp = S.sig("tensor", lambda e, k=k: e.transpose(
                        pD[:, k * 128:(k + 1) * 128],
                        ysl(1)[:, k * 128:(k + 1) * 128], c_ident[:, :]),
                        "T", 1, waits=[t_yg, prev_tpD2] if k == 0 else [])
                t_ygt = S.sig("vector", lambda e, p=p: e.tensor_copy(
                    ygtsl(p), pD[:, 0:1024]), "V", 1,
                    waits=[ttp, prev_ygt_used[p]])
                prev_tpD2 = t_ygt

                wmm = None
                for k in range(NF):
                    for nh in range(2):
                        first = (k == 0 and nh == 0)
                        wmm = S.sig("tensor",
                                    lambda e, p=p, k=k, nh=nh: e.matmul(
                                        pA[:, nh * 512:(nh + 1) * 512],
                                        ygtsl(p)[:, k * 128:(k + 1) * 128],
                                        w_slot(8 + k)[:, nh * 512:(nh + 1) * 512],
                                        start=(k == 0), stop=(k == NF - 1)),
                                    "T", 1,
                                    waits=[t_ygt, tok_wol, prev_pA_user]
                                    if first else [])
                prev_ygt_used[p] = wmm
                if c == NCH - 1:
                    tok_wo15 = wmm
                grp2 = stat_g(8, p)
                sm2, sq2, rs2, nm2 = grp2
                t_ao = S.sig("scalar", lambda e, p=p, sm2=sm2: e.activation(
                    aosl(p), pA[:, :], AF.Identity, accum_out=sm2), "S", 1,
                    waits=[wmm, prev_ao_used[p]])
                prev_pA_user = t_ao
                t2b = S.sig("vector", lambda e, sm2=sm2: e.tensor_scalar(
                    sm2, sm2, -1.0 / D, None, ALU.mult), "V", 1, waits=[t_ao])
                t3b = S.sig("scalar", lambda e, p=p, sm2=sm2, sq2=sq2:
                            e.activation(trash2, aosl(p), AF.Square,
                                         bias=sm2, accum_out=sq2), "S", 1,
                            waits=[t2b])
                t4b = S.sig("scalar", lambda e, sq2=sq2, rs2=rs2: e.activation(
                    rs2, sq2, AF.Ln, scale=1.0 / D, bias=eps_col[:, :]), "S", 1,
                    waits=[t3b, tok_eps])
                t5b = S.sig("scalar", lambda e, rs2=rs2: e.activation(
                    rs2, rs2, AF.Exp, scale=-0.5), "S", 1, waits=[t4b])
                t6b = S.sig("vector", lambda e, sm2=sm2, rs2=rs2, nm2=nm2:
                            e.tensor_tensor(nm2, sm2, rs2, ALU.mult), "V", 1,
                            waits=[t5b])
                t_x0p = S.sig("sync", lambda e, c=c, p=p: e.dma_start(
                    out=x0sl(p), in_=d["x0"][c * CH:(c + 1) * CH, :]),
                    f"dxp{p}", 16, waits=[prev_x0p_used[p]])
                t_xn = S.sig("scalar",
                             lambda e, p=p, rs2=rs2, nm2=nm2: e.activation(
                                 xnsl(p), aosl(p), AF.Identity, scale=rs2,
                                 bias=nm2), "S", 1,
                             waits=[t6b, prev_xn_used[p]])
                prev_ao_used[p] = t_xn
                t_x1 = S.sig("vector", lambda e, p=p: e.tensor_tensor(
                    x1sl(p), xnsl(p), x0sl(p), ALU.add), "V", 1,
                    waits=[t_xn, t_x0p, prev_x1_used[p]])
                prev_xn_used[p] = t_x1
                prev_x0p_used[p] = t_x1
                tok_x1st[c] = S.sig("sync", lambda e, c=c, p=p: e.dma_start(
                    out=x1d.ap()[c * CH:(c + 1) * CH, :], in_=x1sl(p)),
                    "dx1st", 16, waits=[t_x1])

                grp3 = stat_g(16, p)
                rs3, nm3 = grp3[2], grp3[3]
                tok_ln3 = ln_stats(x1sl(p), grp3, t_x1, trash2)
                t_u2 = S.sig("scalar",
                             lambda e, p=p, rs3=rs3, nm3=nm3: e.activation(
                                 u2sl(p), x1sl(p), AF.Identity, scale=rs3,
                                 bias=nm3), "S", 1,
                             waits=[tok_ln3, prev_u2_used[p]])
                prev_x1_used[p] = t_u2

                ttp2 = None
                for k in range(NF):
                    ttp2 = S.sig("tensor", lambda e, p=p, k=k: e.transpose(
                        pD[:, 1024 + k * 128: 1024 + (k + 1) * 128],
                        u2sl(p)[:, k * 128:(k + 1) * 128], c_ident[:, :]),
                        "T", 1,
                        waits=[t_u2, prev_tpD2] if k == 0 else [])
                tok_u2T_ev[c] = S.sig("vector", lambda e, c=c: e.tensor_copy(
                    u2T_cell(c), _apn(pD, 1024, (128, 8), (1, 128))), "V", 1,
                    waits=[ttp2])
                prev_tpD2 = tok_u2T_ev[c]
                prev_u2_used[p] = tok_u2T_ev[c]

                gmm2 = None
                for k in range(NF):
                    gmm2 = S.sig("tensor", lambda e, c=c, k=k: e.matmul(
                        pB[:, 0:8],
                        _apn(arena, VAL + k * 2048 + c * 128, (1, 128)),
                        wgr_b[:, k * 8:(k + 1) * 8],
                        start=(k == 0), stop=False), "T", 1,
                        waits=[tok_u2T_ev[c], tok_wgr, prev_pB_user]
                        if k == 0 else [])
                gmm2 = S.sig("tensor", lambda e: e.matmul(
                    pB[:, 0:8], c_ones1128b[:, :], rbias_b[:, :],
                    start=False, stop=True), "T", 1, waits=[tok_rb])
                tok_gates[c] = S.sig("vector", lambda e, c=c: e.tensor_copy(
                    gates_all[:, c * 8:(c + 1) * 8], pB[:, 0:8]), "V", 1,
                    waits=[gmm2])
                prev_pB_user = tok_gates[c]
            TT["tok"].update(u2T=tok_u2T_ev, x1st=tok_x1st, gates=tok_gates)

        if max_stage >= 6:
            def bc16(t16):
                a = t16 if isinstance(t16, bass.AP) else t16[:, :]
                return bass.AP(a.tensor, a.offset,
                               [list(a.ap[0]), list(a.ap[1]), [0, E]])

            def g3(t):
                a = t if isinstance(t, bass.AP) else t[:, :]
                return a.rearrange("p (c e) -> p c e", e=E)
            _ai = c_iota8[:, :]
            iota_b = bass.AP(_ai.tensor, _ai.offset,
                             [list(_ai.ap[0]), [0, NCH], list(_ai.ap[1])])

            lastv = S.sig("vector", lambda e: e.tensor_copy(
                tkw, gates_all[:, :]), "V", 1,
                waits=[tok_gates[NCH - 1], prev_x0p_used[0], prev_x0p_used[1]])
            lastv = S.sig("vector", lambda e: e.memset(tkm, 0.0), "V", 1,
                          waits=[lastv])
            tok_m1 = None
            for it in range(TOPK):
                lastv = S.sig("vector", lambda e: e.tensor_reduce(
                    gmax[:, :], g3(tkw), AX.X, ALU.max), "V", 1, waits=[lastv])
                if it == 0:
                    tok_m1 = lastv = S.sig("vector", lambda e: e.tensor_copy(
                        gm1[:, :], gmax[:, :]), "V", 1, waits=[lastv])
                lastv = S.sig("vector", lambda e: e.tensor_tensor(
                    g3(tkt), g3(tkw), bc16(gmax), ALU.is_ge), "V", 1,
                    waits=[lastv])
                lastv = S.sig("vector", lambda e: e.tensor_scalar(
                    tkt2, tkt, -8.0, 8.0, ALU.mult, ALU.add), "V", 1,
                    waits=[lastv])
                lastv = S.sig("vector", lambda e: e.tensor_tensor(
                    g3(tkt), g3(tkt), iota_b, ALU.mult), "V", 1, waits=[lastv])
                lastv = S.sig("vector", lambda e: e.tensor_tensor(
                    tkt, tkt, tkt2, ALU.add), "V", 1, waits=[lastv])
                lastv = S.sig("vector", lambda e: e.tensor_reduce(
                    gidx[:, :], g3(tkt), AX.X, ALU.min), "V", 1, waits=[lastv])
                lastv = S.sig("vector", lambda e: e.tensor_tensor(
                    g3(tkt), iota_b, bc16(gidx), ALU.is_equal), "V", 1,
                    waits=[lastv])
                lastv = S.sig("vector", lambda e: e.tensor_tensor(
                    tkm, tkm, tkt, ALU.add), "V", 1, waits=[lastv])
                lastv = S.sig("vector", lambda e: e.tensor_scalar(
                    tkt, tkt, -3.0e38, None, ALU.mult), "V", 1, waits=[lastv])
                lastv = S.sig("vector", lambda e: e.tensor_tensor(
                    tkw, tkw, tkt, ALU.add), "V", 1, waits=[lastv])
            lastv = S.sig("vector", lambda e: e.tensor_tensor(
                g3(tkt), g3(gates_all[:, :]), bc16(gm1), ALU.subtract), "V", 1,
                waits=[lastv, tok_m1])
            t_exp = S.sig("scalar", lambda e: e.activation(
                tkt2, tkt, AF.Exp), "S", 1, waits=[lastv])
            lastv = S.sig("vector", lambda e: e.tensor_tensor(
                tkt, tkt2, tkm, ALU.mult), "V", 1, waits=[t_exp])
            lastv = S.sig("vector", lambda e: e.tensor_reduce(
                gz[:, :], g3(tkt), AX.X, ALU.add), "V", 1, waits=[lastv])
            lastv = S.sig("vector", lambda e: e.reciprocal(
                gmax[:, :], gz[:, :]), "V", 1, waits=[lastv])
            tok_w = S.sig("vector", lambda e: e.tensor_tensor(
                g3(w_all[:, :]), g3(tkt), bc16(gmax), ALU.mult), "V", 1,
                waits=[lastv])
            TT["tok"].update(w=tok_w)

        if max_stage >= 7:
            tok_r1, tok_r2, tok_w3 = {}, {}, {}
            free_r1 = [tok_g15, tok_g15]
            free_r2 = [tok_g15, tok_g15]
            free_w3 = [(tok_g15 if (4 + sl) < 8 else tok_wo15)
                       for sl in range(11)]

            ffn = {
                "pA": prev_pA_user, "pB": prev_pB_user, "pC": prev_pC_user,
                "s1": [prev_u2_used[0], prev_u2_used[1]],
                "stage": [prev_x1_used[0], prev_x1_used[1]],
                "last_z": None,
                "accd": [None] * NCH,
            }
            u2T_all = tok_u2T_ev[NCH - 1]

            def ffn_h_block(ei, th, m, slot):
                w1s, w2s = w_slot(slot), w_slot(2 + slot)
                mm1 = None
                for k in range(NF):
                    for nh in range(2):
                        first = (k == 0 and nh == 0)
                        mm1 = S.sig(
                            "tensor",
                            lambda e, k=k, nh=nh, w1s=w1s, th=th: e.matmul(
                                pA[:, nh * 512:(nh + 1) * 512],
                                w1s[:, k * 128:(k + 1) * 128],
                                u2T_rhs(k, th, nh),
                                start=(k == 0), stop=(k == NF - 1)),
                            "T", 1,
                            waits=[tok_r1[(ei, th, m)], u2T_all, ffn["pA"]]
                            if first else [])
                mm2 = None
                for k in range(NF):
                    for nh in range(2):
                        first = (k == 0 and nh == 0)
                        mm2 = S.sig(
                            "tensor",
                            lambda e, k=k, nh=nh, w2s=w2s, th=th: e.matmul(
                                pB[:, nh * 512:(nh + 1) * 512],
                                w2s[:, k * 128:(k + 1) * 128],
                                u2T_rhs(k, th, nh),
                                start=(k == 0), stop=(k == NF - 1)),
                            "T", 1,
                            waits=[tok_r2[(ei, th, m)], u2T_all, ffn["pB"]]
                            if first else [])
                sil = S.sig("scalar", lambda e: e.activation(
                    s1sl(0), pA[:, :], AF.Silu), "S", 1,
                    waits=[mm1, ffn["s1"][0]])
                ffn["pA"] = sil
                t_z = S.sig("vector", lambda e, m=m: e.tensor_tensor(
                    z_m(m), s1sl(0), pB[:, :], ALU.mult), "V", 1,
                    waits=[sil, mm2])
                ffn["s1"][0] = t_z
                ffn["pB"] = t_z
                ffn["last_z"] = t_z
                free_r1[slot] = mm1
                free_r2[slot] = mm2

            def ffn_w3_phase(ei, th):
                for half in range(2):
                    ms = W3A if half == 0 else W3B
                    for m in ms:
                        slot = m if m < 11 else m - 11
                        src3 = (d["ws3"][m] if ei == 0 else d["w3"][ei - 1, m])
                        tok_w3[(ei, th, m)] = S.sig(
                            "gpsimd",
                            lambda e, s=src3, t=w_slot(4 + slot): e.dma_start(
                                out=t, in_=s),
                            f"w3s{slot}", 16, waits=[free_w3[slot]])
                    for c8 in range(8):
                        c = th * 8 + c8
                        par = c8 % 2
                        eps_ = pA if par == 0 else pB
                        pfree = ffn["pA"] if par == 0 else ffn["pB"]
                        mm = None
                        for mi, m in enumerate(ms):
                            slot = m if m < 11 else m - 11
                            for nh in range(2):
                                w_ = [tok_w3[(ei, th, m)]] if nh == 0 else []
                                if mi == 0 and nh == 0:
                                    w_ = w_ + [ffn["last_z"], pfree]
                                mm = S.sig(
                                    "tensor",
                                    lambda e, m=m, c8=c8, nh=nh, slot=slot,
                                    eps_=eps_, mi=mi, nms=len(ms): e.matmul(
                                        eps_[:, nh * 512:(nh + 1) * 512],
                                        z_tile(m, c8),
                                        w_slot(4 + slot)[:, nh * 512:(nh + 1) * 512],
                                        start=(mi == 0), stop=(mi == nms - 1)),
                                    "T", 1, waits=w_)
                            if c8 == 7:
                                free_w3[slot] = mm
                        sp = par
                        if ei == 0:
                            t_cp = S.sig(
                                "vector",
                                lambda e, sp=sp, eps_=eps_: e.tensor_copy(
                                    stage_sl(sp), eps_[:, :]), "V", 1,
                                waits=[mm, ffn["stage"][sp]])
                        else:
                            t_cp = S.sig(
                                "vector",
                                lambda e, sp=sp, eps_=eps_, c=c, ei=ei:
                                e.tensor_scalar(
                                    stage_sl(sp), eps_[:, :],
                                    w_all[:, c * 8 + (ei - 1): c * 8 + ei],
                                    None, ALU.mult),
                                "V", 1, waits=[mm, ffn["stage"][sp], tok_w])
                        if par == 0:
                            ffn["pA"] = t_cp
                        else:
                            ffn["pB"] = t_cp
                        accum = not (ei == 0 and half == 0)
                        t_dma = S.sig(
                            "gpsimd",
                            lambda e, sp=sp, c=c, accum=accum: e.dma_start(
                                out=accd.ap()[c * CH:(c + 1) * CH, :],
                                in_=stage_sl(sp),
                                **({"accum_op": ALU.add} if accum else {})),
                            f"dsg{sp}", 16, waits=[t_cp, ffn["accd"][c]])
                        ffn["stage"][sp] = t_dma
                        ffn["accd"][c] = t_dma

            for th in range(2):
                for ei in range(FFN_NEI):
                    for m in range(NHT):
                        slot = m % 2
                        src1 = (d["ws1"][m] if ei == 0 else d["w1"][ei - 1, m])
                        tok_r1[(ei, th, m)] = S.sig(
                            "gpsimd",
                            lambda e, s=src1, t=w_slot(slot): e.dma_start(
                                out=t.rearrange("p (k m) -> p k m", m=128),
                                in_=s.rearrange("k p m -> p k m")),
                            f"r1s{slot}", 16, waits=[free_r1[slot]])
                        src2 = (d["ws2"][m] if ei == 0 else d["w2"][ei - 1, m])
                        tok_r2[(ei, th, m)] = S.sig(
                            "gpsimd",
                            lambda e, s=src2, t=w_slot(2 + slot): e.dma_start(
                                out=t.rearrange("p (k m) -> p k m", m=128),
                                in_=s.rearrange("k p m -> p k m")),
                            f"r2s{slot}", 16, waits=[free_r2[slot]])
                        ffn_h_block(ei, th, m, slot)
                    ffn_w3_phase(ei, th)
            TT["tok"].update(last_z=ffn["last_z"])

        if max_stage >= 9:
            wait_acc = [("dsg0", S.counts.get("dsg0", 0)),
                        ("dsg1", S.counts.get("dsg1", 0))]
            wait_x1 = [("dx1st", S.counts.get("dx1st", 0))]
            prev_acc_used = [("V", S.counts["V"]), ("V", S.counts["V"])]
            prev_x1b_used = [ffn["stage"][0], ffn["stage"][1]]
            prev_xn4_used = [prev_xn_used[0], prev_xn_used[1]]
            for c in range(NCH):
                p = c % 2
                t_la = S.sig("sync", lambda e, c=c, p=p: e.dma_start(
                    out=x0sl(p), in_=accd.ap()[c * CH:(c + 1) * CH, :]),
                    f"dba{p}", 16, waits=wait_acc + [prev_acc_used[p]])
                t_lx = S.sig("sync", lambda e, c=c, p=p: e.dma_start(
                    out=x1sl(p), in_=x1d.ap()[c * CH:(c + 1) * CH, :]),
                    f"dbx{p}", 16, waits=wait_x1 + [prev_x1b_used[p]])
                grp = stat_g(0, p)
                tok_ln4 = ln_stats(x0sl(p), grp, t_la, trash2)
                rs4, nm4 = grp[2], grp[3]
                t_xn4 = S.sig("scalar",
                              lambda e, p=p, rs4=rs4, nm4=nm4: e.activation(
                                  xnsl(p), x0sl(p), AF.Identity, scale=rs4,
                                  bias=nm4), "S", 1,
                              waits=[tok_ln4, prev_xn4_used[p]])
                t_x2 = S.sig("vector", lambda e, p=p: e.tensor_tensor(
                    x0sl(p), xnsl(p), x1sl(p), ALU.add), "V", 1,
                    waits=[t_xn4, t_lx])
                prev_xn4_used[p] = t_x2
                prev_x1b_used[p] = t_x2
                t_so = S.sig("sync", lambda e, c=c, p=p: e.dma_start(
                    out=out_d[c * CH:(c + 1) * CH, :], in_=x0sl(p)),
                    "dout", 16, waits=[t_x2])
                prev_acc_used[p] = t_so
            S.emit("sync", lambda e: e.nop(),
                   waits=[("dout", S.counts.get("dout", 0))])

        if dbg is not None:
            dbg(nc, S, TT)

        handles = {}
        for name in S.counts:
            handles[name] = es.enter_context(nc.semaphore(name))
        with nc.Block() as block:
            S.realize(block, handles)

    return nc


def kernel(**inputs):
    in_maps = _host_prep(inputs)
    nc = build_program()
    res = run_bass_kernel_spmd(nc, in_maps, core_ids=list(range(NCORE)))
    out = np.empty((B, L, D), np.float32)
    for c in range(NCORE):
        b, s = c // 4, c % 4
        out[b, s * T:(s + 1) * T, :] = np.asarray(res.results[c]["out"])
    return out


# version 02
# speedup vs baseline: 1.0006x; 1.0006x over previous
import contextlib

import numpy as np
import ml_dtypes

import concourse.bass as bass
import concourse.mybir as mybir
from concourse.bass_utils import run_bass_kernel_spmd

F32 = mybir.dt.float32
BF16 = mybir.dt.bfloat16
AF = mybir.ActivationFunctionType
ALU = mybir.AluOpType
AX = mybir.AxisListType
bfloat16 = ml_dtypes.bfloat16

B, L, D = 2, 8192, 1024
NCORE = 8
T = 2048
NCH, CH = 16, 128
NF = 8
NH, DH = 16, 64
HDIM, HPAD, NHT = 2730, 2816, 22
E = 8
TOPK = 4
LN_EPS = 1e-5
TINY = float(np.finfo(np.float32).tiny)
LN2 = float(np.log(2.0))
BIG = 1.0e38
FFN_NEI = E + 1
W3A = list(range(0, 11))
W3B = list(range(11, NHT))

_ENGINES = ("sync", "gpsimd", "vector", "scalar", "tensor")


def _collapse(waits):
    best = {}
    for w in waits:
        if w is None:
            continue
        s, v = w
        if s not in best or v > best[s]:
            best[s] = v
    return tuple(best.items())


class _Sched:
    def __init__(self):
        self.ops = {e: [] for e in _ENGINES}
        self.counts = {}

    def emit(self, eng, fn, waits=(), sig=()):
        sig = tuple(sig)
        self.ops[eng].append((_collapse(waits), fn, sig))
        toks = []
        for s, a in sig:
            self.counts[s] = self.counts.get(s, 0) + a
            toks.append((s, self.counts[s]))
        if len(toks) == 1:
            return toks[0]
        return toks

    def sig(self, eng, fn, sem, amt=1, waits=()):
        return self.emit(eng, fn, waits=waits, sig=[(sem, amt)])

    def realize(self, block, handles):
        for ename in _ENGINES:
            ops = self.ops[ename]

            def body(eng, ops=ops):
                for waits, fn, sig in ops:
                    for s, v in waits:
                        eng.wait_ge(handles[s], v)
                    inst = fn(eng)
                    for s, a in sig:
                        inst.then_inc(handles[s], a)

            getattr(block, ename)(body)


def _apn(t, off, *dims):
    a = t[:, :]
    return bass.AP(a.tensor, off,
                   [list(a.ap[0])] + [[s, c] for (s, c) in dims])



def _lhsT_tiles(wT, k, m):
    assert wT.shape == (k, m) and k % 128 == 0 and m % 128 == 0
    return np.ascontiguousarray(
        wT.reshape(k // 128, 128, m // 128, 128).transpose(2, 0, 1, 3))


def _pad_h_cols(wT):
    out = np.zeros((wT.shape[0], HPAD), np.float32)
    out[:, :HDIM] = wT
    return out


def _pad_h_rows(wT):
    out = np.zeros((HPAD, wT.shape[1]), np.float32)
    out[:HDIM, :] = wT
    return out


def _host_prep(inputs):
    f32 = lambda a: np.ascontiguousarray(np.asarray(a, np.float32))
    x = f32(inputs["L0_attn_x"])
    sh = {}
    sh["wq"] = np.ascontiguousarray(f32(inputs["L0_attn_Wq"]).T)
    sh["wv"] = np.ascontiguousarray(f32(inputs["L0_attn_Wv"]).T)
    sh["wg"] = np.ascontiguousarray(f32(inputs["L0_attn_Wg"]).T)
    sh["wo"] = np.ascontiguousarray(f32(inputs["L0_attn_Wo"]).T)
    beta = f32(inputs["L0_attn_beta"]).reshape(16)
    sh["beta_row"] = np.ascontiguousarray(beta.reshape(1, 16))
    sh["beta_col"] = np.ascontiguousarray(beta.reshape(16, 1))
    sh["gamma"] = f32(inputs["L0_attn_ln_gamma"]).reshape(1, D)
    sh["lnbeta"] = f32(inputs["L0_attn_ln_beta"]).reshape(1, D)
    sh["wgr"] = np.ascontiguousarray(f32(inputs["L0_ffn_Wg"]).T)
    sh["rbias"] = f32(inputs["L0_ffn_routing_bias"]).reshape(1, E)
    sh["ws1"] = _lhsT_tiles(_pad_h_cols(f32(inputs["L0_ffn_Ws1"]).T), D, HPAD)
    sh["ws2"] = _lhsT_tiles(_pad_h_cols(f32(inputs["L0_ffn_Ws2"]).T), D, HPAD)
    sh["ws3"] = np.ascontiguousarray(
        _pad_h_rows(f32(inputs["L0_ffn_Ws3"]).T).reshape(NHT, 128, D))
    sh["w1"] = np.stack(
        [_lhsT_tiles(_pad_h_cols(f32(inputs["L0_ffn_W1"][e]).T), D, HPAD)
         for e in range(E)])
    sh["w2"] = np.stack(
        [_lhsT_tiles(_pad_h_cols(f32(inputs["L0_ffn_W2"][e]).T), D, HPAD)
         for e in range(E)])
    sh["w3"] = np.stack(
        [np.ascontiguousarray(
            _pad_h_rows(f32(inputs["L0_ffn_W3"][e]).T).reshape(NHT, 128, D))
         for e in range(E)])

    sh["ident"] = np.eye(128, dtype=bfloat16)
    e127 = np.zeros((128, 1), bfloat16); e127[127, 0] = 1
    sh["e127"] = e127
    sh["ones116"] = np.ones((1, 16), bfloat16)
    sh["ones1128b"] = np.ones((1, 128), bfloat16)
    sh["ones1128f"] = np.ones((1, 128), np.float32)
    d2t = np.full((128, 128), BIG, np.float32)
    for k in range(128):
        d2t[k, k:] = np.arange(0, 128 - k, dtype=np.float32)
    sh["d2t"] = d2t.astype(bfloat16)
    sh["tp16"] = np.broadcast_to(
        np.arange(1, 129, dtype=np.float32), (16, 128)).astype(bfloat16).copy()
    dlx = np.full((17, 16), BIG, np.float32)
    for s in range(17):
        for c in range(16):
            if c >= s:
                dlx[s, c] = 128.0 * (c - s)
    sh["dlx"] = dlx
    dend = np.full((17, 1), BIG, np.float32)
    for s in range(1, 17):
        dend[s, 0] = 128.0 * (16 - s)
    sh["dend"] = dend
    bdmask = np.zeros((16, D), bfloat16)
    for h in range(16):
        bdmask[h, h * 64:(h + 1) * 64] = 1
    sh["bdmask"] = bdmask
    sh["iota8"] = np.broadcast_to(
        np.arange(E, dtype=np.float32), (128, E)).copy()

    in_maps = []
    for c in range(NCORE):
        b, s = c // 4, c % 4
        dcore = np.full((8, 1), BIG, np.float32)
        for i in range(8):
            if i // 4 == b and (i % 4) < s:
                dcore[i, 0] = 2048.0 * (s - 1 - (i % 4))
        m = dict(sh)
        m["x0"] = np.ascontiguousarray(x[b, s * T:(s + 1) * T, :])
        m["dcore8"] = dcore
        in_maps.append(m)
    return in_maps



def build_program(max_stage=99, dbg=None):
    nc = bass.Bass("TRN2", target_bir_lowering=False)

    d = {}
    def din(name, shape, dtype=F32):
        d[name] = nc.dram_tensor(name, shape, dtype, kind="ExternalInput").ap()
    din("x0", [T, D])
    for n in ("wq", "wv", "wg", "wo"):
        din(n, [D, D])
    din("wgr", [D, E]); din("rbias", [1, E])
    din("ws1", [NHT, NF, 128, 128]); din("ws2", [NHT, NF, 128, 128])
    din("ws3", [NHT, 128, D])
    din("w1", [E, NHT, NF, 128, 128]); din("w2", [E, NHT, NF, 128, 128])
    din("w3", [E, NHT, 128, D])
    din("beta_row", [1, 16]); din("beta_col", [16, 1])
    din("gamma", [1, D]); din("lnbeta", [1, D])
    din("ident", [128, 128], BF16); din("e127", [128, 1], BF16)
    din("ones116", [1, 16], BF16); din("ones1128b", [1, 128], BF16)
    din("ones1128f", [1, 128])
    din("d2t", [128, 128], BF16); din("tp16", [16, 128], BF16)
    din("dlx", [17, 16]); din("dend", [17, 1]); din("dcore8", [8, 1])
    din("bdmask", [16, D], BF16); din("iota8", [128, E])

    out_d = nc.dram_tensor("out", [T, D], F32, kind="ExternalOutput").ap()
    x1d = nc.dram_tensor("x1d", [T, D], F32, kind="Internal")
    qd = nc.dram_tensor("qd", [T, D], BF16, kind="Internal")
    accd = nc.dram_tensor("accd", [T, D], F32, kind="Internal")
    dscr = nc.dram_tensor("dscr", [16 * D], BF16, kind="Internal")
    dscr2 = nc.dram_tensor("dscr2", [16 * D], BF16, kind="Internal")
    cc_in = nc.dram_tensor("cc_in", [1, D], F32, kind="Internal")
    cc_out = nc.dram_tensor("cc_out", [8, D], F32, kind="Internal")

    S = _Sched()
    TT = {"tok": {}}

    with contextlib.ExitStack() as es:
        def sb(name, shape, dtype):
            t = es.enter_context(nc.sbuf_tensor(name, shape, dtype))
            TT[name] = t
            return t

        def pstn(name, shape, dtype):
            t = es.enter_context(nc.psum_tensor(name, shape, dtype))
            TT[name] = t
            return t

        arena = sb("arena", [128, 49152], BF16)
        U1T, ST0, VAL = 0, 16384, 32768
        warena = sb("warena", [128, 16384], BF16)
        fpool = sb("fpool", [128, 4096], F32)
        bpool = sb("bpool", [128, 18 * 1024], BF16)
        BGS, BST, BRET, BY, BYGT, BAO, BU2, BXN, BQT = (
            0, 2048, 4096, 6144, 8192, 10240, 12288, 14336, 16384)

        gamma_rep = sb("gamma_rep", [128, D], BF16)
        beta_rep = sb("beta_rep", [128, D], BF16)
        gates_all = sb("gates_all", [128, NCH * E], F32)
        w_all = sb("w_all", [128, NCH * E], F32)
        gmax = sb("gmax", [128, NCH], F32)
        gidx = sb("gidx", [128, NCH], F32)
        gm1 = sb("gm1", [128, NCH], F32)
        gz = sb("gz", [128, NCH], F32)

        c_ident = sb("c_ident", [128, 128], BF16)
        c_e127 = sb("c_e127", [128, 1], BF16)
        c_ones116 = sb("c_ones116", [1, 16], BF16)
        c_ones1128b = sb("c_ones1128b", [1, 128], BF16)
        c_ones1128f = sb("c_ones1128f", [1, 128], F32)
        c_d2t = sb("c_d2t", [128, 128], BF16)
        c_tp16 = sb("c_tp16", [16, 128], BF16)
        c_dlx = sb("c_dlx", [17, 16], F32)
        c_dend = sb("c_dend", [17, 1], F32)
        c_dcore = sb("c_dcore", [8, 1], F32)
        c_bdmask = sb("c_bdmask", [16, D], BF16)
        c_iota8 = sb("c_iota8", [128, E], F32)
        beta_row = sb("beta_row_sb", [1, 16], F32)
        beta_col = sb("beta_col_sb", [16, 1], F32)
        lnl_row = sb("lnl_row", [1, 16], F32)
        lnl_col = sb("lnl_col", [16, 1], F32)
        lam_rep = sb("lam_rep", [128, 16], F32)
        A_all = sb("A_all", [128, NH * 128], BF16)
        P_all = sb("P_all", [16, 128], BF16)
        Lx_all = sb("Lx_all", [17, NH * 16], BF16)
        EndW = sb("EndW", [17, 16], BF16)
        CoreW = sb("CoreW", [8, 16], BF16)
        eps_col = sb("eps_col", [128, 1], F32)
        rbias_b = sb("rbias_b", [1, E], BF16)
        rbias_f = sb("rbias_f", [1, E], F32)
        wgr_b = sb("wgr_b", [128, NF * E], BF16)
        stat = sb("stat", [128, 24], F32)
        carr_buf = sb("carr_buf", [1, D], BF16)
        bd_sb = sb("bd_sb", [16, D], BF16)

        pA = pstn("pA", [128, 1024], F32)
        pB = pstn("pB", [128, 1024], F32)
        pC = pstn("pC", [128, 1024], F32)
        pD = pstn("pD", [128, 2048], BF16)

        x0sl = lambda p: fpool[:, p * 1024:(p + 1) * 1024]
        x1sl = lambda p: fpool[:, (2 + p) * 1024:(3 + p) * 1024]
        stage_sl = x1sl
        tkw = fpool[:, 0:128]; tkm = fpool[:, 128:256]
        tkt = fpool[:, 256:384]; tkt2 = fpool[:, 384:512]
        rend_sb = fpool[0:1, 2 * 1024:3 * 1024]
        rendall = fpool[0:8, 3 * 1024:4 * 1024]

        gsl = lambda p: bpool[:, BGS + p * 1024: BGS + (p + 1) * 1024]
        stsl = lambda p: bpool[:, BST + p * 1024: BST + (p + 1) * 1024]
        retsl = lambda p: bpool[:, BRET + p * 1024: BRET + (p + 1) * 1024]
        ysl = lambda i: bpool[:, BY + i * 1024: BY + (i + 1) * 1024]
        u1sl = ysl
        ygtsl = lambda p: bpool[:, BYGT + p * 1024: BYGT + (p + 1) * 1024]
        aosl = lambda p: bpool[:, BAO + p * 1024: BAO + (p + 1) * 1024]
        u2sl = lambda p: bpool[:, BU2 + p * 1024: BU2 + (p + 1) * 1024]
        s1sl = u2sl
        xnsl = lambda p: bpool[:, BXN + p * 1024: BXN + (p + 1) * 1024]
        qtsl = lambda p: bpool[:, BQT + p * 1024: BQT + (p + 1) * 1024]
        ln1_trash = bpool[:, BGS:BGS + 1024]
        trash2 = A_all[:, 0:1024]
        r_buf = bpool[0:1, BGS:BGS + 2048]
        rendall_b = bpool[0:8, BRET:BRET + 1024]
        rx_mat = bpool[0:17, BAO:BAO + 1024]
        carr_sb = bpool[0:16, BYGT:BYGT + 1024]
        gamma_in = bpool[0:1, BU2:BU2 + 1024]
        lnbeta_in = bpool[0:1, BU2 + 1024:BU2 + 2048]

        def stat_g(base, parity):
            o = base + 4 * parity
            return (stat[:, o:o + 1], stat[:, o + 1:o + 2],
                    stat[:, o + 2:o + 3], stat[:, o + 3:o + 4])

        def w_slot(i):
            return warena[:, i * 1024:(i + 1) * 1024]

        def w_region(i0):
            return warena[:, i0 * 1024:(i0 + 8) * 1024]

        def u1T_tile(k, c):
            return _apn(arena, U1T + k * 2048 + c * 128, (1, 128))

        def st0_sl(c):
            return _apn(arena, ST0 + c * 1024, (1, 1024))

        def z_tile(m, c8):
            return _apn(arena, m * 1024 + c8 * 128, (1, 128))

        def z_m(m):
            return _apn(arena, m * 1024, (1, 1024))

        def u2T_rhs(k, th, nh):
            return _apn(arena, VAL + k * 2048 + th * 1024 + nh * 512, (1, 512))

        def u2T_cell(c):
            return _apn(arena, VAL + c * 128, (2048, 8), (1, 128))

        creqs = [
            (d["d2t"], c_d2t), (d["tp16"], c_tp16), (d["dlx"], c_dlx),
            (d["dend"], c_dend), (d["dcore8"], c_dcore), (d["iota8"], c_iota8),
            (d["beta_row"], beta_row), (d["beta_col"], beta_col),
            (d["rbias"], rbias_f),
            (d["ident"], c_ident), (d["e127"], c_e127),
            (d["ones116"], c_ones116), (d["ones1128b"], c_ones1128b),
            (d["ones1128f"], c_ones1128f), (d["bdmask"], c_bdmask),
        ]
        for src, dst in creqs:
            S.sig("sync", lambda e, s=src, t=dst: e.dma_start(out=t[:, :], in_=s),
                  "dcst", 16)
        tok_cst = ("dcst", 16 * len(creqs))

        S.sig("gpsimd", lambda e: e.dma_start(
            out=gamma_in, in_=d["gamma"]), "dgb", 16)
        S.sig("gpsimd", lambda e: e.dma_start(
            out=lnbeta_in, in_=d["lnbeta"]), "dgb", 16)
        tok_gb = ("dgb", 32)
        tok_wgr = S.sig("gpsimd", lambda e: e.dma_start(
            out=wgr_b[:, :].rearrange("p (k m) -> p k m", m=E),
            in_=d["wgr"].rearrange("(k p) m -> p k m", p=128)),
            "dwgr", 16)
        tok_wv = S.sig("gpsimd", lambda e: e.dma_start(
            out=w_region(0).rearrange("p (k m) -> p k m", m=1024), in_=d["wv"].rearrange("(k p) m -> p k m", p=128)),
            "wva", 16)
        tok_wq = S.sig("gpsimd", lambda e: e.dma_start(
            out=w_region(8).rearrange("p (k m) -> p k m", m=1024), in_=d["wq"].rearrange("(k p) m -> p k m", p=128)),
            "wqa", 16)

        tok_eps = S.sig("vector", lambda e: e.memset(eps_col[:, :], LN_EPS),
                        "V", 1)

        def lam_pipe(src, dst):
            t1 = S.sig("scalar", lambda e: e.activation(
                dst[:, :], src[:, :], AF.Exp, scale=-LN2), "S", 1,
                waits=[tok_cst])
            t2 = S.sig("vector", lambda e: e.tensor_scalar(
                dst[:, :], dst[:, :], -1.0, 1.0, ALU.mult, ALU.add), "V", 1,
                waits=[t1])
            t3 = S.sig("vector", lambda e: e.tensor_scalar(
                dst[:, :], dst[:, :], TINY, 1.0 - 1e-9, ALU.max, ALU.min),
                "V", 1, waits=[t2])
            return S.sig("scalar", lambda e: e.activation(
                dst[:, :], dst[:, :], AF.Ln), "S", 1, waits=[t3])
        tok_lr = lam_pipe(beta_row, lnl_row)
        tok_lc = lam_pipe(beta_col, lnl_col)

        tok_t = S.sig("tensor", lambda e: e.matmul(
            pA[:, 0:16], c_ones1128f[:, :], lnl_row[:, :], start=True,
            stop=True), "T", 1, waits=[tok_cst, tok_lr])
        tok_lrep = S.sig("vector", lambda e: e.tensor_copy(
            lam_rep[:, :], pA[:, 0:16]), "V", 1, waits=[tok_t])

        def rep_row(src_row, dst, prev):
            tks = None
            for half in range(2):
                tks = S.sig("tensor", lambda e, h=half: e.matmul(
                    pB[:, h * 512:(h + 1) * 512], c_ones1128b[:, :],
                    src_row[0:1, h * 512:(h + 1) * 512], start=True, stop=True),
                    "T", 1, waits=[tok_cst, tok_gb, prev] if half == 0 else [])
            return S.sig("vector", lambda e: e.tensor_copy(dst[:, :], pB[:, :]),
                         "V", 1, waits=[tks])
        tok_grep = rep_row(gamma_in, gamma_rep, None)
        tok_brep = rep_row(lnbeta_in, beta_rep, tok_grep)

        tok_rb = S.sig("vector", lambda e: e.tensor_copy(
            rbias_b[:, :], rbias_f[:, :]), "V", 1, waits=[tok_cst])

        def act_build(dst_ap, src_ap, scale_ap, waits):
            return S.sig("scalar", lambda e: e.activation(
                dst_ap, src_ap, AF.Exp, scale=scale_ap), "S", 1, waits=waits)
        tok_A = tok_P = tok_Lx = tok_End = tok_Core = None
        for h in range(NH):
            tok_A = act_build(A_all[:, h * 128:(h + 1) * 128], c_d2t[:, :],
                              lam_rep[:, h:h + 1], [tok_lrep, tok_cst])
        tok_P = act_build(P_all[:, :], c_tp16[:, :], lnl_col[:, :],
                          [tok_lc, tok_cst])
        for h in range(NH):
            tok_Lx = act_build(Lx_all[:, h * 16:(h + 1) * 16], c_dlx[:, :],
                               lam_rep[0:17, h:h + 1], [tok_lrep, tok_cst])
        for h in range(NH):
            tok_End = act_build(EndW[:, h:h + 1], c_dend[:, :],
                                lam_rep[0:17, h:h + 1], [tok_lrep, tok_cst])
        for h in range(NH):
            tok_Core = act_build(CoreW[:, h:h + 1], c_dcore[:, :],
                                 lam_rep[0:8, h:h + 1], [tok_lrep, tok_cst])

        def ln_stats(in_ap, grp, wait_in, trash_ap):
            sm, sq, rs, nm = grp
            t1 = S.sig("vector", lambda e: e.tensor_reduce(
                sm, in_ap, AX.X, ALU.add), "V", 1, waits=[wait_in])
            t2 = S.sig("vector", lambda e: e.tensor_scalar(
                sm, sm, -1.0 / D, None, ALU.mult), "V", 1, waits=[t1])
            t3 = S.sig("scalar", lambda e: e.activation(
                trash_ap, in_ap, AF.Square, bias=sm, accum_out=sq), "S", 1,
                waits=[t2])
            t4 = S.sig("scalar", lambda e: e.activation(
                rs, sq, AF.Ln, scale=1.0 / D, bias=eps_col[:, :]), "S", 1,
                waits=[t3, tok_eps])
            t5 = S.sig("scalar", lambda e: e.activation(
                rs, rs, AF.Exp, scale=-0.5), "S", 1, waits=[t4])
            t6 = S.sig("vector", lambda e: e.tensor_tensor(
                nm, sm, rs, ALU.mult), "V", 1, waits=[t5])
            return t6

        if max_stage >= 2:
            tok_x0 = [None] * NCH
            tok_u1T_ev = [None] * NCH
            tok_vev = [None] * NCH
            tok_qst = [None] * NCH
            prev_tpD = None
            prev_u1_used = [None, None]
            prev_x0_used = [None, None]
            prev_pAB_ev = [tok_lrep, tok_brep]
            prev_qt_used = [None, None]

            for c in range(NCH):
                p = c % 2
                tok_x0[c] = S.sig("sync", lambda e, c=c, p=p: e.dma_start(
                    out=x0sl(p), in_=d["x0"][c * CH:(c + 1) * CH, :]),
                    f"dx0{p}", 16, waits=[prev_x0_used[p]])
                grp = stat_g(0, p)
                tok_ln1 = ln_stats(x0sl(p), grp, tok_x0[c], ln1_trash)
                rs, nm = grp[2], grp[3]
                t7 = S.sig("scalar", lambda e, p=p, rs=rs, nm=nm: e.activation(
                    u1sl(p), x0sl(p), AF.Identity, scale=rs, bias=nm), "S", 1,
                    waits=[tok_ln1, prev_u1_used[p]])
                prev_x0_used[p] = t7

                ttp = None
                for k in range(NF):
                    ttp = S.sig("tensor", lambda e, p=p, k=k: e.transpose(
                        pD[:, k * 128:(k + 1) * 128],
                        u1sl(p)[:, k * 128:(k + 1) * 128], c_ident[:, :]),
                        "T", 1, waits=[t7, prev_tpD, tok_cst] if k == 0 else [])
                ev = None
                for half in range(2):
                    ev = S.sig("vector", lambda e, c=c, half=half: e.tensor_copy(
                        _apn(arena, U1T + half * 4 * 2048 + c * 128,
                             (2048, 4), (1, 128)),
                        _apn(pD, half * 512, (128, 4), (1, 128))),
                        "V", 1, waits=[ttp] if half == 0 else [])
                prev_tpD = ev
                prev_u1_used[p] = ev
                tok_u1T_ev[c] = ev

                vps = pA if p == 0 else pB
                tmm = None
                for k in range(NF):
                    for nh in range(2):
                        first = (k == 0 and nh == 0)
                        tmm = S.sig("tensor",
                                    lambda e, c=c, k=k, nh=nh, vps=vps: e.matmul(
                                        vps[:, nh * 512:(nh + 1) * 512],
                                        u1T_tile(k, c),
                                        w_slot(k)[:, nh * 512:(nh + 1) * 512],
                                        start=(k == 0), stop=(k == NF - 1)),
                                    "T", 1,
                                    waits=[ev, tok_wv, prev_pAB_ev[p]]
                                    if first else [])
                tok_vev[c] = S.sig("scalar", lambda e, c=c, vps=vps: e.activation(
                    _apn(arena, VAL + c * 1024, (1, 1024)), vps[:, :],
                    AF.Identity), "S", 1, waits=[tmm])
                qmm = None
                for k in range(NF):
                    for nh in range(2):
                        first = (k == 0 and nh == 0)
                        qmm = S.sig("tensor",
                                    lambda e, c=c, k=k, nh=nh, vps=vps: e.matmul(
                                        vps[:, nh * 512:(nh + 1) * 512],
                                        u1T_tile(k, c),
                                        w_slot(8 + k)[:, nh * 512:(nh + 1) * 512],
                                        start=(k == 0), stop=(k == NF - 1)),
                                    "T", 1,
                                    waits=[tok_vev[c], tok_wq] if first else [])
                tok_qev = S.sig("scalar", lambda e, p=p, vps=vps: e.activation(
                    qtsl(p), vps[:, :], AF.Identity), "S", 1,
                    waits=[qmm, prev_qt_used[p]])
                tok_qst[c] = S.sig("sync", lambda e, c=c, p=p: e.dma_start(
                    out=qd.ap()[c * CH:(c + 1) * CH, :], in_=qtsl(p)),
                    f"dqs{p}", 16, waits=[tok_qev])
                prev_qt_used[p] = tok_qst[c]
                prev_pAB_ev[p] = tok_qev
            TT["tok"].update(vev=tok_vev, u1T=tok_u1T_ev, qst=tok_qst)

        if max_stage >= 3:
            tok_st0 = [None] * NH
            prev_h_ev = [prev_pAB_ev[0], prev_pAB_ev[1]]
            for h in range(NH):
                hp = h % 2
                hps = pA if hp == 0 else pB
                mm = None
                for half in range(2):
                    mm = S.sig("tensor",
                               lambda e, h=h, half=half, hps=hps: e.matmul(
                                   hps[:, half * 512:(half + 1) * 512],
                                   A_all[:, h * 128:(h + 1) * 128],
                                   _apn(arena, VAL + half * 8 * 1024 + h * 64,
                                        (1024, 8), (1, 64)),
                                   start=True, stop=True), "T", 1,
                               waits=[tok_A, tok_vev[NCH - 1], prev_h_ev[hp]]
                               if half == 0 else [])
                tok_st0[h] = S.sig("vector", lambda e, h=h, hps=hps: e.tensor_copy(
                    _apn(arena, ST0 + h * 64, (1024, 16), (1, 64)),
                    _apn(hps, 0, (64, 16), (1, 64))),
                    "V", 1, waits=[mm])
                prev_h_ev[hp] = tok_st0[h]

            tok_rx0 = S.sig("vector", lambda e: e.memset(rx_mat, 0.0),
                            "V", 1, waits=[prev_u1_used[0], prev_u1_used[1]])
            tok_rdma = [None] * NCH
            prev_r = [prev_h_ev[0], prev_h_ev[1]]
            prev_rb = [tok_vev[NCH - 1], tok_vev[NCH - 1]]
            for c in range(NCH):
                rp = c % 2
                rps = pC if rp == 0 else pA
                mm = None
                for half in range(2):
                    mm = S.sig("tensor",
                               lambda e, c=c, half=half, rps=rps: e.matmul(
                                   rps[0:1, half * 512:(half + 1) * 512],
                                   c_e127[:, :],
                                   _apn(arena, ST0 + c * 1024 + half * 512,
                                        (1, 512)),
                                   start=True, stop=True), "T", 1,
                               waits=[tok_st0[NH - 1], tok_st0[NH - 2],
                                      prev_r[rp], tok_cst] if half == 0 else [])
                tok_rev = S.sig("vector", lambda e, rp=rp, rps=rps: e.tensor_copy(
                    r_buf[0:1, rp * 1024:(rp + 1) * 1024], rps[0:1, 0:1024]),
                    "V", 1, waits=[mm, prev_rb[rp]])
                prev_r[rp] = tok_rev
                tok_rdma[c] = S.sig("sync", lambda e, c=c, rp=rp: e.dma_start(
                    out=dscr.ap()[c * 1024:(c + 1) * 1024],
                    in_=r_buf[0:1, rp * 1024:(rp + 1) * 1024]),
                    f"drf{rp}", 16, waits=[tok_rev])
                prev_rb[rp] = tok_rdma[c]

            t_rd2 = S.sig("sync", lambda e: e.dma_start(
                out=rx_mat[1:17, :],
                in_=dscr.ap().rearrange("(a b) -> a b", b=D)), "drd", 16,
                waits=[tok_rdma[NCH - 1], tok_rdma[NCH - 2], tok_rx0])

            mm = None
            for h in range(NH):
                mm = S.sig("tensor", lambda e, h=h: e.matmul(
                    pC[0:1, h * 64:(h + 1) * 64], EndW[:, h:h + 1],
                    rx_mat[0:17, h * 64:(h + 1) * 64], start=True, stop=True),
                    "T", 1,
                    waits=[t_rd2, tok_rx0, tok_End, prev_r[0], prev_r[1]]
                    if h == 0 else [])
            tok_rend = S.sig("vector", lambda e: e.tensor_copy(
                rend_sb, pC[0:1, 0:1024]), "V", 1, waits=[mm])
            TT["tok"].update(st0=tok_st0, rend=tok_rend)

        if max_stage >= 4:
            t_cc1 = S.sig("sync", lambda e: e.dma_start(
                out=cc_in.ap(), in_=rend_sb), "dcc", 16, waits=[tok_rend])
            t_cc = S.sig("gpsimd", lambda e: e.collective_compute(
                "AllGather", ALU.bypass, replica_groups=[list(range(8))],
                ins=[cc_in.ap().opt()], outs=[cc_out.ap().opt()]), "G", 1,
                waits=[t_cc1])
            t_cc2 = S.sig("gpsimd", lambda e: e.dma_start(
                out=rendall, in_=cc_out.ap()), "dcc", 16, waits=[t_cc])
            t_rab = S.sig("vector", lambda e: e.tensor_copy(
                rendall_b, rendall), "V", 1, waits=[t_cc2])

            pre_g = {}
            if max_stage >= 5:
                tok_wgl = S.sig("gpsimd", lambda e: e.dma_start(
                    out=w_region(0).rearrange("p (k m) -> p k m", m=1024),
                    in_=d["wg"].rearrange("(k p) m -> p k m", p=128)),
                    "wva", 16, waits=[tok_vev[NCH - 1]])
                for gc, ps, pfree in ((0, pB, prev_h_ev[1]), (1, pA, prev_r[1])):
                    gmm = None
                    for k in range(NF):
                        for nh in range(2):
                            first = (k == 0 and nh == 0)
                            gmm = S.sig("tensor",
                                        lambda e, gc=gc, k=k, nh=nh, ps=ps:
                                        e.matmul(
                                            ps[:, nh * 512:(nh + 1) * 512],
                                            u1T_tile(k, gc),
                                            w_slot(k)[:, nh * 512:(nh + 1) * 512],
                                            start=(k == 0), stop=(k == NF - 1)),
                                        "T", 1,
                                        waits=[tok_wgl, pfree, tok_u1T_ev[gc]]
                                        if first else [])
                    pre_g[gc] = S.sig("scalar", lambda e, gc=gc, ps=ps:
                                      e.activation(gsl(gc % 2), ps[:, :],
                                                   AF.Silu), "S", 1,
                                      waits=[gmm])

            mm = None
            for h in range(NH):
                mm = S.sig("tensor", lambda e, h=h: e.matmul(
                    pC[0:1, h * 64:(h + 1) * 64], CoreW[:, h:h + 1],
                    rendall_b[0:8, h * 64:(h + 1) * 64], start=True, stop=True),
                    "T", 1, waits=[t_rab, tok_Core, tok_rend] if h == 0 else [])
            tok_cc_ev = S.sig("vector", lambda e: e.tensor_copy(
                rx_mat[0:1, :], pC[0:1, 0:1024]), "V", 1, waits=[mm])

            mm = None
            for h in range(NH):
                mm = S.sig("tensor", lambda e, h=h: e.matmul(
                    pC[0:16, h * 64:(h + 1) * 64],
                    Lx_all[:, h * 16:(h + 1) * 16],
                    rx_mat[0:17, h * 64:(h + 1) * 64], start=True, stop=True),
                    "T", 1, waits=[tok_cc_ev, tok_Lx] if h == 0 else [])
            tok_carr = S.sig("vector", lambda e: e.tensor_copy(
                carr_sb, pC[0:16, :]), "V", 1, waits=[mm])
            t_cf1 = S.sig("sync", lambda e: e.dma_start(
                out=dscr2.ap().rearrange("(a b) -> a b", b=D), in_=carr_sb),
                "drd", 16, waits=[tok_carr])
            TT["tok"].update(carr=tok_carr, cf1=t_cf1)

        if max_stage >= 5:
            tok_wol = S.sig("gpsimd", lambda e: e.dma_start(
                out=w_region(8).rearrange("p (k m) -> p k m", m=1024),
                in_=d["wo"].rearrange("(k p) m -> p k m", p=128)),
                "wqa", 16, waits=[prev_pAB_ev[0], prev_pAB_ev[1]])

            prev_g_used = [None, None]
            prev_pA_user = pre_g[1]
            prev_pB_user = pre_g[0]
            prev_pC_user = tok_carr
            prev_tpD2 = prev_tpD
            prev_x0p_used = [prev_x0_used[0], prev_x0_used[1]]
            prev_xn_used = [None, None]
            prev_x1_used = [None, None]
            prev_st_used = [None, None]
            prev_ret_used = [None, None]
            prev_ao_used = [None, None]
            prev_u2_used = [None, None]
            prev_ygt_used = [tok_carr, tok_carr]
            prev_qt2_used = [prev_qt_used[0], prev_qt_used[1]]
            prev_cb_used = t_cf1
            tok_u2T_ev = [None] * NCH
            tok_x1st = [None] * NCH
            tok_gates = [None] * NCH
            tok_g15 = None
            tok_wo15 = None

            for c in range(NCH):
                p = c % 2
                t_qld = S.sig("sync", lambda e, c=c, p=p: e.dma_start(
                    out=qtsl(p), in_=qd.ap()[c * CH:(c + 1) * CH, :]),
                    f"dql{p}", 16,
                    waits=[prev_qt2_used[p], tok_qst[c]])
                t_cld = S.sig("sync", lambda e, c=c: e.dma_start(
                    out=carr_buf[:, :],
                    in_=dscr2.ap()[c * 1024:(c + 1) * 1024]), "dcl", 16,
                    waits=[t_cf1, prev_cb_used])

                if c in pre_g:
                    tok_gev = pre_g[c]
                else:
                    gmm = None
                    for k in range(NF):
                        for nh in range(2):
                            first = (k == 0 and nh == 0)
                            gmm = S.sig("tensor",
                                        lambda e, c=c, k=k, nh=nh: e.matmul(
                                            pB[:, nh * 512:(nh + 1) * 512],
                                            u1T_tile(k, c),
                                            w_slot(k)[:, nh * 512:(nh + 1) * 512],
                                            start=(k == 0), stop=(k == NF - 1)),
                                        "T", 1,
                                        waits=[prev_pB_user,
                                               tok_u1T_ev[c]] if first else [])
                    tok_gev = S.sig("scalar", lambda e, p=p: e.activation(
                        gsl(p), pB[:, :], AF.Silu), "S", 1,
                        waits=[gmm, prev_g_used[p]])
                    prev_pB_user = tok_gev
                if c == NCH - 1:
                    tok_g15 = tok_gev

                bmm = None
                for nh in range(2):
                    bmm = S.sig("tensor", lambda e, nh=nh: e.matmul(
                        pC[0:16, nh * 512:(nh + 1) * 512], c_ones116[:, :],
                        carr_buf[0:1, nh * 512:(nh + 1) * 512],
                        start=True, stop=True), "T", 1,
                        waits=[t_cld, prev_pC_user] if nh == 0 else [])
                tok_bd = S.sig("vector", lambda e: e.tensor_tensor(
                    bd_sb[:, :], pC[0:16, :], c_bdmask[:, :], ALU.mult), "V", 1,
                    waits=[bmm])
                prev_cb_used = tok_bd
                cmm = None
                for nh in range(2):
                    cmm = S.sig("tensor", lambda e, nh=nh: e.matmul(
                        pC[:, nh * 512:(nh + 1) * 512], P_all[:, :],
                        bd_sb[:, nh * 512:(nh + 1) * 512],
                        start=True, stop=True), "T", 1,
                        waits=[tok_bd, tok_P] if nh == 0 else [])

                grp = stat_g(0, p)
                sm, sq, rs, nm = grp
                t_st = S.sig("vector", lambda e, c=c, p=p: e.scalar_tensor_tensor(
                    stsl(p), pC[:, :], 1.0, st0_sl(c), ALU.mult, ALU.add),
                    "V", 1, waits=[cmm, prev_st_used[p]])
                prev_pC_user = t_st
                t_ret = S.sig("vector",
                              lambda e, p=p, sm=sm: e.scalar_tensor_tensor(
                                  retsl(p), stsl(p), 1.0, qtsl(p),
                                  ALU.mult, ALU.mult, accum_out=sm), "V", 1,
                              waits=[t_st, t_qld, prev_ret_used[p]])
                prev_st_used[p] = t_ret
                prev_qt2_used[p] = t_ret
                t2 = S.sig("vector", lambda e, sm=sm: e.tensor_scalar(
                    sm, sm, -1.0 / D, None, ALU.mult), "V", 1, waits=[t_ret])
                t3 = S.sig("scalar", lambda e, p=p, sm=sm, sq=sq: e.activation(
                    trash2, retsl(p), AF.Square, bias=sm, accum_out=sq),
                    "S", 1, waits=[t2])
                t4 = S.sig("scalar", lambda e, sq=sq, rs=rs: e.activation(
                    rs, sq, AF.Ln, scale=1.0 / D, bias=eps_col[:, :]), "S", 1,
                    waits=[t3, tok_eps])
                t5 = S.sig("scalar", lambda e, rs=rs: e.activation(
                    rs, rs, AF.Exp, scale=-0.5), "S", 1, waits=[t4])
                t6 = S.sig("vector", lambda e, sm=sm, rs=rs, nm=nm:
                           e.tensor_tensor(nm, sm, rs, ALU.mult), "V", 1,
                           waits=[t5])
                t_y0 = S.sig("scalar", lambda e, p=p, rs=rs, nm=nm: e.activation(
                    ysl(0), retsl(p), AF.Identity, scale=rs, bias=nm), "S", 1,
                    waits=[t6])
                prev_ret_used[p] = t_y0
                t_y1 = S.sig("vector", lambda e: e.tensor_tensor(
                    ysl(1), ysl(0), gamma_rep[:, :], ALU.mult), "V", 1,
                    waits=[t_y0, tok_grep])
                t_y2 = S.sig("vector", lambda e: e.tensor_tensor(
                    ysl(0), ysl(1), beta_rep[:, :], ALU.add), "V", 1,
                    waits=[t_y1, tok_brep])
                t_yg = S.sig("vector", lambda e, p=p: e.tensor_tensor(
                    ysl(1), ysl(0), gsl(p), ALU.mult), "V", 1,
                    waits=[t_y2, tok_gev])
                prev_g_used[p] = t_yg

                ttp = None
                for k in range(NF):
                    ttp = S.sig("tensor", lambda e, k=k: e.transpose(
                        pD[:, k * 128:(k + 1) * 128],
                        ysl(1)[:, k * 128:(k + 1) * 128], c_ident[:, :]),
                        "T", 1, waits=[t_yg, prev_tpD2] if k == 0 else [])
                t_ygt = S.sig("vector", lambda e, p=p: e.tensor_copy(
                    ygtsl(p), pD[:, 0:1024]), "V", 1,
                    waits=[ttp, prev_ygt_used[p]])
                prev_tpD2 = t_ygt

                wmm = None
                for k in range(NF):
                    for nh in range(2):
                        first = (k == 0 and nh == 0)
                        wmm = S.sig("tensor",
                                    lambda e, p=p, k=k, nh=nh: e.matmul(
                                        pA[:, nh * 512:(nh + 1) * 512],
                                        ygtsl(p)[:, k * 128:(k + 1) * 128],
                                        w_slot(8 + k)[:, nh * 512:(nh + 1) * 512],
                                        start=(k == 0), stop=(k == NF - 1)),
                                    "T", 1,
                                    waits=[t_ygt, tok_wol, prev_pA_user]
                                    if first else [])
                prev_ygt_used[p] = wmm
                if c == NCH - 1:
                    tok_wo15 = wmm
                grp2 = stat_g(8, p)
                sm2, sq2, rs2, nm2 = grp2
                t_ao = S.sig("scalar", lambda e, p=p, sm2=sm2: e.activation(
                    aosl(p), pA[:, :], AF.Identity, accum_out=sm2), "S", 1,
                    waits=[wmm, prev_ao_used[p]])
                prev_pA_user = t_ao
                t2b = S.sig("vector", lambda e, sm2=sm2: e.tensor_scalar(
                    sm2, sm2, -1.0 / D, None, ALU.mult), "V", 1, waits=[t_ao])
                t3b = S.sig("scalar", lambda e, p=p, sm2=sm2, sq2=sq2:
                            e.activation(trash2, aosl(p), AF.Square,
                                         bias=sm2, accum_out=sq2), "S", 1,
                            waits=[t2b])
                t4b = S.sig("scalar", lambda e, sq2=sq2, rs2=rs2: e.activation(
                    rs2, sq2, AF.Ln, scale=1.0 / D, bias=eps_col[:, :]), "S", 1,
                    waits=[t3b, tok_eps])
                t5b = S.sig("scalar", lambda e, rs2=rs2: e.activation(
                    rs2, rs2, AF.Exp, scale=-0.5), "S", 1, waits=[t4b])
                t6b = S.sig("vector", lambda e, sm2=sm2, rs2=rs2, nm2=nm2:
                            e.tensor_tensor(nm2, sm2, rs2, ALU.mult), "V", 1,
                            waits=[t5b])
                t_x0p = S.sig("sync", lambda e, c=c, p=p: e.dma_start(
                    out=x0sl(p), in_=d["x0"][c * CH:(c + 1) * CH, :]),
                    f"dxp{p}", 16, waits=[prev_x0p_used[p]])
                t_xn = S.sig("scalar",
                             lambda e, p=p, rs2=rs2, nm2=nm2: e.activation(
                                 xnsl(p), aosl(p), AF.Identity, scale=rs2,
                                 bias=nm2), "S", 1,
                             waits=[t6b, prev_xn_used[p]])
                prev_ao_used[p] = t_xn
                t_x1 = S.sig("vector", lambda e, p=p: e.tensor_tensor(
                    x1sl(p), xnsl(p), x0sl(p), ALU.add), "V", 1,
                    waits=[t_xn, t_x0p, prev_x1_used[p]])
                prev_xn_used[p] = t_x1
                prev_x0p_used[p] = t_x1
                tok_x1st[c] = S.sig("sync", lambda e, c=c, p=p: e.dma_start(
                    out=x1d.ap()[c * CH:(c + 1) * CH, :], in_=x1sl(p)),
                    "dx1st", 16, waits=[t_x1])

                grp3 = stat_g(16, p)
                rs3, nm3 = grp3[2], grp3[3]
                tok_ln3 = ln_stats(x1sl(p), grp3, t_x1, trash2)
                t_u2 = S.sig("scalar",
                             lambda e, p=p, rs3=rs3, nm3=nm3: e.activation(
                                 u2sl(p), x1sl(p), AF.Identity, scale=rs3,
                                 bias=nm3), "S", 1,
                             waits=[tok_ln3, prev_u2_used[p]])
                prev_x1_used[p] = t_u2

                ttp2 = None
                for k in range(NF):
                    ttp2 = S.sig("tensor", lambda e, p=p, k=k: e.transpose(
                        pD[:, 1024 + k * 128: 1024 + (k + 1) * 128],
                        u2sl(p)[:, k * 128:(k + 1) * 128], c_ident[:, :]),
                        "T", 1,
                        waits=[t_u2, prev_tpD2] if k == 0 else [])
                tok_u2T_ev[c] = S.sig("vector", lambda e, c=c: e.tensor_copy(
                    u2T_cell(c), _apn(pD, 1024, (128, 8), (1, 128))), "V", 1,
                    waits=[ttp2])
                prev_tpD2 = tok_u2T_ev[c]
                prev_u2_used[p] = tok_u2T_ev[c]

                gmm2 = None
                for k in range(NF):
                    gmm2 = S.sig("tensor", lambda e, c=c, k=k: e.matmul(
                        pB[:, 0:8],
                        _apn(arena, VAL + k * 2048 + c * 128, (1, 128)),
                        wgr_b[:, k * 8:(k + 1) * 8],
                        start=(k == 0), stop=False), "T", 1,
                        waits=[tok_u2T_ev[c], tok_wgr, prev_pB_user]
                        if k == 0 else [])
                gmm2 = S.sig("tensor", lambda e: e.matmul(
                    pB[:, 0:8], c_ones1128b[:, :], rbias_b[:, :],
                    start=False, stop=True), "T", 1, waits=[tok_rb])
                tok_gates[c] = S.sig("vector", lambda e, c=c: e.tensor_copy(
                    gates_all[:, c * 8:(c + 1) * 8], pB[:, 0:8]), "V", 1,
                    waits=[gmm2])
                prev_pB_user = tok_gates[c]
            TT["tok"].update(u2T=tok_u2T_ev, x1st=tok_x1st, gates=tok_gates)

        if max_stage >= 6:
            def bc16(t16):
                a = t16 if isinstance(t16, bass.AP) else t16[:, :]
                return bass.AP(a.tensor, a.offset,
                               [list(a.ap[0]), list(a.ap[1]), [0, E]])

            def g3(t):
                a = t if isinstance(t, bass.AP) else t[:, :]
                return a.rearrange("p (c e) -> p c e", e=E)
            _ai = c_iota8[:, :]
            iota_b = bass.AP(_ai.tensor, _ai.offset,
                             [list(_ai.ap[0]), [0, NCH], list(_ai.ap[1])])

            lastv = S.sig("vector", lambda e: e.tensor_copy(
                tkw, gates_all[:, :]), "V", 1,
                waits=[tok_gates[NCH - 1], prev_x0p_used[0], prev_x0p_used[1]])
            lastv = S.sig("vector", lambda e: e.memset(tkm, 0.0), "V", 1,
                          waits=[lastv])
            tok_m1 = None
            for it in range(TOPK):
                lastv = S.sig("vector", lambda e: e.tensor_reduce(
                    gmax[:, :], g3(tkw), AX.X, ALU.max), "V", 1, waits=[lastv])
                if it == 0:
                    tok_m1 = lastv = S.sig("vector", lambda e: e.tensor_copy(
                        gm1[:, :], gmax[:, :]), "V", 1, waits=[lastv])
                lastv = S.sig("vector", lambda e: e.tensor_tensor(
                    g3(tkt), g3(tkw), bc16(gmax), ALU.is_ge), "V", 1,
                    waits=[lastv])
                lastv = S.sig("vector", lambda e: e.tensor_scalar(
                    tkt2, tkt, -8.0, 8.0, ALU.mult, ALU.add), "V", 1,
                    waits=[lastv])
                lastv = S.sig("vector", lambda e: e.tensor_tensor(
                    g3(tkt), g3(tkt), iota_b, ALU.mult), "V", 1, waits=[lastv])
                lastv = S.sig("vector", lambda e: e.tensor_tensor(
                    tkt, tkt, tkt2, ALU.add), "V", 1, waits=[lastv])
                lastv = S.sig("vector", lambda e: e.tensor_reduce(
                    gidx[:, :], g3(tkt), AX.X, ALU.min), "V", 1, waits=[lastv])
                lastv = S.sig("vector", lambda e: e.tensor_tensor(
                    g3(tkt), iota_b, bc16(gidx), ALU.is_equal), "V", 1,
                    waits=[lastv])
                lastv = S.sig("vector", lambda e: e.tensor_tensor(
                    tkm, tkm, tkt, ALU.add), "V", 1, waits=[lastv])
                lastv = S.sig("vector", lambda e: e.tensor_scalar(
                    tkt, tkt, -3.0e38, None, ALU.mult), "V", 1, waits=[lastv])
                lastv = S.sig("vector", lambda e: e.tensor_tensor(
                    tkw, tkw, tkt, ALU.add), "V", 1, waits=[lastv])
            lastv = S.sig("vector", lambda e: e.tensor_tensor(
                g3(tkt), g3(gates_all[:, :]), bc16(gm1), ALU.subtract), "V", 1,
                waits=[lastv, tok_m1])
            t_exp = S.sig("scalar", lambda e: e.activation(
                tkt2, tkt, AF.Exp), "S", 1, waits=[lastv])
            lastv = S.sig("vector", lambda e: e.tensor_tensor(
                tkt, tkt2, tkm, ALU.mult), "V", 1, waits=[t_exp])
            lastv = S.sig("vector", lambda e: e.tensor_reduce(
                gz[:, :], g3(tkt), AX.X, ALU.add), "V", 1, waits=[lastv])
            lastv = S.sig("vector", lambda e: e.reciprocal(
                gmax[:, :], gz[:, :]), "V", 1, waits=[lastv])
            tok_w = S.sig("vector", lambda e: e.tensor_tensor(
                g3(w_all[:, :]), g3(tkt), bc16(gmax), ALU.mult), "V", 1,
                waits=[lastv])
            TT["tok"].update(w=tok_w)

        if max_stage >= 7:
            tok_r1, tok_r2, tok_w3 = {}, {}, {}
            free_r1 = [tok_g15, tok_g15]
            free_r2 = [tok_g15, tok_g15]
            free_w3 = [(tok_g15 if (4 + sl) < 8 else tok_wo15)
                       for sl in range(12)]
            w3slot = {}
            w3ctr = [0]

            ffn = {
                "pA": prev_pA_user, "pB": prev_pB_user, "pC": prev_pC_user,
                "s1": [prev_u2_used[0], prev_u2_used[1]],
                "stage": [prev_x1_used[0], prev_x1_used[1]],
                "last_z": None,
                "accd": [None] * NCH,
            }
            u2T_all = tok_u2T_ev[NCH - 1]

            def ffn_h_block(ei, th, m, slot):
                w1s, w2s = w_slot(slot), w_slot(2 + slot)
                mm1 = None
                for k in range(NF):
                    for nh in range(2):
                        first = (k == 0 and nh == 0)
                        mm1 = S.sig(
                            "tensor",
                            lambda e, k=k, nh=nh, w1s=w1s, th=th: e.matmul(
                                pA[:, nh * 512:(nh + 1) * 512],
                                w1s[:, k * 128:(k + 1) * 128],
                                u2T_rhs(k, th, nh),
                                start=(k == 0), stop=(k == NF - 1)),
                            "T", 1,
                            waits=[tok_r1[(ei, th, m)], u2T_all, ffn["pA"]]
                            if first else [])
                mm2 = None
                for k in range(NF):
                    for nh in range(2):
                        first = (k == 0 and nh == 0)
                        mm2 = S.sig(
                            "tensor",
                            lambda e, k=k, nh=nh, w2s=w2s, th=th: e.matmul(
                                pB[:, nh * 512:(nh + 1) * 512],
                                w2s[:, k * 128:(k + 1) * 128],
                                u2T_rhs(k, th, nh),
                                start=(k == 0), stop=(k == NF - 1)),
                            "T", 1,
                            waits=[tok_r2[(ei, th, m)], u2T_all, ffn["pB"]]
                            if first else [])
                sil = S.sig("scalar", lambda e: e.activation(
                    s1sl(0), pA[:, :], AF.Silu), "S", 1,
                    waits=[mm1, ffn["s1"][0]])
                ffn["pA"] = sil
                t_z = S.sig("vector", lambda e, m=m: e.tensor_tensor(
                    z_m(m), s1sl(0), pB[:, :], ALU.mult), "V", 1,
                    waits=[sil, mm2])
                ffn["s1"][0] = t_z
                ffn["pB"] = t_z
                ffn["last_z"] = t_z
                free_r1[slot] = mm1
                free_r2[slot] = mm2

            def ffn_w3_phase(ei, th):
                for half in range(2):
                    ms = W3A if half == 0 else W3B
                    for m in ms:
                        slot = w3ctr[0] % 12
                        w3ctr[0] += 1
                        w3slot[(ei, th, m)] = slot
                        src3 = (d["ws3"][m] if ei == 0 else d["w3"][ei - 1, m])
                        tok_w3[(ei, th, m)] = S.sig(
                            "gpsimd",
                            lambda e, s=src3, t=w_slot(4 + slot): e.dma_start(
                                out=t, in_=s),
                            f"w3s{slot}", 16, waits=[free_w3[slot]])
                    for c8 in range(8):
                        c = th * 8 + c8
                        par = c8 % 2
                        eps_ = pA if par == 0 else pB
                        pfree = ffn["pA"] if par == 0 else ffn["pB"]
                        mm = None
                        for mi, m in enumerate(ms):
                            slot = w3slot[(ei, th, m)]
                            for nh in range(2):
                                w_ = [tok_w3[(ei, th, m)]] if nh == 0 else []
                                if mi == 0 and nh == 0:
                                    w_ = w_ + [ffn["last_z"], pfree]
                                mm = S.sig(
                                    "tensor",
                                    lambda e, m=m, c8=c8, nh=nh, slot=slot,
                                    eps_=eps_, mi=mi, nms=len(ms): e.matmul(
                                        eps_[:, nh * 512:(nh + 1) * 512],
                                        z_tile(m, c8),
                                        w_slot(4 + slot)[:, nh * 512:(nh + 1) * 512],
                                        start=(mi == 0), stop=(mi == nms - 1)),
                                    "T", 1, waits=w_)
                            if c8 == 7:
                                free_w3[slot] = mm
                        sp = par
                        if ei == 0:
                            t_cp = S.sig(
                                "vector",
                                lambda e, sp=sp, eps_=eps_: e.tensor_copy(
                                    stage_sl(sp), eps_[:, :]), "V", 1,
                                waits=[mm, ffn["stage"][sp]])
                        else:
                            t_cp = S.sig(
                                "vector",
                                lambda e, sp=sp, eps_=eps_, c=c, ei=ei:
                                e.tensor_scalar(
                                    stage_sl(sp), eps_[:, :],
                                    w_all[:, c * 8 + (ei - 1): c * 8 + ei],
                                    None, ALU.mult),
                                "V", 1, waits=[mm, ffn["stage"][sp], tok_w])
                        if par == 0:
                            ffn["pA"] = t_cp
                        else:
                            ffn["pB"] = t_cp
                        accum = not (ei == 0 and half == 0)
                        t_dma = S.sig(
                            "gpsimd",
                            lambda e, sp=sp, c=c, accum=accum: e.dma_start(
                                out=accd.ap()[c * CH:(c + 1) * CH, :],
                                in_=stage_sl(sp),
                                **({"accum_op": ALU.add} if accum else {})),
                            f"dsg{sp}", 16, waits=[t_cp, ffn["accd"][c]])
                        ffn["stage"][sp] = t_dma
                        ffn["accd"][c] = t_dma

            for th in range(2):
                for ei in range(FFN_NEI):
                    for m in range(NHT):
                        slot = m % 2
                        src1 = (d["ws1"][m] if ei == 0 else d["w1"][ei - 1, m])
                        tok_r1[(ei, th, m)] = S.sig(
                            "gpsimd",
                            lambda e, s=src1, t=w_slot(slot): e.dma_start(
                                out=t.rearrange("p (k m) -> p k m", m=128),
                                in_=s.rearrange("k p m -> p k m")),
                            f"r1s{slot}", 16, waits=[free_r1[slot]])
                        src2 = (d["ws2"][m] if ei == 0 else d["w2"][ei - 1, m])
                        tok_r2[(ei, th, m)] = S.sig(
                            "gpsimd",
                            lambda e, s=src2, t=w_slot(2 + slot): e.dma_start(
                                out=t.rearrange("p (k m) -> p k m", m=128),
                                in_=s.rearrange("k p m -> p k m")),
                            f"r2s{slot}", 16, waits=[free_r2[slot]])
                        ffn_h_block(ei, th, m, slot)
                    ffn_w3_phase(ei, th)
            TT["tok"].update(last_z=ffn["last_z"])

        if max_stage >= 9:
            wait_acc = [("dsg0", S.counts.get("dsg0", 0)),
                        ("dsg1", S.counts.get("dsg1", 0))]
            wait_x1 = [("dx1st", S.counts.get("dx1st", 0))]
            prev_acc_used = [("V", S.counts["V"]), ("V", S.counts["V"])]
            prev_x1b_used = [ffn["stage"][0], ffn["stage"][1]]
            prev_xn4_used = [prev_xn_used[0], prev_xn_used[1]]
            for c in range(NCH):
                p = c % 2
                t_la = S.sig("sync", lambda e, c=c, p=p: e.dma_start(
                    out=x0sl(p), in_=accd.ap()[c * CH:(c + 1) * CH, :]),
                    f"dba{p}", 16, waits=wait_acc + [prev_acc_used[p]])
                t_lx = S.sig("sync", lambda e, c=c, p=p: e.dma_start(
                    out=x1sl(p), in_=x1d.ap()[c * CH:(c + 1) * CH, :]),
                    f"dbx{p}", 16, waits=wait_x1 + [prev_x1b_used[p]])
                grp = stat_g(0, p)
                tok_ln4 = ln_stats(x0sl(p), grp, t_la, trash2)
                rs4, nm4 = grp[2], grp[3]
                t_xn4 = S.sig("scalar",
                              lambda e, p=p, rs4=rs4, nm4=nm4: e.activation(
                                  xnsl(p), x0sl(p), AF.Identity, scale=rs4,
                                  bias=nm4), "S", 1,
                              waits=[tok_ln4, prev_xn4_used[p]])
                t_x2 = S.sig("vector", lambda e, p=p: e.tensor_tensor(
                    x0sl(p), xnsl(p), x1sl(p), ALU.add), "V", 1,
                    waits=[t_xn4, t_lx])
                prev_xn4_used[p] = t_x2
                prev_x1b_used[p] = t_x2
                t_so = S.sig("sync", lambda e, c=c, p=p: e.dma_start(
                    out=out_d[c * CH:(c + 1) * CH, :], in_=x0sl(p)),
                    "dout", 16, waits=[t_x2])
                prev_acc_used[p] = t_so
            S.emit("sync", lambda e: e.nop(),
                   waits=[("dout", S.counts.get("dout", 0))])

        if dbg is not None:
            dbg(nc, S, TT)

        handles = {}
        for name in S.counts:
            handles[name] = es.enter_context(nc.semaphore(name))
        with nc.Block() as block:
            S.realize(block, handles)

    return nc


def kernel(**inputs):
    in_maps = _host_prep(inputs)
    nc = build_program()
    res = run_bass_kernel_spmd(nc, in_maps, core_ids=list(range(NCORE)))
    out = np.empty((B, L, D), np.float32)
    for c in range(NCORE):
        b, s = c // 4, c % 4
        out[b, s * T:(s + 1) * T, :] = np.asarray(res.results[c]["out"])
    return out


# version 03
# speedup vs baseline: 1.0018x; 1.0013x over previous
import contextlib

import numpy as np
import ml_dtypes

import concourse.bass as bass
import concourse.mybir as mybir
from concourse.bass_utils import run_bass_kernel_spmd

F32 = mybir.dt.float32
BF16 = mybir.dt.bfloat16
AF = mybir.ActivationFunctionType
ALU = mybir.AluOpType
AX = mybir.AxisListType
bfloat16 = ml_dtypes.bfloat16

B, L, D = 2, 8192, 1024
NCORE = 8
T = 2048
NCH, CH = 16, 128
NF = 8
NH, DH = 16, 64
HDIM, HPAD, NHT = 2730, 2816, 22
E = 8
TOPK = 4
LN_EPS = 1e-5
TINY = float(np.finfo(np.float32).tiny)
LN2 = float(np.log(2.0))
BIG = 1.0e38
FFN_NEI = E + 1
W3A = list(range(0, 11))
W3B = list(range(11, NHT))

_ENGINES = ("sync", "gpsimd", "vector", "scalar", "tensor")


def _collapse(waits):
    best = {}
    for w in waits:
        if w is None:
            continue
        s, v = w
        if s not in best or v > best[s]:
            best[s] = v
    return tuple(best.items())


class _Sched:
    def __init__(self):
        self.ops = {e: [] for e in _ENGINES}
        self.counts = {}

    def emit(self, eng, fn, waits=(), sig=()):
        sig = tuple(sig)
        self.ops[eng].append((_collapse(waits), fn, sig))
        toks = []
        for s, a in sig:
            self.counts[s] = self.counts.get(s, 0) + a
            toks.append((s, self.counts[s]))
        if len(toks) == 1:
            return toks[0]
        return toks

    def sig(self, eng, fn, sem, amt=1, waits=()):
        return self.emit(eng, fn, waits=waits, sig=[(sem, amt)])

    def realize(self, block, handles):
        for ename in _ENGINES:
            ops = self.ops[ename]

            def body(eng, ops=ops):
                for waits, fn, sig in ops:
                    for s, v in waits:
                        eng.wait_ge(handles[s], v)
                    inst = fn(eng)
                    for s, a in sig:
                        inst.then_inc(handles[s], a)

            getattr(block, ename)(body)


def _apn(t, off, *dims):
    a = t[:, :]
    return bass.AP(a.tensor, off,
                   [list(a.ap[0])] + [[s, c] for (s, c) in dims])



def _lhsT_tiles(wT, k, m):
    assert wT.shape == (k, m) and k % 128 == 0 and m % 128 == 0
    return np.ascontiguousarray(
        wT.reshape(k // 128, 128, m // 128, 128).transpose(2, 0, 1, 3))


def _pad_h_cols(wT):
    out = np.zeros((wT.shape[0], HPAD), np.float32)
    out[:, :HDIM] = wT
    return out


def _pad_h_rows(wT):
    out = np.zeros((HPAD, wT.shape[1]), np.float32)
    out[:HDIM, :] = wT
    return out


def _host_prep(inputs):
    f32 = lambda a: np.ascontiguousarray(np.asarray(a, np.float32))
    x = f32(inputs["L0_attn_x"])
    sh = {}
    sh["wq"] = np.ascontiguousarray(f32(inputs["L0_attn_Wq"]).T)
    sh["wv"] = np.ascontiguousarray(f32(inputs["L0_attn_Wv"]).T)
    sh["wg"] = np.ascontiguousarray(f32(inputs["L0_attn_Wg"]).T)
    sh["wo"] = np.ascontiguousarray(f32(inputs["L0_attn_Wo"]).T)
    beta = f32(inputs["L0_attn_beta"]).reshape(16)
    sh["beta_row"] = np.ascontiguousarray(beta.reshape(1, 16))
    sh["beta_col"] = np.ascontiguousarray(beta.reshape(16, 1))
    sh["gamma"] = f32(inputs["L0_attn_ln_gamma"]).reshape(1, D)
    sh["lnbeta"] = f32(inputs["L0_attn_ln_beta"]).reshape(1, D)
    sh["wgr"] = np.ascontiguousarray(f32(inputs["L0_ffn_Wg"]).T)
    sh["rbias"] = f32(inputs["L0_ffn_routing_bias"]).reshape(1, E)
    sh["ws1"] = _lhsT_tiles(_pad_h_cols(f32(inputs["L0_ffn_Ws1"]).T), D, HPAD)
    sh["ws2"] = _lhsT_tiles(_pad_h_cols(f32(inputs["L0_ffn_Ws2"]).T), D, HPAD)
    sh["ws3"] = np.ascontiguousarray(
        _pad_h_rows(f32(inputs["L0_ffn_Ws3"]).T).reshape(NHT, 128, D))
    sh["w1"] = np.stack(
        [_lhsT_tiles(_pad_h_cols(f32(inputs["L0_ffn_W1"][e]).T), D, HPAD)
         for e in range(E)])
    sh["w2"] = np.stack(
        [_lhsT_tiles(_pad_h_cols(f32(inputs["L0_ffn_W2"][e]).T), D, HPAD)
         for e in range(E)])
    sh["w3"] = np.stack(
        [np.ascontiguousarray(
            _pad_h_rows(f32(inputs["L0_ffn_W3"][e]).T).reshape(NHT, 128, D))
         for e in range(E)])

    sh["ident"] = np.eye(128, dtype=bfloat16)
    e127 = np.zeros((128, 1), bfloat16); e127[127, 0] = 1
    sh["e127"] = e127
    sh["ones116"] = np.ones((1, 16), bfloat16)
    sh["ones1128b"] = np.ones((1, 128), bfloat16)
    sh["ones1128f"] = np.ones((1, 128), np.float32)
    d2t = np.full((128, 128), BIG, np.float32)
    for k in range(128):
        d2t[k, k:] = np.arange(0, 128 - k, dtype=np.float32)
    sh["d2t"] = d2t.astype(bfloat16)
    sh["tp16"] = np.broadcast_to(
        np.arange(1, 129, dtype=np.float32), (16, 128)).astype(bfloat16).copy()
    dlx = np.full((17, 16), BIG, np.float32)
    for s in range(17):
        for c in range(16):
            if c >= s:
                dlx[s, c] = 128.0 * (c - s)
    sh["dlx"] = dlx
    dend = np.full((17, 1), BIG, np.float32)
    for s in range(1, 17):
        dend[s, 0] = 128.0 * (16 - s)
    sh["dend"] = dend
    bdmask = np.zeros((16, D), bfloat16)
    for h in range(16):
        bdmask[h, h * 64:(h + 1) * 64] = 1
    sh["bdmask"] = bdmask
    sh["iota8"] = np.broadcast_to(
        np.arange(E, dtype=np.float32), (128, E)).copy()

    in_maps = []
    for c in range(NCORE):
        b, s = c // 4, c % 4
        dcore = np.full((8, 1), BIG, np.float32)
        for i in range(8):
            if i // 4 == b and (i % 4) < s:
                dcore[i, 0] = 2048.0 * (s - 1 - (i % 4))
        m = dict(sh)
        m["x0"] = np.ascontiguousarray(x[b, s * T:(s + 1) * T, :])
        m["dcore8"] = dcore
        in_maps.append(m)
    return in_maps



def build_program(max_stage=99, dbg=None):
    nc = bass.Bass("TRN2", target_bir_lowering=False)

    d = {}
    def din(name, shape, dtype=F32):
        d[name] = nc.dram_tensor(name, shape, dtype, kind="ExternalInput").ap()
    din("x0", [T, D])
    for n in ("wq", "wv", "wg", "wo"):
        din(n, [D, D])
    din("wgr", [D, E]); din("rbias", [1, E])
    din("ws1", [NHT, NF, 128, 128]); din("ws2", [NHT, NF, 128, 128])
    din("ws3", [NHT, 128, D])
    din("w1", [E, NHT, NF, 128, 128]); din("w2", [E, NHT, NF, 128, 128])
    din("w3", [E, NHT, 128, D])
    din("beta_row", [1, 16]); din("beta_col", [16, 1])
    din("gamma", [1, D]); din("lnbeta", [1, D])
    din("ident", [128, 128], BF16); din("e127", [128, 1], BF16)
    din("ones116", [1, 16], BF16); din("ones1128b", [1, 128], BF16)
    din("ones1128f", [1, 128])
    din("d2t", [128, 128], BF16); din("tp16", [16, 128], BF16)
    din("dlx", [17, 16]); din("dend", [17, 1]); din("dcore8", [8, 1])
    din("bdmask", [16, D], BF16); din("iota8", [128, E])

    out_d = nc.dram_tensor("out", [T, D], F32, kind="ExternalOutput").ap()
    x1d = nc.dram_tensor("x1d", [T, D], F32, kind="Internal")
    qd = nc.dram_tensor("qd", [T, D], BF16, kind="Internal")
    accd = nc.dram_tensor("accd", [T, D], F32, kind="Internal")
    dscr = nc.dram_tensor("dscr", [16 * D], BF16, kind="Internal")
    dscr2 = nc.dram_tensor("dscr2", [16 * D], BF16, kind="Internal")
    cc_in = nc.dram_tensor("cc_in", [1, D], F32, kind="Internal")
    cc_out = nc.dram_tensor("cc_out", [8, D], F32, kind="Internal")

    S = _Sched()
    TT = {"tok": {}}

    with contextlib.ExitStack() as es:
        def sb(name, shape, dtype):
            t = es.enter_context(nc.sbuf_tensor(name, shape, dtype))
            TT[name] = t
            return t

        def pstn(name, shape, dtype):
            t = es.enter_context(nc.psum_tensor(name, shape, dtype))
            TT[name] = t
            return t

        arena = sb("arena", [128, 49152], BF16)
        U1T, ST0, VAL = 0, 16384, 32768
        warena = sb("warena", [128, 16384], BF16)
        fpool = sb("fpool", [128, 4096], F32)
        bpool = sb("bpool", [128, 18 * 1024], BF16)
        BGS, BST, BRET, BY, BYGT, BAO, BU2, BXN, BQT = (
            0, 2048, 4096, 6144, 8192, 10240, 12288, 14336, 16384)

        gamma_rep = sb("gamma_rep", [128, D], BF16)
        beta_rep = sb("beta_rep", [128, D], BF16)
        gates_all = sb("gates_all", [128, NCH * E], F32)
        w_all = sb("w_all", [128, NCH * E], F32)
        gmax = sb("gmax", [128, NCH], F32)
        gidx = sb("gidx", [128, NCH], F32)
        gm1 = sb("gm1", [128, NCH], F32)
        gz = sb("gz", [128, NCH], F32)

        c_ident = sb("c_ident", [128, 128], BF16)
        c_e127 = sb("c_e127", [128, 1], BF16)
        c_ones116 = sb("c_ones116", [1, 16], BF16)
        c_ones1128b = sb("c_ones1128b", [1, 128], BF16)
        c_ones1128f = sb("c_ones1128f", [1, 128], F32)
        c_d2t = sb("c_d2t", [128, 128], BF16)
        c_tp16 = sb("c_tp16", [16, 128], BF16)
        c_dlx = sb("c_dlx", [17, 16], F32)
        c_dend = sb("c_dend", [17, 1], F32)
        c_dcore = sb("c_dcore", [8, 1], F32)
        c_bdmask = sb("c_bdmask", [16, D], BF16)
        c_iota8 = sb("c_iota8", [128, E], F32)
        beta_row = sb("beta_row_sb", [1, 16], F32)
        beta_col = sb("beta_col_sb", [16, 1], F32)
        lnl_row = sb("lnl_row", [1, 16], F32)
        lnl_col = sb("lnl_col", [16, 1], F32)
        lam_rep = sb("lam_rep", [128, 16], F32)
        A_all = sb("A_all", [128, NH * 128], BF16)
        P_all = sb("P_all", [16, 128], BF16)
        Lx_all = sb("Lx_all", [17, NH * 16], BF16)
        EndW = sb("EndW", [17, 16], BF16)
        CoreW = sb("CoreW", [8, 16], BF16)
        eps_col = sb("eps_col", [128, 1], F32)
        rbias_b = sb("rbias_b", [1, E], BF16)
        rbias_f = sb("rbias_f", [1, E], F32)
        wgr_b = sb("wgr_b", [128, NF * E], BF16)
        stat = sb("stat", [128, 24], F32)
        carr_buf = sb("carr_buf", [1, D], BF16)
        bd_sb = sb("bd_sb", [16, D], BF16)

        pA = pstn("pA", [128, 1024], F32)
        pB = pstn("pB", [128, 1024], F32)
        pC = pstn("pC", [128, 1024], F32)
        pD = pstn("pD", [128, 2048], BF16)

        x0sl = lambda p: fpool[:, p * 1024:(p + 1) * 1024]
        x1sl = lambda p: fpool[:, (2 + p) * 1024:(3 + p) * 1024]
        stage_sl = x1sl
        tkw = fpool[:, 0:128]; tkm = fpool[:, 128:256]
        tkt = fpool[:, 256:384]; tkt2 = fpool[:, 384:512]
        rend_sb = fpool[0:1, 2 * 1024:3 * 1024]
        rendall = fpool[0:8, 3 * 1024:4 * 1024]

        gsl = lambda p: bpool[:, BGS + p * 1024: BGS + (p + 1) * 1024]
        stsl = lambda p: bpool[:, BST + p * 1024: BST + (p + 1) * 1024]
        retsl = lambda p: bpool[:, BRET + p * 1024: BRET + (p + 1) * 1024]
        ysl = lambda i: bpool[:, BY + i * 1024: BY + (i + 1) * 1024]
        u1sl = ysl
        ygtsl = lambda p: bpool[:, BYGT + p * 1024: BYGT + (p + 1) * 1024]
        aosl = lambda p: bpool[:, BAO + p * 1024: BAO + (p + 1) * 1024]
        u2sl = lambda p: bpool[:, BU2 + p * 1024: BU2 + (p + 1) * 1024]
        s1sl = u2sl
        xnsl = lambda p: bpool[:, BXN + p * 1024: BXN + (p + 1) * 1024]
        qtsl = lambda p: bpool[:, BQT + p * 1024: BQT + (p + 1) * 1024]
        ln1_trash = bpool[:, BGS:BGS + 1024]
        trash2 = A_all[:, 0:1024]
        r_buf = bpool[0:1, BGS:BGS + 2048]
        rendall_b = bpool[0:8, BRET:BRET + 1024]
        rx_mat = bpool[0:17, BAO:BAO + 1024]
        carr_sb = bpool[0:16, BYGT:BYGT + 1024]
        gamma_in = bpool[0:1, BU2:BU2 + 1024]
        lnbeta_in = bpool[0:1, BU2 + 1024:BU2 + 2048]

        def stat_g(base, parity):
            o = base + 4 * parity
            return (stat[:, o:o + 1], stat[:, o + 1:o + 2],
                    stat[:, o + 2:o + 3], stat[:, o + 3:o + 4])

        def w_slot(i):
            return warena[:, i * 1024:(i + 1) * 1024]

        def w_region(i0):
            return warena[:, i0 * 1024:(i0 + 8) * 1024]

        def u1T_tile(k, c):
            return _apn(arena, U1T + k * 2048 + c * 128, (1, 128))

        def st0_sl(c):
            return _apn(arena, ST0 + c * 1024, (1, 1024))

        def z_tile(m, c8):
            return _apn(arena, m * 1024 + c8 * 128, (1, 128))

        def z_m(m):
            return _apn(arena, m * 1024, (1, 1024))

        def u2T_rhs(k, th, nh):
            return _apn(arena, VAL + k * 2048 + th * 1024 + nh * 512, (1, 512))

        def u2T_cell(c):
            return _apn(arena, VAL + c * 128, (2048, 8), (1, 128))

        creqs = [
            (d["d2t"], c_d2t), (d["tp16"], c_tp16), (d["dlx"], c_dlx),
            (d["dend"], c_dend), (d["dcore8"], c_dcore), (d["iota8"], c_iota8),
            (d["beta_row"], beta_row), (d["beta_col"], beta_col),
            (d["rbias"], rbias_f),
            (d["ident"], c_ident), (d["e127"], c_e127),
            (d["ones116"], c_ones116), (d["ones1128b"], c_ones1128b),
            (d["ones1128f"], c_ones1128f), (d["bdmask"], c_bdmask),
        ]
        for src, dst in creqs:
            S.sig("sync", lambda e, s=src, t=dst: e.dma_start(out=t[:, :], in_=s),
                  "dcst", 16)
        tok_cst = ("dcst", 16 * len(creqs))

        S.sig("gpsimd", lambda e: e.dma_start(
            out=gamma_in, in_=d["gamma"]), "dgb", 16)
        S.sig("gpsimd", lambda e: e.dma_start(
            out=lnbeta_in, in_=d["lnbeta"]), "dgb", 16)
        tok_gb = ("dgb", 32)
        tok_wgr = S.sig("gpsimd", lambda e: e.dma_start(
            out=wgr_b[:, :].rearrange("p (k m) -> p k m", m=E),
            in_=d["wgr"].rearrange("(k p) m -> p k m", p=128)),
            "dwgr", 16)
        tok_wv = S.sig("gpsimd", lambda e: e.dma_start(
            out=w_region(0).rearrange("p (k m) -> p k m", m=1024), in_=d["wv"].rearrange("(k p) m -> p k m", p=128)),
            "wva", 16)
        tok_wq = S.sig("gpsimd", lambda e: e.dma_start(
            out=w_region(8).rearrange("p (k m) -> p k m", m=1024), in_=d["wq"].rearrange("(k p) m -> p k m", p=128)),
            "wqa", 16)

        tok_eps = S.sig("vector", lambda e: e.memset(eps_col[:, :], LN_EPS),
                        "V", 1)

        def lam_pipe(src, dst):
            t1 = S.sig("scalar", lambda e: e.activation(
                dst[:, :], src[:, :], AF.Exp, scale=-LN2), "S", 1,
                waits=[tok_cst])
            t2 = S.sig("vector", lambda e: e.tensor_scalar(
                dst[:, :], dst[:, :], -1.0, 1.0, ALU.mult, ALU.add), "V", 1,
                waits=[t1])
            t3 = S.sig("vector", lambda e: e.tensor_scalar(
                dst[:, :], dst[:, :], TINY, 1.0 - 1e-9, ALU.max, ALU.min),
                "V", 1, waits=[t2])
            return S.sig("scalar", lambda e: e.activation(
                dst[:, :], dst[:, :], AF.Ln), "S", 1, waits=[t3])
        tok_lr = lam_pipe(beta_row, lnl_row)
        tok_lc = lam_pipe(beta_col, lnl_col)

        tok_t = S.sig("tensor", lambda e: e.matmul(
            pA[:, 0:16], c_ones1128f[:, :], lnl_row[:, :], start=True,
            stop=True), "T", 1, waits=[tok_cst, tok_lr])
        tok_lrep = S.sig("vector", lambda e: e.tensor_copy(
            lam_rep[:, :], pA[:, 0:16]), "V", 1, waits=[tok_t])

        def rep_row(src_row, dst, prev):
            tks = None
            for half in range(2):
                tks = S.sig("tensor", lambda e, h=half: e.matmul(
                    pB[:, h * 512:(h + 1) * 512], c_ones1128b[:, :],
                    src_row[0:1, h * 512:(h + 1) * 512], start=True, stop=True),
                    "T", 1, waits=[tok_cst, tok_gb, prev] if half == 0 else [])
            return S.sig("vector", lambda e: e.tensor_copy(dst[:, :], pB[:, :]),
                         "V", 1, waits=[tks])
        tok_grep = rep_row(gamma_in, gamma_rep, None)
        tok_brep = rep_row(lnbeta_in, beta_rep, tok_grep)

        tok_rb = S.sig("vector", lambda e: e.tensor_copy(
            rbias_b[:, :], rbias_f[:, :]), "V", 1, waits=[tok_cst])

        def act_build(dst_ap, src_ap, scale_ap, waits):
            return S.sig("scalar", lambda e: e.activation(
                dst_ap, src_ap, AF.Exp, scale=scale_ap), "S", 1, waits=waits)
        tok_A = tok_P = tok_Lx = tok_End = tok_Core = None
        for h in range(NH):
            tok_A = act_build(A_all[:, h * 128:(h + 1) * 128], c_d2t[:, :],
                              lam_rep[:, h:h + 1], [tok_lrep, tok_cst])
        tok_P = act_build(P_all[:, :], c_tp16[:, :], lnl_col[:, :],
                          [tok_lc, tok_cst])
        for h in range(NH):
            tok_Lx = act_build(Lx_all[:, h * 16:(h + 1) * 16], c_dlx[:, :],
                               lam_rep[0:17, h:h + 1], [tok_lrep, tok_cst])
        for h in range(NH):
            tok_End = act_build(EndW[:, h:h + 1], c_dend[:, :],
                                lam_rep[0:17, h:h + 1], [tok_lrep, tok_cst])
        for h in range(NH):
            tok_Core = act_build(CoreW[:, h:h + 1], c_dcore[:, :],
                                 lam_rep[0:8, h:h + 1], [tok_lrep, tok_cst])

        def ln_stats(in_ap, grp, wait_in, trash_ap):
            sm, sq, rs, nm = grp
            t1 = S.sig("vector", lambda e: e.tensor_reduce(
                sm, in_ap, AX.X, ALU.add), "V", 1, waits=[wait_in])
            t2 = S.sig("vector", lambda e: e.tensor_scalar(
                sm, sm, -1.0 / D, None, ALU.mult), "V", 1, waits=[t1])
            t3 = S.sig("scalar", lambda e: e.activation(
                trash_ap, in_ap, AF.Square, bias=sm, accum_out=sq), "S", 1,
                waits=[t2])
            t4 = S.sig("scalar", lambda e: e.activation(
                rs, sq, AF.Ln, scale=1.0 / D, bias=eps_col[:, :]), "S", 1,
                waits=[t3, tok_eps])
            t5 = S.sig("scalar", lambda e: e.activation(
                rs, rs, AF.Exp, scale=-0.5), "S", 1, waits=[t4])
            t6 = S.sig("vector", lambda e: e.tensor_tensor(
                nm, sm, rs, ALU.mult), "V", 1, waits=[t5])
            return t6

        if max_stage >= 2:
            tok_x0 = [None] * NCH
            tok_u1T_ev = [None] * NCH
            tok_vev = [None] * NCH
            tok_qst = [None] * NCH
            prev_tpD = None
            prev_u1_used = [None, None]
            prev_x0_used = [None, None]
            prev_pAB_ev = [tok_lrep, tok_brep]
            prev_qt_used = [None, None]

            for c in range(NCH):
                p = c % 2
                tok_x0[c] = S.sig("sync", lambda e, c=c, p=p: e.dma_start(
                    out=x0sl(p), in_=d["x0"][c * CH:(c + 1) * CH, :]),
                    f"dx0{p}", 16, waits=[prev_x0_used[p]])
                grp = stat_g(0, p)
                tok_ln1 = ln_stats(x0sl(p), grp, tok_x0[c], ln1_trash)
                rs, nm = grp[2], grp[3]
                t7 = S.sig("scalar", lambda e, p=p, rs=rs, nm=nm: e.activation(
                    u1sl(p), x0sl(p), AF.Identity, scale=rs, bias=nm), "S", 1,
                    waits=[tok_ln1, prev_u1_used[p]])
                prev_x0_used[p] = t7

                ttp = None
                for k in range(NF):
                    ttp = S.sig("tensor", lambda e, p=p, k=k: e.transpose(
                        pD[:, k * 128:(k + 1) * 128],
                        u1sl(p)[:, k * 128:(k + 1) * 128], c_ident[:, :]),
                        "T", 1, waits=[t7, prev_tpD, tok_cst] if k == 0 else [])
                ev = None
                for half in range(2):
                    ev = S.sig("vector", lambda e, c=c, half=half: e.tensor_copy(
                        _apn(arena, U1T + half * 4 * 2048 + c * 128,
                             (2048, 4), (1, 128)),
                        _apn(pD, half * 512, (128, 4), (1, 128))),
                        "V", 1, waits=[ttp] if half == 0 else [])
                prev_tpD = ev
                prev_u1_used[p] = ev
                tok_u1T_ev[c] = ev

                vps = pA if p == 0 else pB
                tmm = None
                for k in range(NF):
                    for nh in range(2):
                        first = (k == 0 and nh == 0)
                        tmm = S.sig("tensor",
                                    lambda e, c=c, k=k, nh=nh, vps=vps: e.matmul(
                                        vps[:, nh * 512:(nh + 1) * 512],
                                        u1T_tile(k, c),
                                        w_slot(k)[:, nh * 512:(nh + 1) * 512],
                                        start=(k == 0), stop=(k == NF - 1)),
                                    "T", 1,
                                    waits=[ev, tok_wv, prev_pAB_ev[p]]
                                    if first else [])
                tok_vev[c] = S.sig("scalar", lambda e, c=c, vps=vps: e.activation(
                    _apn(arena, VAL + c * 1024, (1, 1024)), vps[:, :],
                    AF.Identity), "S", 1, waits=[tmm])
                qmm = None
                for k in range(NF):
                    for nh in range(2):
                        first = (k == 0 and nh == 0)
                        qmm = S.sig("tensor",
                                    lambda e, c=c, k=k, nh=nh, vps=vps: e.matmul(
                                        vps[:, nh * 512:(nh + 1) * 512],
                                        u1T_tile(k, c),
                                        w_slot(8 + k)[:, nh * 512:(nh + 1) * 512],
                                        start=(k == 0), stop=(k == NF - 1)),
                                    "T", 1,
                                    waits=[tok_vev[c], tok_wq] if first else [])
                tok_qev = S.sig("scalar", lambda e, p=p, vps=vps: e.activation(
                    qtsl(p), vps[:, :], AF.Identity), "S", 1,
                    waits=[qmm, prev_qt_used[p]])
                tok_qst[c] = S.sig("sync", lambda e, c=c, p=p: e.dma_start(
                    out=qd.ap()[c * CH:(c + 1) * CH, :], in_=qtsl(p)),
                    f"dqs{p}", 16, waits=[tok_qev])
                prev_qt_used[p] = tok_qst[c]
                prev_pAB_ev[p] = tok_qev
            TT["tok"].update(vev=tok_vev, u1T=tok_u1T_ev, qst=tok_qst)

        if max_stage >= 3:
            tok_st0 = [None] * NH
            prev_h_ev = [prev_pAB_ev[0], prev_pAB_ev[1]]
            for h in range(NH):
                hp = h % 2
                hps = pA if hp == 0 else pB
                mm = None
                for half in range(2):
                    mm = S.sig("tensor",
                               lambda e, h=h, half=half, hps=hps: e.matmul(
                                   hps[:, half * 512:(half + 1) * 512],
                                   A_all[:, h * 128:(h + 1) * 128],
                                   _apn(arena, VAL + half * 8 * 1024 + h * 64,
                                        (1024, 8), (1, 64)),
                                   start=True, stop=True), "T", 1,
                               waits=[tok_A, tok_vev[NCH - 1], prev_h_ev[hp]]
                               if half == 0 else [])
                tok_st0[h] = S.sig("vector", lambda e, h=h, hps=hps: e.tensor_copy(
                    _apn(arena, ST0 + h * 64, (1024, 16), (1, 64)),
                    _apn(hps, 0, (64, 16), (1, 64))),
                    "V", 1, waits=[mm])
                prev_h_ev[hp] = tok_st0[h]

            tok_rx0 = S.sig("vector", lambda e: e.memset(rx_mat, 0.0),
                            "V", 1, waits=[prev_u1_used[0], prev_u1_used[1]])
            tok_rdma = [None] * NCH
            prev_r = [prev_h_ev[0], prev_h_ev[1]]
            prev_rb = [tok_vev[NCH - 1], tok_vev[NCH - 1]]
            for c in range(NCH):
                rp = c % 2
                rps = pC if rp == 0 else pA
                mm = None
                for half in range(2):
                    mm = S.sig("tensor",
                               lambda e, c=c, half=half, rps=rps: e.matmul(
                                   rps[0:1, half * 512:(half + 1) * 512],
                                   c_e127[:, :],
                                   _apn(arena, ST0 + c * 1024 + half * 512,
                                        (1, 512)),
                                   start=True, stop=True), "T", 1,
                               waits=[tok_st0[NH - 1], tok_st0[NH - 2],
                                      prev_r[rp], tok_cst] if half == 0 else [])
                tok_rev = S.sig("vector", lambda e, rp=rp, rps=rps: e.tensor_copy(
                    r_buf[0:1, rp * 1024:(rp + 1) * 1024], rps[0:1, 0:1024]),
                    "V", 1, waits=[mm, prev_rb[rp]])
                prev_r[rp] = tok_rev
                tok_rdma[c] = S.sig("sync", lambda e, c=c, rp=rp: e.dma_start(
                    out=dscr.ap()[c * 1024:(c + 1) * 1024],
                    in_=r_buf[0:1, rp * 1024:(rp + 1) * 1024]),
                    f"drf{rp}", 16, waits=[tok_rev])
                prev_rb[rp] = tok_rdma[c]

            t_rd2 = S.sig("sync", lambda e: e.dma_start(
                out=rx_mat[1:17, :],
                in_=dscr.ap().rearrange("(a b) -> a b", b=D)), "drd", 16,
                waits=[tok_rdma[NCH - 1], tok_rdma[NCH - 2], tok_rx0])

            mm = None
            for h in range(NH):
                mm = S.sig("tensor", lambda e, h=h: e.matmul(
                    pC[0:1, h * 64:(h + 1) * 64], EndW[:, h:h + 1],
                    rx_mat[0:17, h * 64:(h + 1) * 64], start=True, stop=True),
                    "T", 1,
                    waits=[t_rd2, tok_rx0, tok_End, prev_r[0], prev_r[1]]
                    if h == 0 else [])
            tok_rend = S.sig("vector", lambda e: e.tensor_copy(
                rend_sb, pC[0:1, 0:1024]), "V", 1, waits=[mm])
            TT["tok"].update(st0=tok_st0, rend=tok_rend)

        if max_stage >= 4:
            t_cc1 = S.sig("sync", lambda e: e.dma_start(
                out=cc_in.ap(), in_=rend_sb), "dcc", 16, waits=[tok_rend])
            t_cc = S.sig("gpsimd", lambda e: e.collective_compute(
                "AllGather", ALU.bypass, replica_groups=[list(range(8))],
                ins=[cc_in.ap().opt()], outs=[cc_out.ap().opt()]), "G", 1,
                waits=[t_cc1])
            t_cc2 = S.sig("gpsimd", lambda e: e.dma_start(
                out=rendall, in_=cc_out.ap()), "dcc", 16, waits=[t_cc])
            t_rab = S.sig("vector", lambda e: e.tensor_copy(
                rendall_b, rendall), "V", 1, waits=[t_cc2])

            pre_g = {}
            if max_stage >= 5:
                tok_wgl = S.sig("gpsimd", lambda e: e.dma_start(
                    out=w_region(0).rearrange("p (k m) -> p k m", m=1024),
                    in_=d["wg"].rearrange("(k p) m -> p k m", p=128)),
                    "wva", 16, waits=[tok_vev[NCH - 1]])
                for gc, ps, pfree in ((0, pB, prev_h_ev[1]), (1, pA, prev_r[1])):
                    gmm = None
                    for k in range(NF):
                        for nh in range(2):
                            first = (k == 0 and nh == 0)
                            gmm = S.sig("tensor",
                                        lambda e, gc=gc, k=k, nh=nh, ps=ps:
                                        e.matmul(
                                            ps[:, nh * 512:(nh + 1) * 512],
                                            u1T_tile(k, gc),
                                            w_slot(k)[:, nh * 512:(nh + 1) * 512],
                                            start=(k == 0), stop=(k == NF - 1)),
                                        "T", 1,
                                        waits=[tok_wgl, pfree, tok_u1T_ev[gc]]
                                        if first else [])
                    pre_g[gc] = S.sig("scalar", lambda e, gc=gc, ps=ps:
                                      e.activation(gsl(gc % 2), ps[:, :],
                                                   AF.Silu), "S", 1,
                                      waits=[gmm])

            mm = None
            for h in range(NH):
                mm = S.sig("tensor", lambda e, h=h: e.matmul(
                    pC[0:1, h * 64:(h + 1) * 64], CoreW[:, h:h + 1],
                    rendall_b[0:8, h * 64:(h + 1) * 64], start=True, stop=True),
                    "T", 1, waits=[t_rab, tok_Core, tok_rend] if h == 0 else [])
            tok_cc_ev = S.sig("vector", lambda e: e.tensor_copy(
                rx_mat[0:1, :], pC[0:1, 0:1024]), "V", 1, waits=[mm])

            mm = None
            for h in range(NH):
                mm = S.sig("tensor", lambda e, h=h: e.matmul(
                    pC[0:16, h * 64:(h + 1) * 64],
                    Lx_all[:, h * 16:(h + 1) * 16],
                    rx_mat[0:17, h * 64:(h + 1) * 64], start=True, stop=True),
                    "T", 1, waits=[tok_cc_ev, tok_Lx] if h == 0 else [])
            tok_carr = S.sig("vector", lambda e: e.tensor_copy(
                carr_sb, pC[0:16, :]), "V", 1, waits=[mm])
            t_cf1 = S.sig("sync", lambda e: e.dma_start(
                out=dscr2.ap().rearrange("(a b) -> a b", b=D), in_=carr_sb),
                "drd", 16, waits=[tok_carr])
            TT["tok"].update(carr=tok_carr, cf1=t_cf1)

        if max_stage >= 5:
            tok_wol = S.sig("gpsimd", lambda e: e.dma_start(
                out=w_region(8).rearrange("p (k m) -> p k m", m=1024),
                in_=d["wo"].rearrange("(k p) m -> p k m", p=128)),
                "wqa", 16, waits=[prev_pAB_ev[0], prev_pAB_ev[1]])

            prev_g_used = [None, None]
            prev_pA_user = pre_g[1]
            prev_pB_user = pre_g[0]
            prev_pC_user = tok_carr
            prev_tpD2 = prev_tpD
            prev_x0p_used = [prev_x0_used[0], prev_x0_used[1]]
            prev_xn_used = [None, None]
            prev_x1_used = [None, None]
            prev_st_used = [None, None]
            prev_ret_used = [None, None]
            prev_ao_used = [None, None]
            prev_u2_used = [None, None]
            prev_ygt_used = [tok_carr, tok_carr]
            prev_qt2_used = [prev_qt_used[0], prev_qt_used[1]]
            prev_cb_used = t_cf1
            tok_u2T_ev = [None] * NCH
            tok_x1st = [None] * NCH
            tok_gates = [None] * NCH
            tok_g15 = None
            tok_wo15 = None

            for c in range(NCH):
                p = c % 2
                t_qld = S.sig("sync", lambda e, c=c, p=p: e.dma_start(
                    out=qtsl(p), in_=qd.ap()[c * CH:(c + 1) * CH, :]),
                    f"dql{p}", 16,
                    waits=[prev_qt2_used[p], tok_qst[c]])
                t_cld = S.sig("sync", lambda e, c=c: e.dma_start(
                    out=carr_buf[:, :],
                    in_=dscr2.ap()[c * 1024:(c + 1) * 1024]), "dcl", 16,
                    waits=[t_cf1, prev_cb_used])

                if c in pre_g:
                    tok_gev = pre_g[c]
                else:
                    gmm = None
                    for k in range(NF):
                        for nh in range(2):
                            first = (k == 0 and nh == 0)
                            gmm = S.sig("tensor",
                                        lambda e, c=c, k=k, nh=nh: e.matmul(
                                            pB[:, nh * 512:(nh + 1) * 512],
                                            u1T_tile(k, c),
                                            w_slot(k)[:, nh * 512:(nh + 1) * 512],
                                            start=(k == 0), stop=(k == NF - 1)),
                                        "T", 1,
                                        waits=[prev_pB_user,
                                               tok_u1T_ev[c]] if first else [])
                    tok_gev = S.sig("scalar", lambda e, p=p: e.activation(
                        gsl(p), pB[:, :], AF.Silu), "S", 1,
                        waits=[gmm, prev_g_used[p]])
                    prev_pB_user = tok_gev
                if c == NCH - 1:
                    tok_g15 = tok_gev

                bmm = None
                for nh in range(2):
                    bmm = S.sig("tensor", lambda e, nh=nh: e.matmul(
                        pC[0:16, nh * 512:(nh + 1) * 512], c_ones116[:, :],
                        carr_buf[0:1, nh * 512:(nh + 1) * 512],
                        start=True, stop=True), "T", 1,
                        waits=[t_cld, prev_pC_user] if nh == 0 else [])
                tok_bd = S.sig("vector", lambda e: e.tensor_tensor(
                    bd_sb[:, :], pC[0:16, :], c_bdmask[:, :], ALU.mult), "V", 1,
                    waits=[bmm])
                prev_cb_used = tok_bd
                cmm = None
                for nh in range(2):
                    cmm = S.sig("tensor", lambda e, nh=nh: e.matmul(
                        pC[:, nh * 512:(nh + 1) * 512], P_all[:, :],
                        bd_sb[:, nh * 512:(nh + 1) * 512],
                        start=True, stop=True), "T", 1,
                        waits=[tok_bd, tok_P] if nh == 0 else [])

                grp = stat_g(0, p)
                sm, sq, rs, nm = grp
                t_st = S.sig("vector", lambda e, c=c, p=p: e.scalar_tensor_tensor(
                    stsl(p), pC[:, :], 1.0, st0_sl(c), ALU.mult, ALU.add),
                    "V", 1, waits=[cmm, prev_st_used[p]])
                prev_pC_user = t_st
                t_ret = S.sig("vector",
                              lambda e, p=p, sm=sm: e.scalar_tensor_tensor(
                                  retsl(p), stsl(p), 1.0, qtsl(p),
                                  ALU.mult, ALU.mult, accum_out=sm), "V", 1,
                              waits=[t_st, t_qld, prev_ret_used[p]])
                prev_st_used[p] = t_ret
                prev_qt2_used[p] = t_ret
                t2 = S.sig("vector", lambda e, sm=sm: e.tensor_scalar(
                    sm, sm, -1.0 / D, None, ALU.mult), "V", 1, waits=[t_ret])
                t3 = S.sig("scalar", lambda e, p=p, sm=sm, sq=sq: e.activation(
                    trash2, retsl(p), AF.Square, bias=sm, accum_out=sq),
                    "S", 1, waits=[t2])
                t4 = S.sig("scalar", lambda e, sq=sq, rs=rs: e.activation(
                    rs, sq, AF.Ln, scale=1.0 / D, bias=eps_col[:, :]), "S", 1,
                    waits=[t3, tok_eps])
                t5 = S.sig("scalar", lambda e, rs=rs: e.activation(
                    rs, rs, AF.Exp, scale=-0.5), "S", 1, waits=[t4])
                t6 = S.sig("vector", lambda e, sm=sm, rs=rs, nm=nm:
                           e.tensor_tensor(nm, sm, rs, ALU.mult), "V", 1,
                           waits=[t5])
                t_y0 = S.sig("scalar", lambda e, p=p, rs=rs, nm=nm: e.activation(
                    ysl(0), retsl(p), AF.Identity, scale=rs, bias=nm), "S", 1,
                    waits=[t6])
                prev_ret_used[p] = t_y0
                t_y1 = S.sig("vector", lambda e: e.tensor_tensor(
                    ysl(1), ysl(0), gamma_rep[:, :], ALU.mult), "V", 1,
                    waits=[t_y0, tok_grep])
                t_y2 = S.sig("vector", lambda e: e.tensor_tensor(
                    ysl(0), ysl(1), beta_rep[:, :], ALU.add), "V", 1,
                    waits=[t_y1, tok_brep])
                t_yg = S.sig("vector", lambda e, p=p: e.tensor_tensor(
                    ysl(1), ysl(0), gsl(p), ALU.mult), "V", 1,
                    waits=[t_y2, tok_gev])
                prev_g_used[p] = t_yg

                ttp = None
                for k in range(NF):
                    ttp = S.sig("tensor", lambda e, k=k: e.transpose(
                        pD[:, k * 128:(k + 1) * 128],
                        ysl(1)[:, k * 128:(k + 1) * 128], c_ident[:, :]),
                        "T", 1, waits=[t_yg, prev_tpD2] if k == 0 else [])
                t_ygt = S.sig("vector", lambda e, p=p: e.tensor_copy(
                    ygtsl(p), pD[:, 0:1024]), "V", 1,
                    waits=[ttp, prev_ygt_used[p]])
                prev_tpD2 = t_ygt

                wmm = None
                for k in range(NF):
                    for nh in range(2):
                        first = (k == 0 and nh == 0)
                        wmm = S.sig("tensor",
                                    lambda e, p=p, k=k, nh=nh: e.matmul(
                                        pA[:, nh * 512:(nh + 1) * 512],
                                        ygtsl(p)[:, k * 128:(k + 1) * 128],
                                        w_slot(8 + k)[:, nh * 512:(nh + 1) * 512],
                                        start=(k == 0), stop=(k == NF - 1)),
                                    "T", 1,
                                    waits=[t_ygt, tok_wol, prev_pA_user]
                                    if first else [])
                prev_ygt_used[p] = wmm
                if c == NCH - 1:
                    tok_wo15 = wmm
                grp2 = stat_g(8, p)
                sm2, sq2, rs2, nm2 = grp2
                t_ao = S.sig("scalar", lambda e, p=p, sm2=sm2: e.activation(
                    aosl(p), pA[:, :], AF.Identity, accum_out=sm2), "S", 1,
                    waits=[wmm, prev_ao_used[p]])
                prev_pA_user = t_ao
                t2b = S.sig("vector", lambda e, sm2=sm2: e.tensor_scalar(
                    sm2, sm2, -1.0 / D, None, ALU.mult), "V", 1, waits=[t_ao])
                t3b = S.sig("scalar", lambda e, p=p, sm2=sm2, sq2=sq2:
                            e.activation(trash2, aosl(p), AF.Square,
                                         bias=sm2, accum_out=sq2), "S", 1,
                            waits=[t2b])
                t4b = S.sig("scalar", lambda e, sq2=sq2, rs2=rs2: e.activation(
                    rs2, sq2, AF.Ln, scale=1.0 / D, bias=eps_col[:, :]), "S", 1,
                    waits=[t3b, tok_eps])
                t5b = S.sig("scalar", lambda e, rs2=rs2: e.activation(
                    rs2, rs2, AF.Exp, scale=-0.5), "S", 1, waits=[t4b])
                t6b = S.sig("vector", lambda e, sm2=sm2, rs2=rs2, nm2=nm2:
                            e.tensor_tensor(nm2, sm2, rs2, ALU.mult), "V", 1,
                            waits=[t5b])
                t_x0p = S.sig("sync", lambda e, c=c, p=p: e.dma_start(
                    out=x0sl(p), in_=d["x0"][c * CH:(c + 1) * CH, :]),
                    f"dxp{p}", 16, waits=[prev_x0p_used[p]])
                t_xn = S.sig("scalar",
                             lambda e, p=p, rs2=rs2, nm2=nm2: e.activation(
                                 xnsl(p), aosl(p), AF.Identity, scale=rs2,
                                 bias=nm2), "S", 1,
                             waits=[t6b, prev_xn_used[p]])
                prev_ao_used[p] = t_xn
                t_x1 = S.sig("vector", lambda e, p=p: e.tensor_tensor(
                    x1sl(p), xnsl(p), x0sl(p), ALU.add), "V", 1,
                    waits=[t_xn, t_x0p, prev_x1_used[p]])
                prev_xn_used[p] = t_x1
                prev_x0p_used[p] = t_x1
                tok_x1st[c] = S.sig("sync", lambda e, c=c, p=p: e.dma_start(
                    out=x1d.ap()[c * CH:(c + 1) * CH, :], in_=x1sl(p)),
                    "dx1st", 16, waits=[t_x1])

                grp3 = stat_g(16, p)
                rs3, nm3 = grp3[2], grp3[3]
                tok_ln3 = ln_stats(x1sl(p), grp3, t_x1, trash2)
                t_u2 = S.sig("scalar",
                             lambda e, p=p, rs3=rs3, nm3=nm3: e.activation(
                                 u2sl(p), x1sl(p), AF.Identity, scale=rs3,
                                 bias=nm3), "S", 1,
                             waits=[tok_ln3, prev_u2_used[p]])
                prev_x1_used[p] = t_u2

                ttp2 = None
                for k in range(NF):
                    ttp2 = S.sig("tensor", lambda e, p=p, k=k: e.transpose(
                        pD[:, 1024 + k * 128: 1024 + (k + 1) * 128],
                        u2sl(p)[:, k * 128:(k + 1) * 128], c_ident[:, :]),
                        "T", 1,
                        waits=[t_u2, prev_tpD2] if k == 0 else [])
                tok_u2T_ev[c] = S.sig("vector", lambda e, c=c: e.tensor_copy(
                    u2T_cell(c), _apn(pD, 1024, (128, 8), (1, 128))), "V", 1,
                    waits=[ttp2])
                prev_tpD2 = tok_u2T_ev[c]
                prev_u2_used[p] = tok_u2T_ev[c]

                gmm2 = None
                for k in range(NF):
                    gmm2 = S.sig("tensor", lambda e, c=c, k=k: e.matmul(
                        pB[:, 0:8],
                        _apn(arena, VAL + k * 2048 + c * 128, (1, 128)),
                        wgr_b[:, k * 8:(k + 1) * 8],
                        start=(k == 0), stop=False), "T", 1,
                        waits=[tok_u2T_ev[c], tok_wgr, prev_pB_user]
                        if k == 0 else [])
                gmm2 = S.sig("tensor", lambda e: e.matmul(
                    pB[:, 0:8], c_ones1128b[:, :], rbias_b[:, :],
                    start=False, stop=True), "T", 1, waits=[tok_rb])
                tok_gates[c] = S.sig("vector", lambda e, c=c: e.tensor_copy(
                    gates_all[:, c * 8:(c + 1) * 8], pB[:, 0:8]), "V", 1,
                    waits=[gmm2])
                prev_pB_user = tok_gates[c]
            TT["tok"].update(u2T=tok_u2T_ev, x1st=tok_x1st, gates=tok_gates)

        if max_stage >= 6:
            def bc16(t16):
                a = t16 if isinstance(t16, bass.AP) else t16[:, :]
                return bass.AP(a.tensor, a.offset,
                               [list(a.ap[0]), list(a.ap[1]), [0, E]])

            def g3(t):
                a = t if isinstance(t, bass.AP) else t[:, :]
                return a.rearrange("p (c e) -> p c e", e=E)
            _ai = c_iota8[:, :]
            iota_b = bass.AP(_ai.tensor, _ai.offset,
                             [list(_ai.ap[0]), [0, NCH], list(_ai.ap[1])])

            lastv = S.sig("vector", lambda e: e.tensor_copy(
                tkw, gates_all[:, :]), "V", 1,
                waits=[tok_gates[NCH - 1], prev_x0p_used[0], prev_x0p_used[1]])
            lastv = S.sig("vector", lambda e: e.memset(tkm, 0.0), "V", 1,
                          waits=[lastv])
            tok_m1 = None
            for it in range(TOPK):
                lastv = S.sig("vector", lambda e: e.tensor_reduce(
                    gmax[:, :], g3(tkw), AX.X, ALU.max), "V", 1, waits=[lastv])
                if it == 0:
                    tok_m1 = lastv = S.sig("vector", lambda e: e.tensor_copy(
                        gm1[:, :], gmax[:, :]), "V", 1, waits=[lastv])
                lastv = S.sig("vector", lambda e: e.tensor_tensor(
                    g3(tkt), g3(tkw), bc16(gmax), ALU.is_ge), "V", 1,
                    waits=[lastv])
                lastv = S.sig("vector", lambda e: e.tensor_scalar(
                    tkt2, tkt, -8.0, 8.0, ALU.mult, ALU.add), "V", 1,
                    waits=[lastv])
                lastv = S.sig("vector", lambda e: e.tensor_tensor(
                    g3(tkt), g3(tkt), iota_b, ALU.mult), "V", 1, waits=[lastv])
                lastv = S.sig("vector", lambda e: e.tensor_tensor(
                    tkt, tkt, tkt2, ALU.add), "V", 1, waits=[lastv])
                lastv = S.sig("vector", lambda e: e.tensor_reduce(
                    gidx[:, :], g3(tkt), AX.X, ALU.min), "V", 1, waits=[lastv])
                lastv = S.sig("vector", lambda e: e.tensor_tensor(
                    g3(tkt), iota_b, bc16(gidx), ALU.is_equal), "V", 1,
                    waits=[lastv])
                lastv = S.sig("vector", lambda e: e.tensor_tensor(
                    tkm, tkm, tkt, ALU.add), "V", 1, waits=[lastv])
                lastv = S.sig("vector", lambda e: e.tensor_scalar(
                    tkt, tkt, -3.0e38, None, ALU.mult), "V", 1, waits=[lastv])
                lastv = S.sig("vector", lambda e: e.tensor_tensor(
                    tkw, tkw, tkt, ALU.add), "V", 1, waits=[lastv])
            lastv = S.sig("vector", lambda e: e.tensor_tensor(
                g3(tkt), g3(gates_all[:, :]), bc16(gm1), ALU.subtract), "V", 1,
                waits=[lastv, tok_m1])
            t_exp = S.sig("scalar", lambda e: e.activation(
                tkt2, tkt, AF.Exp), "S", 1, waits=[lastv])
            lastv = S.sig("vector", lambda e: e.tensor_tensor(
                tkt, tkt2, tkm, ALU.mult), "V", 1, waits=[t_exp])
            lastv = S.sig("vector", lambda e: e.tensor_reduce(
                gz[:, :], g3(tkt), AX.X, ALU.add), "V", 1, waits=[lastv])
            lastv = S.sig("vector", lambda e: e.reciprocal(
                gmax[:, :], gz[:, :]), "V", 1, waits=[lastv])
            tok_w = S.sig("vector", lambda e: e.tensor_tensor(
                g3(w_all[:, :]), g3(tkt), bc16(gmax), ALU.mult), "V", 1,
                waits=[lastv])
            TT["tok"].update(w=tok_w)

        if max_stage >= 7:
            tok_r1, tok_r2, tok_w3 = {}, {}, {}
            free_r1 = [tok_g15, tok_g15]
            free_r2 = [tok_g15, tok_g15]
            free_w3 = [(tok_g15 if (4 + sl) < 8 else tok_wo15)
                       for sl in range(12)]
            w3slot = {}
            w3ctr = [0]

            ffn = {
                "pA": prev_pA_user, "pB": prev_pB_user, "pC": prev_pC_user,
                "s1": [prev_u2_used[0], prev_u2_used[1]],
                "stage": [prev_x1_used[0], prev_x1_used[1]],
                "last_z": None,
                "accd": [None] * NCH,
            }
            u2T_all = tok_u2T_ev[NCH - 1]

            def ffn_h_block(ei, th, m, slot):
                w1s, w2s = w_slot(slot), w_slot(2 + slot)
                mm1 = None
                for k in range(NF):
                    for nh in range(2):
                        first = (k == 0 and nh == 0)
                        mm1 = S.sig(
                            "tensor",
                            lambda e, k=k, nh=nh, w1s=w1s, th=th: e.matmul(
                                pA[:, nh * 512:(nh + 1) * 512],
                                w1s[:, k * 128:(k + 1) * 128],
                                u2T_rhs(k, th, nh),
                                start=(k == 0), stop=(k == NF - 1)),
                            "T", 1,
                            waits=[tok_r1[(ei, th, m)], u2T_all, ffn["pA"]]
                            if first else [])
                mm2 = None
                for k in range(NF):
                    for nh in range(2):
                        first = (k == 0 and nh == 0)
                        mm2 = S.sig(
                            "tensor",
                            lambda e, k=k, nh=nh, w2s=w2s, th=th: e.matmul(
                                pB[:, nh * 512:(nh + 1) * 512],
                                w2s[:, k * 128:(k + 1) * 128],
                                u2T_rhs(k, th, nh),
                                start=(k == 0), stop=(k == NF - 1)),
                            "T", 1,
                            waits=[tok_r2[(ei, th, m)], u2T_all, ffn["pB"]]
                            if first else [])
                sil = S.sig("scalar", lambda e: e.activation(
                    s1sl(0), pA[:, :], AF.Silu), "S", 1,
                    waits=[mm1, ffn["s1"][0]])
                ffn["pA"] = sil
                t_z = S.sig("vector", lambda e, m=m: e.tensor_tensor(
                    z_m(m), s1sl(0), pB[:, :], ALU.mult), "V", 1,
                    waits=[sil, mm2])
                ffn["s1"][0] = t_z
                ffn["pB"] = t_z
                ffn["last_z"] = t_z
                free_r1[slot] = mm1
                free_r2[slot] = mm2

            def ffn_w3_phase(ei, th):
                for half in range(2):
                    ms = W3A if half == 0 else W3B
                    for m in ms:
                        slot = w3ctr[0] % 12
                        w3ctr[0] += 1
                        w3slot[(ei, th, m)] = slot
                        src3 = (d["ws3"][m] if ei == 0 else d["w3"][ei - 1, m])
                        tok_w3[(ei, th, m)] = S.sig(
                            "gpsimd",
                            lambda e, s=src3, t=w_slot(4 + slot): e.dma_start(
                                out=t, in_=s),
                            f"w3s{slot}", 16, waits=[free_w3[slot]])
                    for c8 in range(8):
                        c = th * 8 + c8
                        par = c8 % 2
                        eps_ = pA if par == 0 else pB
                        pfree = ffn["pA"] if par == 0 else ffn["pB"]
                        mm = None
                        for mi, m in enumerate(ms):
                            slot = w3slot[(ei, th, m)]
                            for nh in range(2):
                                w_ = [tok_w3[(ei, th, m)]] if nh == 0 else []
                                if mi == 0 and nh == 0:
                                    w_ = w_ + [ffn["last_z"], pfree]
                                mm = S.sig(
                                    "tensor",
                                    lambda e, m=m, c8=c8, nh=nh, slot=slot,
                                    eps_=eps_, mi=mi, nms=len(ms): e.matmul(
                                        eps_[:, nh * 512:(nh + 1) * 512],
                                        z_tile(m, c8),
                                        w_slot(4 + slot)[:, nh * 512:(nh + 1) * 512],
                                        start=(mi == 0), stop=(mi == nms - 1)),
                                    "T", 1, waits=w_)
                            if c8 == 7:
                                free_w3[slot] = mm
                        sp = par
                        if ei == 0:
                            t_cp = S.sig(
                                "vector",
                                lambda e, sp=sp, eps_=eps_: e.tensor_copy(
                                    stage_sl(sp), eps_[:, :]), "V", 1,
                                waits=[mm, ffn["stage"][sp]])
                        else:
                            t_cp = S.sig(
                                "vector",
                                lambda e, sp=sp, eps_=eps_, c=c, ei=ei:
                                e.tensor_scalar(
                                    stage_sl(sp), eps_[:, :],
                                    w_all[:, c * 8 + (ei - 1): c * 8 + ei],
                                    None, ALU.mult),
                                "V", 1, waits=[mm, ffn["stage"][sp], tok_w])
                        if par == 0:
                            ffn["pA"] = t_cp
                        else:
                            ffn["pB"] = t_cp
                        accum = not (ei == 0 and half == 0)
                        t_dma = S.sig(
                            "gpsimd",
                            lambda e, sp=sp, c=c, accum=accum: e.dma_start(
                                out=accd.ap()[c * CH:(c + 1) * CH, :],
                                in_=stage_sl(sp),
                                **({"accum_op": ALU.add} if accum else {})),
                            f"dsg{sp}", 16, waits=[t_cp, ffn["accd"][c]])
                        ffn["stage"][sp] = t_dma
                        ffn["accd"][c] = t_dma

            def load_r12(ei, th, m):
                slot = m % 2
                src1 = (d["ws1"][m] if ei == 0 else d["w1"][ei - 1, m])
                tok_r1[(ei, th, m)] = S.sig(
                    "gpsimd",
                    lambda e, s=src1, t=w_slot(slot): e.dma_start(
                        out=t.rearrange("p (k m) -> p k m", m=128),
                        in_=s.rearrange("k p m -> p k m")),
                    f"r1s{slot}", 16, waits=[free_r1[slot]])
                src2 = (d["ws2"][m] if ei == 0 else d["w2"][ei - 1, m])
                tok_r2[(ei, th, m)] = S.sig(
                    "gpsimd",
                    lambda e, s=src2, t=w_slot(2 + slot): e.dma_start(
                        out=t.rearrange("p (k m) -> p k m", m=128),
                        in_=s.rearrange("k p m -> p k m")),
                    f"r2s{slot}", 16, waits=[free_r2[slot]])
                return slot

            seq = [(th, ei) for th in range(2) for ei in range(FFN_NEI)]
            for si, (th, ei) in enumerate(seq):
                for m in range(NHT):
                    if (ei, th, m) not in tok_r1:
                        load_r12(ei, th, m)
                    ffn_h_block(ei, th, m, m % 2)
                if si + 1 < len(seq):
                    nth, nei = seq[si + 1]
                    for m in (0, 1):
                        load_r12(nei, nth, m)
                ffn_w3_phase(ei, th)
            TT["tok"].update(last_z=ffn["last_z"])

        if max_stage >= 9:
            wait_acc = [("dsg0", S.counts.get("dsg0", 0)),
                        ("dsg1", S.counts.get("dsg1", 0))]
            wait_x1 = [("dx1st", S.counts.get("dx1st", 0))]
            prev_acc_used = [("V", S.counts["V"]), ("V", S.counts["V"])]
            prev_x1b_used = [ffn["stage"][0], ffn["stage"][1]]
            prev_xn4_used = [prev_xn_used[0], prev_xn_used[1]]
            for c in range(NCH):
                p = c % 2
                t_la = S.sig("sync", lambda e, c=c, p=p: e.dma_start(
                    out=x0sl(p), in_=accd.ap()[c * CH:(c + 1) * CH, :]),
                    f"dba{p}", 16, waits=wait_acc + [prev_acc_used[p]])
                t_lx = S.sig("sync", lambda e, c=c, p=p: e.dma_start(
                    out=x1sl(p), in_=x1d.ap()[c * CH:(c + 1) * CH, :]),
                    f"dbx{p}", 16, waits=wait_x1 + [prev_x1b_used[p]])
                grp = stat_g(0, p)
                tok_ln4 = ln_stats(x0sl(p), grp, t_la, trash2)
                rs4, nm4 = grp[2], grp[3]
                t_xn4 = S.sig("scalar",
                              lambda e, p=p, rs4=rs4, nm4=nm4: e.activation(
                                  xnsl(p), x0sl(p), AF.Identity, scale=rs4,
                                  bias=nm4), "S", 1,
                              waits=[tok_ln4, prev_xn4_used[p]])
                t_x2 = S.sig("vector", lambda e, p=p: e.tensor_tensor(
                    x0sl(p), xnsl(p), x1sl(p), ALU.add), "V", 1,
                    waits=[t_xn4, t_lx])
                prev_xn4_used[p] = t_x2
                prev_x1b_used[p] = t_x2
                t_so = S.sig("sync", lambda e, c=c, p=p: e.dma_start(
                    out=out_d[c * CH:(c + 1) * CH, :], in_=x0sl(p)),
                    "dout", 16, waits=[t_x2])
                prev_acc_used[p] = t_so
            S.emit("sync", lambda e: e.nop(),
                   waits=[("dout", S.counts.get("dout", 0))])

        if dbg is not None:
            dbg(nc, S, TT)

        handles = {}
        for name in S.counts:
            handles[name] = es.enter_context(nc.semaphore(name))
        with nc.Block() as block:
            S.realize(block, handles)

    return nc


def kernel(**inputs):
    in_maps = _host_prep(inputs)
    nc = build_program()
    res = run_bass_kernel_spmd(nc, in_maps, core_ids=list(range(NCORE)))
    out = np.empty((B, L, D), np.float32)
    for c in range(NCORE):
        b, s = c // 4, c % 4
        out[b, s * T:(s + 1) * T, :] = np.asarray(res.results[c]["out"])
    return out


# version 07
# speedup vs baseline: 1.0037x; 1.0018x over previous
import contextlib

import numpy as np
import ml_dtypes

import concourse.bass as bass
import concourse.mybir as mybir
from concourse.bass_utils import run_bass_kernel_spmd

F32 = mybir.dt.float32
BF16 = mybir.dt.bfloat16
AF = mybir.ActivationFunctionType
ALU = mybir.AluOpType
AX = mybir.AxisListType
bfloat16 = ml_dtypes.bfloat16

B, L, D = 2, 8192, 1024
NCORE = 8
T = 2048
NCH, CH = 16, 128
NF = 8
NH, DH = 16, 64
HDIM, HPAD, NHT = 2730, 2816, 22
E = 8
TOPK = 4
LN_EPS = 1e-5
TINY = float(np.finfo(np.float32).tiny)
LN2 = float(np.log(2.0))
BIG = 1.0e38
FFN_NEI = E + 1
W3A = list(range(0, 12))
W3B = list(range(12, NHT))

_ENGINES = ("sync", "gpsimd", "vector", "scalar", "tensor")


def _collapse(waits):
    best = {}
    for w in waits:
        if w is None:
            continue
        s, v = w
        if s not in best or v > best[s]:
            best[s] = v
    return tuple(best.items())


class _Sched:
    def __init__(self):
        self.ops = {e: [] for e in _ENGINES}
        self.counts = {}

    def emit(self, eng, fn, waits=(), sig=()):
        sig = tuple(sig)
        self.ops[eng].append((_collapse(waits), fn, sig))
        toks = []
        for s, a in sig:
            self.counts[s] = self.counts.get(s, 0) + a
            toks.append((s, self.counts[s]))
        if len(toks) == 1:
            return toks[0]
        return toks

    def sig(self, eng, fn, sem, amt=1, waits=()):
        return self.emit(eng, fn, waits=waits, sig=[(sem, amt)])

    def realize(self, block, handles):
        for ename in _ENGINES:
            ops = self.ops[ename]

            def body(eng, ops=ops):
                for waits, fn, sig in ops:
                    for s, v in waits:
                        eng.wait_ge(handles[s], v)
                    inst = fn(eng)
                    for s, a in sig:
                        inst.then_inc(handles[s], a)

            getattr(block, ename)(body)


def _apn(t, off, *dims):
    a = t[:, :]
    return bass.AP(a.tensor, off,
                   [list(a.ap[0])] + [[s, c] for (s, c) in dims])



def _lhsT_tiles(wT, k, m):
    assert wT.shape == (k, m) and k % 128 == 0 and m % 128 == 0
    return np.ascontiguousarray(
        wT.reshape(k // 128, 128, m // 128, 128).transpose(2, 0, 1, 3))


def _pad_h_cols(wT):
    out = np.zeros((wT.shape[0], HPAD), np.float32)
    out[:, :HDIM] = wT
    return out


def _pad_h_rows(wT):
    out = np.zeros((HPAD, wT.shape[1]), np.float32)
    out[:HDIM, :] = wT
    return out


def _host_prep(inputs):
    f32 = lambda a: np.ascontiguousarray(np.asarray(a, np.float32))
    x = f32(inputs["L0_attn_x"])
    sh = {}
    sh["wq"] = np.ascontiguousarray(f32(inputs["L0_attn_Wq"]).T)
    sh["wv"] = np.ascontiguousarray(f32(inputs["L0_attn_Wv"]).T)
    sh["wg"] = np.ascontiguousarray(f32(inputs["L0_attn_Wg"]).T)
    sh["wo"] = np.ascontiguousarray(f32(inputs["L0_attn_Wo"]).T)
    beta = f32(inputs["L0_attn_beta"]).reshape(16)
    sh["beta_row"] = np.ascontiguousarray(beta.reshape(1, 16))
    sh["beta_col"] = np.ascontiguousarray(beta.reshape(16, 1))
    sh["gamma"] = f32(inputs["L0_attn_ln_gamma"]).reshape(1, D)
    sh["lnbeta"] = f32(inputs["L0_attn_ln_beta"]).reshape(1, D)
    sh["wgr"] = np.ascontiguousarray(f32(inputs["L0_ffn_Wg"]).T)
    sh["rbias"] = f32(inputs["L0_ffn_routing_bias"]).reshape(1, E)
    sh["ws1"] = _lhsT_tiles(_pad_h_cols(f32(inputs["L0_ffn_Ws1"]).T), D, HPAD)
    sh["ws2"] = _lhsT_tiles(_pad_h_cols(f32(inputs["L0_ffn_Ws2"]).T), D, HPAD)
    sh["ws3"] = np.ascontiguousarray(
        _pad_h_rows(f32(inputs["L0_ffn_Ws3"]).T).reshape(NHT, 128, D))
    sh["w1"] = np.stack(
        [_lhsT_tiles(_pad_h_cols(f32(inputs["L0_ffn_W1"][e]).T), D, HPAD)
         for e in range(E)])
    sh["w2"] = np.stack(
        [_lhsT_tiles(_pad_h_cols(f32(inputs["L0_ffn_W2"][e]).T), D, HPAD)
         for e in range(E)])
    sh["w3"] = np.stack(
        [np.ascontiguousarray(
            _pad_h_rows(f32(inputs["L0_ffn_W3"][e]).T).reshape(NHT, 128, D))
         for e in range(E)])

    sh["ident"] = np.eye(128, dtype=bfloat16)
    e127 = np.zeros((128, 1), bfloat16); e127[127, 0] = 1
    sh["e127"] = e127
    sh["ones116"] = np.ones((1, 16), bfloat16)
    sh["ones1128b"] = np.ones((1, 128), bfloat16)
    sh["ones1128f"] = np.ones((1, 128), np.float32)
    d2t = np.full((128, 128), BIG, np.float32)
    for k in range(128):
        d2t[k, k:] = np.arange(0, 128 - k, dtype=np.float32)
    sh["d2t"] = d2t.astype(bfloat16)
    sh["tp16"] = np.broadcast_to(
        np.arange(1, 129, dtype=np.float32), (16, 128)).astype(bfloat16).copy()
    dlx = np.full((17, 16), BIG, np.float32)
    for s in range(17):
        for c in range(16):
            if c >= s:
                dlx[s, c] = 128.0 * (c - s)
    sh["dlx"] = dlx
    dend = np.full((17, 1), BIG, np.float32)
    for s in range(1, 17):
        dend[s, 0] = 128.0 * (16 - s)
    sh["dend"] = dend
    bdmask = np.zeros((16, D), bfloat16)
    for h in range(16):
        bdmask[h, h * 64:(h + 1) * 64] = 1
    sh["bdmask"] = bdmask
    sh["iota8"] = np.broadcast_to(
        np.arange(E, dtype=np.float32), (128, E)).copy()

    in_maps = []
    for c in range(NCORE):
        b, s = c // 4, c % 4
        dcore = np.full((8, 1), BIG, np.float32)
        for i in range(8):
            if i // 4 == b and (i % 4) < s:
                dcore[i, 0] = 2048.0 * (s - 1 - (i % 4))
        m = dict(sh)
        m["x0"] = np.ascontiguousarray(x[b, s * T:(s + 1) * T, :])
        m["dcore8"] = dcore
        in_maps.append(m)
    return in_maps



def build_program(max_stage=99, dbg=None):
    nc = bass.Bass("TRN2", target_bir_lowering=False)

    d = {}
    def din(name, shape, dtype=F32):
        d[name] = nc.dram_tensor(name, shape, dtype, kind="ExternalInput").ap()
    din("x0", [T, D])
    for n in ("wq", "wv", "wg", "wo"):
        din(n, [D, D])
    din("wgr", [D, E]); din("rbias", [1, E])
    din("ws1", [NHT, NF, 128, 128]); din("ws2", [NHT, NF, 128, 128])
    din("ws3", [NHT, 128, D])
    din("w1", [E, NHT, NF, 128, 128]); din("w2", [E, NHT, NF, 128, 128])
    din("w3", [E, NHT, 128, D])
    din("beta_row", [1, 16]); din("beta_col", [16, 1])
    din("gamma", [1, D]); din("lnbeta", [1, D])
    din("ident", [128, 128], BF16); din("e127", [128, 1], BF16)
    din("ones116", [1, 16], BF16); din("ones1128b", [1, 128], BF16)
    din("ones1128f", [1, 128])
    din("d2t", [128, 128], BF16); din("tp16", [16, 128], BF16)
    din("dlx", [17, 16]); din("dend", [17, 1]); din("dcore8", [8, 1])
    din("bdmask", [16, D], BF16); din("iota8", [128, E])

    out_d = nc.dram_tensor("out", [T, D], F32, kind="ExternalOutput").ap()
    x1d = nc.dram_tensor("x1d", [T, D], F32, kind="Internal")
    qd = nc.dram_tensor("qd", [T, D], BF16, kind="Internal")
    accd = nc.dram_tensor("accd", [T, D], F32, kind="Internal")
    dscr = nc.dram_tensor("dscr", [16 * D], BF16, kind="Internal")
    dscr2 = nc.dram_tensor("dscr2", [16 * D], BF16, kind="Internal")
    cc_in = nc.dram_tensor("cc_in", [1, D], F32, kind="Internal")
    cc_out = nc.dram_tensor("cc_out", [8, D], F32, kind="Internal")

    S = _Sched()
    TT = {"tok": {}}

    with contextlib.ExitStack() as es:
        def sb(name, shape, dtype):
            t = es.enter_context(nc.sbuf_tensor(name, shape, dtype))
            TT[name] = t
            return t

        def pstn(name, shape, dtype):
            t = es.enter_context(nc.psum_tensor(name, shape, dtype))
            TT[name] = t
            return t

        arena = sb("arena", [128, 49152], BF16)
        U1T, ST0, VAL = 0, 16384, 32768
        warena = sb("warena", [128, 16384], BF16)
        fpool = sb("fpool", [128, 4096], F32)
        bpool = sb("bpool", [128, 18 * 1024], BF16)
        BGS, BST, BRET, BY, BYGT, BAO, BU2, BXN, BQT = (
            0, 2048, 4096, 6144, 8192, 10240, 12288, 14336, 16384)

        gamma_rep = sb("gamma_rep", [128, D], BF16)
        beta_rep = sb("beta_rep", [128, D], BF16)
        gates_all = sb("gates_all", [128, NCH * E], F32)
        w_all = sb("w_all", [128, NCH * E], F32)
        gmax = sb("gmax", [128, NCH], F32)
        gidx = sb("gidx", [128, NCH], F32)
        gm1 = sb("gm1", [128, NCH], F32)
        gz = sb("gz", [128, NCH], F32)

        c_ident = sb("c_ident", [128, 128], BF16)
        c_e127 = sb("c_e127", [128, 1], BF16)
        c_ones116 = sb("c_ones116", [1, 16], BF16)
        c_ones1128b = sb("c_ones1128b", [1, 128], BF16)
        c_ones1128f = sb("c_ones1128f", [1, 128], F32)
        c_d2t = sb("c_d2t", [128, 128], BF16)
        c_tp16 = sb("c_tp16", [16, 128], BF16)
        c_dlx = sb("c_dlx", [17, 16], F32)
        c_dend = sb("c_dend", [17, 1], F32)
        c_dcore = sb("c_dcore", [8, 1], F32)
        c_bdmask = sb("c_bdmask", [16, D], BF16)
        c_iota8 = sb("c_iota8", [128, E], F32)
        beta_row = sb("beta_row_sb", [1, 16], F32)
        beta_col = sb("beta_col_sb", [16, 1], F32)
        lnl_row = sb("lnl_row", [1, 16], F32)
        lnl_col = sb("lnl_col", [16, 1], F32)
        lam_rep = sb("lam_rep", [128, 16], F32)
        A_all = sb("A_all", [128, NH * 128], BF16)
        P_all = sb("P_all", [16, 128], BF16)
        Lx_all = sb("Lx_all", [17, NH * 16], BF16)
        EndW = sb("EndW", [17, 16], BF16)
        CoreW = sb("CoreW", [8, 16], BF16)
        eps_col = sb("eps_col", [128, 1], F32)
        rbias_b = sb("rbias_b", [1, E], BF16)
        rbias_f = sb("rbias_f", [1, E], F32)
        wgr_b = sb("wgr_b", [128, NF * E], BF16)
        stat = sb("stat", [128, 24], F32)
        carr_buf = sb("carr_buf", [1, D], BF16)
        bd_sb = sb("bd_sb", [16, D], BF16)

        pA = pstn("pA", [128, 1024], F32)
        pB = pstn("pB", [128, 1024], F32)
        pC = pstn("pC", [128, 1024], F32)
        pD = pstn("pD", [128, 2048], BF16)

        x0sl = lambda p: fpool[:, p * 1024:(p + 1) * 1024]
        x1sl = lambda p: fpool[:, (2 + p) * 1024:(3 + p) * 1024]
        stage_sl = x1sl
        tkw = fpool[:, 0:128]; tkm = fpool[:, 128:256]
        tkt = fpool[:, 256:384]; tkt2 = fpool[:, 384:512]
        rend_sb = fpool[0:1, 2 * 1024:3 * 1024]
        rendall = fpool[0:8, 3 * 1024:4 * 1024]

        gsl = lambda p: bpool[:, BGS + p * 1024: BGS + (p + 1) * 1024]
        stsl = lambda p: bpool[:, BST + p * 1024: BST + (p + 1) * 1024]
        retsl = lambda p: bpool[:, BRET + p * 1024: BRET + (p + 1) * 1024]
        ysl = lambda i: bpool[:, BY + i * 1024: BY + (i + 1) * 1024]
        u1sl = ysl
        ygtsl = lambda p: bpool[:, BYGT + p * 1024: BYGT + (p + 1) * 1024]
        aosl = lambda p: bpool[:, BAO + p * 1024: BAO + (p + 1) * 1024]
        u2sl = lambda p: bpool[:, BU2 + p * 1024: BU2 + (p + 1) * 1024]
        s1sl = u2sl
        xnsl = lambda p: bpool[:, BXN + p * 1024: BXN + (p + 1) * 1024]
        qtsl = lambda p: bpool[:, BQT + p * 1024: BQT + (p + 1) * 1024]
        ln1_trash = bpool[:, BGS:BGS + 1024]
        trash2 = A_all[:, 0:1024]
        r_buf = bpool[0:1, BGS:BGS + 2048]
        rendall_b = bpool[0:8, BRET:BRET + 1024]
        rx_mat = bpool[0:17, BAO:BAO + 1024]
        carr_sb = bpool[0:16, BYGT:BYGT + 1024]
        gamma_in = bpool[0:1, BU2:BU2 + 1024]
        lnbeta_in = bpool[0:1, BU2 + 1024:BU2 + 2048]

        def stat_g(base, parity):
            o = base + 4 * parity
            return (stat[:, o:o + 1], stat[:, o + 1:o + 2],
                    stat[:, o + 2:o + 3], stat[:, o + 3:o + 4])

        def w_slot(i):
            return warena[:, i * 1024:(i + 1) * 1024]

        def w_region(i0):
            return warena[:, i0 * 1024:(i0 + 8) * 1024]

        def u1T_tile(k, c):
            return _apn(arena, U1T + k * 2048 + c * 128, (1, 128))

        def st0_sl(c):
            return _apn(arena, ST0 + c * 1024, (1, 1024))

        def z_tile(m, c8):
            return _apn(arena, m * 1024 + c8 * 128, (1, 128))

        def z_m(m):
            return _apn(arena, m * 1024, (1, 1024))

        def u2T_rhs(k, th, nh):
            return _apn(arena, VAL + k * 2048 + th * 1024 + nh * 512, (1, 512))

        def u2T_cell(c):
            return _apn(arena, VAL + c * 128, (2048, 8), (1, 128))

        creqs = [
            (d["d2t"], c_d2t), (d["tp16"], c_tp16), (d["dlx"], c_dlx),
            (d["dend"], c_dend), (d["dcore8"], c_dcore), (d["iota8"], c_iota8),
            (d["beta_row"], beta_row), (d["beta_col"], beta_col),
            (d["rbias"], rbias_f),
            (d["ident"], c_ident), (d["e127"], c_e127),
            (d["ones116"], c_ones116), (d["ones1128b"], c_ones1128b),
            (d["ones1128f"], c_ones1128f), (d["bdmask"], c_bdmask),
        ]
        for src, dst in creqs:
            S.sig("sync", lambda e, s=src, t=dst: e.dma_start(out=t[:, :], in_=s),
                  "dcst", 16)
        tok_cst = ("dcst", 16 * len(creqs))

        S.sig("gpsimd", lambda e: e.dma_start(
            out=gamma_in, in_=d["gamma"]), "dgb", 16)
        S.sig("gpsimd", lambda e: e.dma_start(
            out=lnbeta_in, in_=d["lnbeta"]), "dgb", 16)
        tok_gb = ("dgb", 32)
        tok_wgr = S.sig("gpsimd", lambda e: e.dma_start(
            out=wgr_b[:, :].rearrange("p (k m) -> p k m", m=E),
            in_=d["wgr"].rearrange("(k p) m -> p k m", p=128)),
            "dwgr", 16)
        tok_wv = S.sig("gpsimd", lambda e: e.dma_start(
            out=w_region(0).rearrange("p (k m) -> p k m", m=1024), in_=d["wv"].rearrange("(k p) m -> p k m", p=128)),
            "wva", 16)
        tok_wq = S.sig("gpsimd", lambda e: e.dma_start(
            out=w_region(8).rearrange("p (k m) -> p k m", m=1024), in_=d["wq"].rearrange("(k p) m -> p k m", p=128)),
            "wqa", 16)

        tok_eps = S.sig("vector", lambda e: e.memset(eps_col[:, :], LN_EPS),
                        "V", 1)

        def lam_pipe(src, dst):
            t1 = S.sig("scalar", lambda e: e.activation(
                dst[:, :], src[:, :], AF.Exp, scale=-LN2), "S", 1,
                waits=[tok_cst])
            t2 = S.sig("vector", lambda e: e.tensor_scalar(
                dst[:, :], dst[:, :], -1.0, 1.0, ALU.mult, ALU.add), "V", 1,
                waits=[t1])
            t3 = S.sig("vector", lambda e: e.tensor_scalar(
                dst[:, :], dst[:, :], TINY, 1.0 - 1e-9, ALU.max, ALU.min),
                "V", 1, waits=[t2])
            return S.sig("scalar", lambda e: e.activation(
                dst[:, :], dst[:, :], AF.Ln), "S", 1, waits=[t3])
        tok_lr = lam_pipe(beta_row, lnl_row)
        tok_lc = lam_pipe(beta_col, lnl_col)

        tok_t = S.sig("tensor", lambda e: e.matmul(
            pA[:, 0:16], c_ones1128f[:, :], lnl_row[:, :], start=True,
            stop=True), "T", 1, waits=[tok_cst, tok_lr])
        tok_lrep = S.sig("vector", lambda e: e.tensor_copy(
            lam_rep[:, :], pA[:, 0:16]), "V", 1, waits=[tok_t])

        def rep_row(src_row, dst, prev):
            tks = None
            for half in range(2):
                tks = S.sig("tensor", lambda e, h=half: e.matmul(
                    pB[:, h * 512:(h + 1) * 512], c_ones1128b[:, :],
                    src_row[0:1, h * 512:(h + 1) * 512], start=True, stop=True),
                    "T", 1, waits=[tok_cst, tok_gb, prev] if half == 0 else [])
            return S.sig("vector", lambda e: e.tensor_copy(dst[:, :], pB[:, :]),
                         "V", 1, waits=[tks])
        tok_grep = rep_row(gamma_in, gamma_rep, None)
        tok_brep = rep_row(lnbeta_in, beta_rep, tok_grep)

        tok_rb = S.sig("vector", lambda e: e.tensor_copy(
            rbias_b[:, :], rbias_f[:, :]), "V", 1, waits=[tok_cst])

        def act_build(dst_ap, src_ap, scale_ap, waits):
            return S.sig("scalar", lambda e: e.activation(
                dst_ap, src_ap, AF.Exp, scale=scale_ap), "S", 1, waits=waits)
        tok_A = tok_P = tok_Lx = tok_End = tok_Core = None
        for h in range(NH):
            tok_A = act_build(A_all[:, h * 128:(h + 1) * 128], c_d2t[:, :],
                              lam_rep[:, h:h + 1], [tok_lrep, tok_cst])
        tok_P = act_build(P_all[:, :], c_tp16[:, :], lnl_col[:, :],
                          [tok_lc, tok_cst])
        for h in range(NH):
            tok_Lx = act_build(Lx_all[:, h * 16:(h + 1) * 16], c_dlx[:, :],
                               lam_rep[0:17, h:h + 1], [tok_lrep, tok_cst])
        for h in range(NH):
            tok_End = act_build(EndW[:, h:h + 1], c_dend[:, :],
                                lam_rep[0:17, h:h + 1], [tok_lrep, tok_cst])
        for h in range(NH):
            tok_Core = act_build(CoreW[:, h:h + 1], c_dcore[:, :],
                                 lam_rep[0:8, h:h + 1], [tok_lrep, tok_cst])

        def ln_stats(in_ap, grp, wait_in, trash_ap):
            sm, sq, rs, nm = grp
            t1 = S.sig("vector", lambda e: e.tensor_reduce(
                sm, in_ap, AX.X, ALU.add), "V", 1, waits=[wait_in])
            t2 = S.sig("vector", lambda e: e.tensor_scalar(
                sm, sm, -1.0 / D, None, ALU.mult), "V", 1, waits=[t1])
            t3 = S.sig("scalar", lambda e: e.activation(
                trash_ap, in_ap, AF.Square, bias=sm, accum_out=sq), "S", 1,
                waits=[t2])
            t4 = S.sig("scalar", lambda e: e.activation(
                rs, sq, AF.Ln, scale=1.0 / D, bias=eps_col[:, :]), "S", 1,
                waits=[t3, tok_eps])
            t5 = S.sig("scalar", lambda e: e.activation(
                rs, rs, AF.Exp, scale=-0.5), "S", 1, waits=[t4])
            t6 = S.sig("vector", lambda e: e.tensor_tensor(
                nm, sm, rs, ALU.mult), "V", 1, waits=[t5])
            return t6

        if max_stage >= 2:
            tok_x0 = [None] * NCH
            tok_u1T_ev = [None] * NCH
            tok_vev = [None] * NCH
            tok_qst = [None] * NCH
            prev_tpD = None
            prev_u1_used = [None, None]
            prev_x0_used = [None, None]
            prev_pAB_ev = [tok_lrep, tok_brep]
            prev_qt_used = [None, None]

            for c in range(NCH):
                p = c % 2
                tok_x0[c] = S.sig("sync", lambda e, c=c, p=p: e.dma_start(
                    out=x0sl(p), in_=d["x0"][c * CH:(c + 1) * CH, :]),
                    f"dx0{p}", 16, waits=[prev_x0_used[p]])
                grp = stat_g(0, p)
                tok_ln1 = ln_stats(x0sl(p), grp, tok_x0[c], ln1_trash)
                rs, nm = grp[2], grp[3]
                t7 = S.sig("scalar", lambda e, p=p, rs=rs, nm=nm: e.activation(
                    u1sl(p), x0sl(p), AF.Identity, scale=rs, bias=nm), "S", 1,
                    waits=[tok_ln1, prev_u1_used[p]])
                prev_x0_used[p] = t7

                ttp = None
                for k in range(NF):
                    ttp = S.sig("tensor", lambda e, p=p, k=k: e.transpose(
                        pD[:, k * 128:(k + 1) * 128],
                        u1sl(p)[:, k * 128:(k + 1) * 128], c_ident[:, :]),
                        "T", 1, waits=[t7, prev_tpD, tok_cst] if k == 0 else [])
                ev = None
                for half in range(2):
                    ev = S.sig("vector", lambda e, c=c, half=half: e.tensor_copy(
                        _apn(arena, U1T + half * 4 * 2048 + c * 128,
                             (2048, 4), (1, 128)),
                        _apn(pD, half * 512, (128, 4), (1, 128))),
                        "V", 1, waits=[ttp] if half == 0 else [])
                prev_tpD = ev
                prev_u1_used[p] = ev
                tok_u1T_ev[c] = ev

                vps = pA if p == 0 else pB
                tmm = None
                for k in range(NF):
                    for nh in range(2):
                        first = (k == 0 and nh == 0)
                        tmm = S.sig("tensor",
                                    lambda e, c=c, k=k, nh=nh, vps=vps: e.matmul(
                                        vps[:, nh * 512:(nh + 1) * 512],
                                        u1T_tile(k, c),
                                        w_slot(k)[:, nh * 512:(nh + 1) * 512],
                                        start=(k == 0), stop=(k == NF - 1)),
                                    "T", 1,
                                    waits=[ev, tok_wv, prev_pAB_ev[p]]
                                    if first else [])
                tok_vev[c] = S.sig("scalar", lambda e, c=c, vps=vps: e.activation(
                    _apn(arena, VAL + c * 1024, (1, 1024)), vps[:, :],
                    AF.Identity), "S", 1, waits=[tmm])
                qmm = None
                for k in range(NF):
                    for nh in range(2):
                        first = (k == 0 and nh == 0)
                        qmm = S.sig("tensor",
                                    lambda e, c=c, k=k, nh=nh, vps=vps: e.matmul(
                                        vps[:, nh * 512:(nh + 1) * 512],
                                        u1T_tile(k, c),
                                        w_slot(8 + k)[:, nh * 512:(nh + 1) * 512],
                                        start=(k == 0), stop=(k == NF - 1)),
                                    "T", 1,
                                    waits=[tok_vev[c], tok_wq] if first else [])
                tok_qev = S.sig("scalar", lambda e, p=p, vps=vps: e.activation(
                    qtsl(p), vps[:, :], AF.Identity), "S", 1,
                    waits=[qmm, prev_qt_used[p]])
                tok_qst[c] = S.sig("sync", lambda e, c=c, p=p: e.dma_start(
                    out=qd.ap()[c * CH:(c + 1) * CH, :], in_=qtsl(p)),
                    f"dqs{p}", 16, waits=[tok_qev])
                prev_qt_used[p] = tok_qst[c]
                prev_pAB_ev[p] = tok_qev
            TT["tok"].update(vev=tok_vev, u1T=tok_u1T_ev, qst=tok_qst)

        if max_stage >= 3:
            tok_st0 = [None] * NH
            prev_h_ev = [prev_pAB_ev[0], prev_pAB_ev[1]]
            for h in range(NH):
                hp = h % 2
                hps = pA if hp == 0 else pB
                mm = None
                for half in range(2):
                    mm = S.sig("tensor",
                               lambda e, h=h, half=half, hps=hps: e.matmul(
                                   hps[:, half * 512:(half + 1) * 512],
                                   A_all[:, h * 128:(h + 1) * 128],
                                   _apn(arena, VAL + half * 8 * 1024 + h * 64,
                                        (1024, 8), (1, 64)),
                                   start=True, stop=True), "T", 1,
                               waits=[tok_A, tok_vev[NCH - 1], prev_h_ev[hp]]
                               if half == 0 else [])
                tok_st0[h] = S.sig("vector", lambda e, h=h, hps=hps: e.tensor_copy(
                    _apn(arena, ST0 + h * 64, (1024, 16), (1, 64)),
                    _apn(hps, 0, (64, 16), (1, 64))),
                    "V", 1, waits=[mm])
                prev_h_ev[hp] = tok_st0[h]

            tok_rx0 = S.sig("vector", lambda e: e.memset(rx_mat, 0.0),
                            "V", 1, waits=[prev_u1_used[0], prev_u1_used[1]])
            tok_rdma = [None] * NCH
            prev_r = [prev_h_ev[0], prev_h_ev[1]]
            prev_rb = [tok_vev[NCH - 1], tok_vev[NCH - 1]]
            for c in range(NCH):
                rp = c % 2
                rps = pC if rp == 0 else pA
                mm = None
                for half in range(2):
                    mm = S.sig("tensor",
                               lambda e, c=c, half=half, rps=rps: e.matmul(
                                   rps[0:1, half * 512:(half + 1) * 512],
                                   c_e127[:, :],
                                   _apn(arena, ST0 + c * 1024 + half * 512,
                                        (1, 512)),
                                   start=True, stop=True), "T", 1,
                               waits=[tok_st0[NH - 1], tok_st0[NH - 2],
                                      prev_r[rp], tok_cst] if half == 0 else [])
                tok_rev = S.sig("vector", lambda e, rp=rp, rps=rps: e.tensor_copy(
                    r_buf[0:1, rp * 1024:(rp + 1) * 1024], rps[0:1, 0:1024]),
                    "V", 1, waits=[mm, prev_rb[rp]])
                prev_r[rp] = tok_rev
                tok_rdma[c] = S.sig("sync", lambda e, c=c, rp=rp: e.dma_start(
                    out=dscr.ap()[c * 1024:(c + 1) * 1024],
                    in_=r_buf[0:1, rp * 1024:(rp + 1) * 1024]),
                    f"drf{rp}", 16, waits=[tok_rev])
                prev_rb[rp] = tok_rdma[c]

            t_rd2 = S.sig("sync", lambda e: e.dma_start(
                out=rx_mat[1:17, :],
                in_=dscr.ap().rearrange("(a b) -> a b", b=D)), "drd", 16,
                waits=[tok_rdma[NCH - 1], tok_rdma[NCH - 2], tok_rx0])

            mm = None
            for h in range(NH):
                mm = S.sig("tensor", lambda e, h=h: e.matmul(
                    pC[0:1, h * 64:(h + 1) * 64], EndW[:, h:h + 1],
                    rx_mat[0:17, h * 64:(h + 1) * 64], start=True, stop=True),
                    "T", 1,
                    waits=[t_rd2, tok_rx0, tok_End, prev_r[0], prev_r[1]]
                    if h == 0 else [])
            tok_rend = S.sig("vector", lambda e: e.tensor_copy(
                rend_sb, pC[0:1, 0:1024]), "V", 1, waits=[mm])
            TT["tok"].update(st0=tok_st0, rend=tok_rend)

        if max_stage >= 4:
            t_cc1 = S.sig("sync", lambda e: e.dma_start(
                out=cc_in.ap(), in_=rend_sb), "dcc", 16, waits=[tok_rend])
            t_cc = S.sig("gpsimd", lambda e: e.collective_compute(
                "AllGather", ALU.bypass, replica_groups=[list(range(8))],
                ins=[cc_in.ap().opt()], outs=[cc_out.ap().opt()]), "G", 1,
                waits=[t_cc1])
            t_cc2 = S.sig("gpsimd", lambda e: e.dma_start(
                out=rendall, in_=cc_out.ap()), "dcc", 16, waits=[t_cc])
            t_rab = S.sig("vector", lambda e: e.tensor_copy(
                rendall_b, rendall), "V", 1, waits=[t_cc2])

            pre_g = {}
            if max_stage >= 5:
                tok_wgl = S.sig("gpsimd", lambda e: e.dma_start(
                    out=w_region(0).rearrange("p (k m) -> p k m", m=1024),
                    in_=d["wg"].rearrange("(k p) m -> p k m", p=128)),
                    "wva", 16, waits=[tok_vev[NCH - 1]])
                for gc, ps, pfree in ((0, pB, prev_h_ev[1]), (1, pA, prev_r[1])):
                    gmm = None
                    for k in range(NF):
                        for nh in range(2):
                            first = (k == 0 and nh == 0)
                            gmm = S.sig("tensor",
                                        lambda e, gc=gc, k=k, nh=nh, ps=ps:
                                        e.matmul(
                                            ps[:, nh * 512:(nh + 1) * 512],
                                            u1T_tile(k, gc),
                                            w_slot(k)[:, nh * 512:(nh + 1) * 512],
                                            start=(k == 0), stop=(k == NF - 1)),
                                        "T", 1,
                                        waits=[tok_wgl, pfree, tok_u1T_ev[gc]]
                                        if first else [])
                    pre_g[gc] = S.sig("scalar", lambda e, gc=gc, ps=ps:
                                      e.activation(gsl(gc % 2), ps[:, :],
                                                   AF.Silu), "S", 1,
                                      waits=[gmm])

            mm = None
            for h in range(NH):
                mm = S.sig("tensor", lambda e, h=h: e.matmul(
                    pC[0:1, h * 64:(h + 1) * 64], CoreW[:, h:h + 1],
                    rendall_b[0:8, h * 64:(h + 1) * 64], start=True, stop=True),
                    "T", 1, waits=[t_rab, tok_Core, tok_rend] if h == 0 else [])
            tok_cc_ev = S.sig("vector", lambda e: e.tensor_copy(
                rx_mat[0:1, :], pC[0:1, 0:1024]), "V", 1, waits=[mm])

            mm = None
            for h in range(NH):
                mm = S.sig("tensor", lambda e, h=h: e.matmul(
                    pC[0:16, h * 64:(h + 1) * 64],
                    Lx_all[:, h * 16:(h + 1) * 16],
                    rx_mat[0:17, h * 64:(h + 1) * 64], start=True, stop=True),
                    "T", 1, waits=[tok_cc_ev, tok_Lx] if h == 0 else [])
            tok_carr = S.sig("vector", lambda e: e.tensor_copy(
                carr_sb, pC[0:16, :]), "V", 1, waits=[mm])
            t_cf1 = S.sig("sync", lambda e: e.dma_start(
                out=dscr2.ap().rearrange("(a b) -> a b", b=D), in_=carr_sb),
                "drd", 16, waits=[tok_carr])
            TT["tok"].update(carr=tok_carr, cf1=t_cf1)

        if max_stage >= 5:
            tok_wol = S.sig("gpsimd", lambda e: e.dma_start(
                out=w_region(8).rearrange("p (k m) -> p k m", m=1024),
                in_=d["wo"].rearrange("(k p) m -> p k m", p=128)),
                "wqa", 16, waits=[prev_pAB_ev[0], prev_pAB_ev[1]])

            prev_g_used = [None, None]
            prev_pA_user = pre_g[1]
            prev_pB_user = pre_g[0]
            prev_pC_user = tok_carr
            prev_tpD2 = prev_tpD
            prev_x0p_used = [prev_x0_used[0], prev_x0_used[1]]
            prev_xn_used = [None, None]
            prev_x1_used = [None, None]
            prev_st_used = [None, None]
            prev_ret_used = [None, None]
            prev_ao_used = [None, None]
            prev_u2_used = [None, None]
            prev_ygt_used = [tok_carr, tok_carr]
            prev_qt2_used = [prev_qt_used[0], prev_qt_used[1]]
            prev_cb_used = t_cf1
            tok_u2T_ev = [None] * NCH
            tok_x1st = [None] * NCH
            tok_gates = [None] * NCH
            tok_g15 = None
            tok_wo15 = None

            for c in range(NCH):
                p = c % 2
                t_qld = S.sig("sync", lambda e, c=c, p=p: e.dma_start(
                    out=qtsl(p), in_=qd.ap()[c * CH:(c + 1) * CH, :]),
                    f"dql{p}", 16,
                    waits=[prev_qt2_used[p], tok_qst[c]])
                t_cld = S.sig("sync", lambda e, c=c: e.dma_start(
                    out=carr_buf[:, :],
                    in_=dscr2.ap()[c * 1024:(c + 1) * 1024]), "dcl", 16,
                    waits=[t_cf1, prev_cb_used])

                if c in pre_g:
                    tok_gev = pre_g[c]
                else:
                    gmm = None
                    for k in range(NF):
                        for nh in range(2):
                            first = (k == 0 and nh == 0)
                            gmm = S.sig("tensor",
                                        lambda e, c=c, k=k, nh=nh: e.matmul(
                                            pB[:, nh * 512:(nh + 1) * 512],
                                            u1T_tile(k, c),
                                            w_slot(k)[:, nh * 512:(nh + 1) * 512],
                                            start=(k == 0), stop=(k == NF - 1)),
                                        "T", 1,
                                        waits=[prev_pB_user,
                                               tok_u1T_ev[c]] if first else [])
                    tok_gev = S.sig("scalar", lambda e, p=p: e.activation(
                        gsl(p), pB[:, :], AF.Silu), "S", 1,
                        waits=[gmm, prev_g_used[p]])
                    prev_pB_user = tok_gev
                if c == NCH - 1:
                    tok_g15 = tok_gev

                bmm = None
                for nh in range(2):
                    bmm = S.sig("tensor", lambda e, nh=nh: e.matmul(
                        pC[0:16, nh * 512:(nh + 1) * 512], c_ones116[:, :],
                        carr_buf[0:1, nh * 512:(nh + 1) * 512],
                        start=True, stop=True), "T", 1,
                        waits=[t_cld, prev_pC_user] if nh == 0 else [])
                tok_bd = S.sig("vector", lambda e: e.tensor_tensor(
                    bd_sb[:, :], pC[0:16, :], c_bdmask[:, :], ALU.mult), "V", 1,
                    waits=[bmm])
                prev_cb_used = tok_bd
                cmm = None
                for nh in range(2):
                    cmm = S.sig("tensor", lambda e, nh=nh: e.matmul(
                        pC[:, nh * 512:(nh + 1) * 512], P_all[:, :],
                        bd_sb[:, nh * 512:(nh + 1) * 512],
                        start=True, stop=True), "T", 1,
                        waits=[tok_bd, tok_P] if nh == 0 else [])

                grp = stat_g(0, p)
                sm, sq, rs, nm = grp
                t_st = S.sig("vector", lambda e, c=c, p=p: e.scalar_tensor_tensor(
                    stsl(p), pC[:, :], 1.0, st0_sl(c), ALU.mult, ALU.add),
                    "V", 1, waits=[cmm, prev_st_used[p]])
                prev_pC_user = t_st
                t_ret = S.sig("vector",
                              lambda e, p=p, sm=sm: e.scalar_tensor_tensor(
                                  retsl(p), stsl(p), 1.0, qtsl(p),
                                  ALU.mult, ALU.mult, accum_out=sm), "V", 1,
                              waits=[t_st, t_qld, prev_ret_used[p]])
                prev_st_used[p] = t_ret
                prev_qt2_used[p] = t_ret
                t2 = S.sig("vector", lambda e, sm=sm: e.tensor_scalar(
                    sm, sm, -1.0 / D, None, ALU.mult), "V", 1, waits=[t_ret])
                t3 = S.sig("scalar", lambda e, p=p, sm=sm, sq=sq: e.activation(
                    trash2, retsl(p), AF.Square, bias=sm, accum_out=sq),
                    "S", 1, waits=[t2])
                t4 = S.sig("scalar", lambda e, sq=sq, rs=rs: e.activation(
                    rs, sq, AF.Ln, scale=1.0 / D, bias=eps_col[:, :]), "S", 1,
                    waits=[t3, tok_eps])
                t5 = S.sig("scalar", lambda e, rs=rs: e.activation(
                    rs, rs, AF.Exp, scale=-0.5), "S", 1, waits=[t4])
                t6 = S.sig("vector", lambda e, sm=sm, rs=rs, nm=nm:
                           e.tensor_tensor(nm, sm, rs, ALU.mult), "V", 1,
                           waits=[t5])
                t_y0 = S.sig("scalar", lambda e, p=p, rs=rs, nm=nm: e.activation(
                    ysl(0), retsl(p), AF.Identity, scale=rs, bias=nm), "S", 1,
                    waits=[t6])
                prev_ret_used[p] = t_y0
                t_y1 = S.sig("vector", lambda e: e.tensor_tensor(
                    ysl(1), ysl(0), gamma_rep[:, :], ALU.mult), "V", 1,
                    waits=[t_y0, tok_grep])
                t_y2 = S.sig("vector", lambda e: e.tensor_tensor(
                    ysl(0), ysl(1), beta_rep[:, :], ALU.add), "V", 1,
                    waits=[t_y1, tok_brep])
                t_yg = S.sig("vector", lambda e, p=p: e.tensor_tensor(
                    ysl(1), ysl(0), gsl(p), ALU.mult), "V", 1,
                    waits=[t_y2, tok_gev])
                prev_g_used[p] = t_yg

                ttp = None
                for k in range(NF):
                    ttp = S.sig("tensor", lambda e, k=k: e.transpose(
                        pD[:, k * 128:(k + 1) * 128],
                        ysl(1)[:, k * 128:(k + 1) * 128], c_ident[:, :]),
                        "T", 1, waits=[t_yg, prev_tpD2] if k == 0 else [])
                t_ygt = S.sig("vector", lambda e, p=p: e.tensor_copy(
                    ygtsl(p), pD[:, 0:1024]), "V", 1,
                    waits=[ttp, prev_ygt_used[p]])
                prev_tpD2 = t_ygt

                wmm = None
                for k in range(NF):
                    for nh in range(2):
                        first = (k == 0 and nh == 0)
                        wmm = S.sig("tensor",
                                    lambda e, p=p, k=k, nh=nh: e.matmul(
                                        pA[:, nh * 512:(nh + 1) * 512],
                                        ygtsl(p)[:, k * 128:(k + 1) * 128],
                                        w_slot(8 + k)[:, nh * 512:(nh + 1) * 512],
                                        start=(k == 0), stop=(k == NF - 1)),
                                    "T", 1,
                                    waits=[t_ygt, tok_wol, prev_pA_user]
                                    if first else [])
                prev_ygt_used[p] = wmm
                if c == NCH - 1:
                    tok_wo15 = wmm
                grp2 = stat_g(8, p)
                sm2, sq2, rs2, nm2 = grp2
                t_ao = S.sig("scalar", lambda e, p=p, sm2=sm2: e.activation(
                    aosl(p), pA[:, :], AF.Identity, accum_out=sm2), "S", 1,
                    waits=[wmm, prev_ao_used[p]])
                prev_pA_user = t_ao
                t2b = S.sig("vector", lambda e, sm2=sm2: e.tensor_scalar(
                    sm2, sm2, -1.0 / D, None, ALU.mult), "V", 1, waits=[t_ao])
                t3b = S.sig("scalar", lambda e, p=p, sm2=sm2, sq2=sq2:
                            e.activation(trash2, aosl(p), AF.Square,
                                         bias=sm2, accum_out=sq2), "S", 1,
                            waits=[t2b])
                t4b = S.sig("scalar", lambda e, sq2=sq2, rs2=rs2: e.activation(
                    rs2, sq2, AF.Ln, scale=1.0 / D, bias=eps_col[:, :]), "S", 1,
                    waits=[t3b, tok_eps])
                t5b = S.sig("scalar", lambda e, rs2=rs2: e.activation(
                    rs2, rs2, AF.Exp, scale=-0.5), "S", 1, waits=[t4b])
                t6b = S.sig("vector", lambda e, sm2=sm2, rs2=rs2, nm2=nm2:
                            e.tensor_tensor(nm2, sm2, rs2, ALU.mult), "V", 1,
                            waits=[t5b])
                t_x0p = S.sig("sync", lambda e, c=c, p=p: e.dma_start(
                    out=x0sl(p), in_=d["x0"][c * CH:(c + 1) * CH, :]),
                    f"dxp{p}", 16, waits=[prev_x0p_used[p]])
                t_xn = S.sig("scalar",
                             lambda e, p=p, rs2=rs2, nm2=nm2: e.activation(
                                 xnsl(p), aosl(p), AF.Identity, scale=rs2,
                                 bias=nm2), "S", 1,
                             waits=[t6b, prev_xn_used[p]])
                prev_ao_used[p] = t_xn
                t_x1 = S.sig("vector", lambda e, p=p: e.tensor_tensor(
                    x1sl(p), xnsl(p), x0sl(p), ALU.add), "V", 1,
                    waits=[t_xn, t_x0p, prev_x1_used[p]])
                prev_xn_used[p] = t_x1
                prev_x0p_used[p] = t_x1
                tok_x1st[c] = S.sig("sync", lambda e, c=c, p=p: e.dma_start(
                    out=x1d.ap()[c * CH:(c + 1) * CH, :], in_=x1sl(p)),
                    "dx1st", 16, waits=[t_x1])

                grp3 = stat_g(16, p)
                rs3, nm3 = grp3[2], grp3[3]
                tok_ln3 = ln_stats(x1sl(p), grp3, t_x1, trash2)
                t_u2 = S.sig("scalar",
                             lambda e, p=p, rs3=rs3, nm3=nm3: e.activation(
                                 u2sl(p), x1sl(p), AF.Identity, scale=rs3,
                                 bias=nm3), "S", 1,
                             waits=[tok_ln3, prev_u2_used[p]])
                prev_x1_used[p] = t_u2

                ttp2 = None
                for k in range(NF):
                    ttp2 = S.sig("tensor", lambda e, p=p, k=k: e.transpose(
                        pD[:, 1024 + k * 128: 1024 + (k + 1) * 128],
                        u2sl(p)[:, k * 128:(k + 1) * 128], c_ident[:, :]),
                        "T", 1,
                        waits=[t_u2, prev_tpD2] if k == 0 else [])
                tok_u2T_ev[c] = S.sig("vector", lambda e, c=c: e.tensor_copy(
                    u2T_cell(c), _apn(pD, 1024, (128, 8), (1, 128))), "V", 1,
                    waits=[ttp2])
                prev_tpD2 = tok_u2T_ev[c]
                prev_u2_used[p] = tok_u2T_ev[c]

                gmm2 = None
                for k in range(NF):
                    gmm2 = S.sig("tensor", lambda e, c=c, k=k: e.matmul(
                        pB[:, 0:8],
                        _apn(arena, VAL + k * 2048 + c * 128, (1, 128)),
                        wgr_b[:, k * 8:(k + 1) * 8],
                        start=(k == 0), stop=False), "T", 1,
                        waits=[tok_u2T_ev[c], tok_wgr, prev_pB_user]
                        if k == 0 else [])
                gmm2 = S.sig("tensor", lambda e: e.matmul(
                    pB[:, 0:8], c_ones1128b[:, :], rbias_b[:, :],
                    start=False, stop=True), "T", 1, waits=[tok_rb])
                tok_gates[c] = S.sig("vector", lambda e, c=c: e.tensor_copy(
                    gates_all[:, c * 8:(c + 1) * 8], pB[:, 0:8]), "V", 1,
                    waits=[gmm2])
                prev_pB_user = tok_gates[c]
            TT["tok"].update(u2T=tok_u2T_ev, x1st=tok_x1st, gates=tok_gates)

        if max_stage >= 6:
            def bc16(t16):
                a = t16 if isinstance(t16, bass.AP) else t16[:, :]
                return bass.AP(a.tensor, a.offset,
                               [list(a.ap[0]), list(a.ap[1]), [0, E]])

            def g3(t):
                a = t if isinstance(t, bass.AP) else t[:, :]
                return a.rearrange("p (c e) -> p c e", e=E)
            _ai = c_iota8[:, :]
            iota_b = bass.AP(_ai.tensor, _ai.offset,
                             [list(_ai.ap[0]), [0, NCH], list(_ai.ap[1])])

            lastv = S.sig("vector", lambda e: e.tensor_copy(
                tkw, gates_all[:, :]), "V", 1,
                waits=[tok_gates[NCH - 1], prev_x0p_used[0], prev_x0p_used[1]])
            lastv = S.sig("vector", lambda e: e.memset(tkm, 0.0), "V", 1,
                          waits=[lastv])
            tok_m1 = None
            for it in range(TOPK):
                lastv = S.sig("vector", lambda e: e.tensor_reduce(
                    gmax[:, :], g3(tkw), AX.X, ALU.max), "V", 1, waits=[lastv])
                if it == 0:
                    tok_m1 = lastv = S.sig("vector", lambda e: e.tensor_copy(
                        gm1[:, :], gmax[:, :]), "V", 1, waits=[lastv])
                lastv = S.sig("vector", lambda e: e.tensor_tensor(
                    g3(tkt), g3(tkw), bc16(gmax), ALU.is_ge), "V", 1,
                    waits=[lastv])
                lastv = S.sig("vector", lambda e: e.tensor_scalar(
                    tkt2, tkt, -8.0, 8.0, ALU.mult, ALU.add), "V", 1,
                    waits=[lastv])
                lastv = S.sig("vector", lambda e: e.tensor_tensor(
                    g3(tkt), g3(tkt), iota_b, ALU.mult), "V", 1, waits=[lastv])
                lastv = S.sig("vector", lambda e: e.tensor_tensor(
                    tkt, tkt, tkt2, ALU.add), "V", 1, waits=[lastv])
                lastv = S.sig("vector", lambda e: e.tensor_reduce(
                    gidx[:, :], g3(tkt), AX.X, ALU.min), "V", 1, waits=[lastv])
                lastv = S.sig("vector", lambda e: e.tensor_tensor(
                    g3(tkt), iota_b, bc16(gidx), ALU.is_equal), "V", 1,
                    waits=[lastv])
                lastv = S.sig("vector", lambda e: e.tensor_tensor(
                    tkm, tkm, tkt, ALU.add), "V", 1, waits=[lastv])
                lastv = S.sig("vector", lambda e: e.tensor_scalar(
                    tkt, tkt, -3.0e38, None, ALU.mult), "V", 1, waits=[lastv])
                lastv = S.sig("vector", lambda e: e.tensor_tensor(
                    tkw, tkw, tkt, ALU.add), "V", 1, waits=[lastv])
            lastv = S.sig("vector", lambda e: e.tensor_tensor(
                g3(tkt), g3(gates_all[:, :]), bc16(gm1), ALU.subtract), "V", 1,
                waits=[lastv, tok_m1])
            t_exp = S.sig("scalar", lambda e: e.activation(
                tkt2, tkt, AF.Exp), "S", 1, waits=[lastv])
            lastv = S.sig("vector", lambda e: e.tensor_tensor(
                tkt, tkt2, tkm, ALU.mult), "V", 1, waits=[t_exp])
            lastv = S.sig("vector", lambda e: e.tensor_reduce(
                gz[:, :], g3(tkt), AX.X, ALU.add), "V", 1, waits=[lastv])
            lastv = S.sig("vector", lambda e: e.reciprocal(
                gmax[:, :], gz[:, :]), "V", 1, waits=[lastv])
            tok_w = S.sig("vector", lambda e: e.tensor_tensor(
                g3(w_all[:, :]), g3(tkt), bc16(gmax), ALU.mult), "V", 1,
                waits=[lastv])
            TT["tok"].update(w=tok_w)

        if max_stage >= 7:
            tok_r1, tok_r2, tok_w3 = {}, {}, {}
            free_r1 = [tok_g15, tok_g15]
            free_r2 = [tok_g15, tok_g15]
            free_w3 = [(tok_g15 if (4 + sl) < 8 else tok_wo15)
                       for sl in range(12)]

            stage4_sl = lambda sp: fpool[:, sp * 1024:(sp + 1) * 1024]
            s1sl = u2sl

            ffn = {
                "pA": prev_pA_user, "pB": prev_pB_user,
                "pC": prev_pC_user,
                "s1": [prev_u2_used[0], prev_u2_used[1]],
                "stage": [tok_w, tok_w, prev_x1_used[0], prev_x1_used[1]],
                "last_z": None,
                "accd": [None] * NCH,
            }
            u2T_all = tok_u2T_ev[NCH - 1]

            def load_r12(ei, th, m):
                slot = m % 2
                src1 = (d["ws1"][m] if ei == 0 else d["w1"][ei - 1, m])
                tok_r1[(ei, th, m)] = S.sig(
                    "gpsimd",
                    lambda e, s=src1, t=w_slot(slot): e.dma_start(
                        out=t.rearrange("p (k m) -> p k m", m=128),
                        in_=s.rearrange("k p m -> p k m")),
                    f"r1s{slot}", 16, waits=[free_r1[slot]])
                src2 = (d["ws2"][m] if ei == 0 else d["w2"][ei - 1, m])
                tok_r2[(ei, th, m)] = S.sig(
                    "gpsimd",
                    lambda e, s=src2, t=w_slot(2 + slot): e.dma_start(
                        out=t.rearrange("p (k m) -> p k m", m=128),
                        in_=s.rearrange("k p m -> p k m")),
                    f"r2s{slot}", 16, waits=[free_r2[slot]])

            def load_w3_pair(ei, th, m0, cnt):
                slot = m0 if m0 < 12 else m0 - 12
                src3 = (d["ws3"][m0:m0 + cnt] if ei == 0
                        else d["w3"][ei - 1, m0:m0 + cnt])
                tk = S.sig(
                    "gpsimd",
                    lambda e, s=src3, sl=slot, cnt=cnt: e.dma_start(
                        out=warena[:, (4 + sl) * 1024:(4 + sl + cnt) * 1024]
                        .rearrange("p (g m) -> p g m", g=cnt),
                        in_=s.rearrange("g k m -> p ...", p=128)
                        if False else s.rearrange("g p m -> p g m")),
                    f"w3s{slot}", 16,
                    waits=[free_w3[slot + i] for i in range(cnt)])
                for i in range(cnt):
                    tok_w3[(ei, th, m0 + i)] = tk

            def ffn_h_block(ei, th, m, slot):
                w1s, w2s = w_slot(slot), w_slot(2 + slot)
                mm1 = None
                for k in range(NF):
                    for nh in range(2):
                        first = (k == 0 and nh == 0)
                        mm1 = S.sig(
                            "tensor",
                            lambda e, k=k, nh=nh, w1s=w1s, th=th: e.matmul(
                                pA[:, nh * 512:(nh + 1) * 512],
                                w1s[:, k * 128:(k + 1) * 128],
                                u2T_rhs(k, th, nh),
                                start=(k == 0), stop=(k == NF - 1)),
                            "T", 1,
                            waits=[tok_r1[(ei, th, m)], u2T_all, ffn["pA"]]
                            if first else [])
                mm2 = None
                for k in range(NF):
                    for nh in range(2):
                        first = (k == 0 and nh == 0)
                        mm2 = S.sig(
                            "tensor",
                            lambda e, k=k, nh=nh, w2s=w2s, th=th: e.matmul(
                                pB[:, nh * 512:(nh + 1) * 512],
                                w2s[:, k * 128:(k + 1) * 128],
                                u2T_rhs(k, th, nh),
                                start=(k == 0), stop=(k == NF - 1)),
                            "T", 1,
                            waits=[tok_r2[(ei, th, m)], u2T_all, ffn["pB"]]
                            if first else [])
                sil = S.sig("scalar", lambda e: e.activation(
                    s1sl(0), pA[:, :], AF.Silu), "S", 1,
                    waits=[mm1, ffn["s1"][0]])
                ffn["pA"] = sil
                t_z = S.sig("vector", lambda e, m=m: e.tensor_tensor(
                    z_m(m), s1sl(0), pB[:, :], ALU.mult), "V", 1,
                    waits=[sil, mm2])
                ffn["s1"][0] = t_z
                ffn["pB"] = t_z
                ffn["last_z"] = t_z
                free_r1[slot] = mm1
                free_r2[slot] = mm2

            def ffn_w3_phase(ei, th):
                for half in range(2):
                    ms = W3A if half == 0 else W3B
                    if half == 1:
                        for m3 in W3B:
                            load_w3_pair(ei, th, m3, 1)
                    for c8 in range(8):
                        c = th * 8 + c8
                        par = c8 % 2
                        eps_ = pA if par == 0 else pB
                        pfree = ffn["pA"] if par == 0 else ffn["pB"]
                        mm = None
                        for mi, m in enumerate(ms):
                            slot = m if m < 12 else m - 12
                            for nh in range(2):
                                w_ = [tok_w3[(ei, th, m)]] if nh == 0 else []
                                if mi == 0 and nh == 0:
                                    w_ = w_ + [ffn["last_z"], pfree]
                                mm = S.sig(
                                    "tensor",
                                    lambda e, m=m, c8=c8, nh=nh, slot=slot,
                                    eps_=eps_, mi=mi, nms=len(ms): e.matmul(
                                        eps_[:, nh * 512:(nh + 1) * 512],
                                        z_tile(m, c8),
                                        w_slot(4 + slot)[:, nh * 512:(nh + 1) * 512],
                                        start=(mi == 0), stop=(mi == nms - 1)),
                                    "T", 1, waits=w_)
                            if c8 == 7:
                                free_w3[slot] = mm
                        sp = (half * 8 + c8) % 4
                        if ei == 0:
                            t_cp = S.sig(
                                "vector",
                                lambda e, sp=sp, eps_=eps_: e.tensor_copy(
                                    stage4_sl(sp), eps_[:, :]), "V", 1,
                                waits=[mm, ffn["stage"][sp]])
                        else:
                            t_cp = S.sig(
                                "vector",
                                lambda e, sp=sp, eps_=eps_, c=c, ei=ei:
                                e.tensor_scalar(
                                    stage4_sl(sp), eps_[:, :],
                                    w_all[:, c * 8 + (ei - 1): c * 8 + ei],
                                    None, ALU.mult),
                                "V", 1, waits=[mm, ffn["stage"][sp], tok_w])
                        if par == 0:
                            ffn["pA"] = t_cp
                        else:
                            ffn["pB"] = t_cp
                        accum = not (ei == 0 and half == 0)
                        t_dma = S.sig(
                            "gpsimd",
                            lambda e, sp=sp, c=c, accum=accum: e.dma_start(
                                out=accd.ap()[c * CH:(c + 1) * CH, :],
                                in_=stage4_sl(sp),
                                **({"accum_op": ALU.add} if accum else {})),
                            f"dsg{sp}", 16,
                            waits=[t_cp, ffn["accd"][c]])
                        ffn["stage"][sp] = t_dma
                        ffn["accd"][c] = t_dma

            seq = [(th, ei) for th in range(2) for ei in range(FFN_NEI)]
            for m0 in (0, 1):
                load_r12(seq[0][1], seq[0][0], m0)
            for si, (th, ei) in enumerate(seq):
                for m3 in W3A:
                    load_w3_pair(ei, th, m3, 1)
                for m in range(NHT):
                    if (ei, th, m) not in tok_r1:
                        load_r12(ei, th, m)
                    ffn_h_block(ei, th, m, m % 2)
                if si + 1 < len(seq):
                    nth, nei = seq[si + 1]
                    for m0 in (0, 1):
                        load_r12(nei, nth, m0)
                ffn_w3_phase(ei, th)
            TT["tok"].update(last_z=ffn["last_z"])

        if max_stage >= 9:
            wait_acc = [(f"dsg{i}", S.counts.get(f"dsg{i}", 0))
                        for i in range(4)]
            wait_x1 = [("dx1st", S.counts.get("dx1st", 0))]
            prev_acc_used = [ffn["stage"][0], ffn["stage"][1]]
            prev_x1b_used = [ffn["stage"][2], ffn["stage"][3]]
            prev_xn4_used = [prev_xn_used[0], prev_xn_used[1]]
            for c in range(NCH):
                p = c % 2
                t_la = S.sig("sync", lambda e, c=c, p=p: e.dma_start(
                    out=x0sl(p), in_=accd.ap()[c * CH:(c + 1) * CH, :]),
                    f"dba{p}", 16, waits=wait_acc + [prev_acc_used[p]])
                t_lx = S.sig("sync", lambda e, c=c, p=p: e.dma_start(
                    out=x1sl(p), in_=x1d.ap()[c * CH:(c + 1) * CH, :]),
                    f"dbx{p}", 16, waits=wait_x1 + [prev_x1b_used[p]])
                grp = stat_g(0, p)
                tok_ln4 = ln_stats(x0sl(p), grp, t_la, trash2)
                rs4, nm4 = grp[2], grp[3]
                t_xn4 = S.sig("scalar",
                              lambda e, p=p, rs4=rs4, nm4=nm4: e.activation(
                                  xnsl(p), x0sl(p), AF.Identity, scale=rs4,
                                  bias=nm4), "S", 1,
                              waits=[tok_ln4, prev_xn4_used[p]])
                t_x2 = S.sig("vector", lambda e, p=p: e.tensor_tensor(
                    x0sl(p), xnsl(p), x1sl(p), ALU.add), "V", 1,
                    waits=[t_xn4, t_lx])
                prev_xn4_used[p] = t_x2
                prev_x1b_used[p] = t_x2
                t_so = S.sig("sync", lambda e, c=c, p=p: e.dma_start(
                    out=out_d[c * CH:(c + 1) * CH, :], in_=x0sl(p)),
                    "dout", 16, waits=[t_x2])
                prev_acc_used[p] = t_so
            S.emit("sync", lambda e: e.nop(),
                   waits=[("dout", S.counts.get("dout", 0))])

        if dbg is not None:
            dbg(nc, S, TT)

        handles = {}
        for name in S.counts:
            handles[name] = es.enter_context(nc.semaphore(name))
        with nc.Block() as block:
            S.realize(block, handles)

    return nc


def kernel(**inputs):
    in_maps = _host_prep(inputs)
    nc = build_program()
    res = run_bass_kernel_spmd(nc, in_maps, core_ids=list(range(NCORE)))
    out = np.empty((B, L, D), np.float32)
    for c in range(NCORE):
        b, s = c // 4, c % 4
        out[b, s * T:(s + 1) * T, :] = np.asarray(res.results[c]["out"])
    return out
